# version 22
# baseline (speedup 1.0000x reference)
import os
import numpy as np
import concourse.bass as bass
import concourse.mybir as mybir
from concourse.bass_utils import run_bass_kernel_spmd

F32 = mybir.dt.float32
BF16 = mybir.dt.bfloat16
I32 = mybir.dt.int32
AF = mybir.ActivationFunctionType
ALU = mybir.AluOpType
AX = mybir.AxisListType

D = 1024
H = 4
DV = 512
VD = 2048
FF = 4096
LN16 = -2.772588722239781



class Buf:
    __slots__ = ("name", "w", "readers")

    def __init__(self, name=""):
        self.name = name
        self.w = None
        self.readers = []


class Prog:
    ENGS = ("v", "s", "t", "g", "y")

    def __init__(self, nc):
        self.nc = nc
        self.plan = {e: [] for e in self.ENGS}
        self.cnt = {e: 0 for e in self.ENGS}
        self.done = {}
        self.dma_sems = {}
        self.waited = {e: {} for e in self.ENGS}
        self._dma_names = []

    def dma_stream(self, name):
        self._dma_names.append(name)
        self.dma_sems[name] = [None, 0]
        return name

    def _wait(self, eng, tok):
        if tok is None:
            return
        kind, key, val = tok
        if kind == "o":
            assert key == eng, f"cross-engine dep on unfenced group ({key}->{eng})"
            return
        if kind == "df":
            if self.waited[eng].get(("df", key)):
                return
            self.waited[eng][("df", key)] = True
            self.waited[eng][("d", key)] = 10 ** 9
            ds = self.dma_sems

            def thunk(obj, key=key):
                obj.wait_ge(ds[key][0], ds[key][1])
            self.plan[eng].append(thunk)
            return
        cur = self.waited[eng].get((kind, key), 0)
        if val <= cur:
            return
        self.waited[eng][(kind, key)] = val
        if kind == "e":
            done = self.done

            def thunk(obj, key=key, val=val):
                obj.wait_ge(done[key], val)
        else:
            ds = self.dma_sems

            def thunk(obj, key=key, val=val):
                obj.wait_ge(ds[key][0], val)
        self.plan[eng].append(thunk)

    def _deps(self, reads, writes):
        toks = []
        for b in reads:
            if b.w is not None:
                toks.append(b.w)
        for b in writes:
            if b.w is not None:
                toks.append(b.w)
            toks.extend(b.readers)
        return toks

    def emit(self, eng, fn, reads=(), writes=(), inc=True):
        for tok in self._deps(reads, writes):
            self._wait(eng, tok)
        if inc:
            self.cnt[eng] += 1
            val = self.cnt[eng]
            tok = ("e", eng, val)
            done = self.done

            def thunk(obj, fn=fn, eng=eng):
                fn(obj).then_inc(done[eng], 1)
        else:
            tok = None

            def thunk(obj, fn=fn):
                fn(obj)
        self.plan[eng].append(thunk)
        for b in reads:
            if tok is not None:
                b.readers.append(tok)
        for b in writes:
            b.w = tok if tok is not None else ("o", eng, None)
            b.readers = []
        return tok

    def dma(self, eng, stream, out, in_, reads=(), writes=(), slow=False,
            dram_final=()):
        toks = self._deps(reads, writes)
        for b in dram_final:
            toks.extend(b.readers)
        for tok in toks:
            self._wait(eng, tok)
        ds = self.dma_sems[stream]
        ds[1] += 16
        val = ds[1]
        dsd = self.dma_sems

        def thunk(obj, out=out, in_=in_, stream=stream, slow=slow):
            kw = dict(allow_slow_non_contiguous=True) if slow else {}
            obj.dma_start(out=out, in_=in_, **kw).then_inc(dsd[stream][0], 16)
        self.plan[eng].append(thunk)
        tok = ("d", stream, val)
        for b in reads:
            b.readers.append(tok)
        for b in writes:
            b.w = tok
            b.readers = []
        for b in dram_final:
            b.w = ("df", stream, None)
            b.readers = []
        return tok

    def raw(self, eng, fn):
        self.plan[eng].append(fn)

    def build(self, block):
        import contextlib
        with contextlib.ExitStack() as st:
            for e in self.ENGS:
                self.done[e] = st.enter_context(self.nc.semaphore("done_" + e))
            for n in self._dma_names:
                self.dma_sems[n][0] = st.enter_context(self.nc.semaphore("dma_" + n))

            @block.sync
            def _(o):
                for f in self.plan["y"]:
                    f(o)

            @block.vector
            def _(o):
                for f in self.plan["v"]:
                    f(o)

            @block.scalar
            def _(o):
                for f in self.plan["s"]:
                    f(o)

            @block.tensor
            def _(o):
                for f in self.plan["t"]:
                    f(o)

            @block.gpsimd
            def _(o):
                for f in self.plan["g"]:
                    f(o)


def build(n_cores=8, T=4096, dbg=False):
    NT = T // 128
    G = 2
    W = G * 128
    NG = NT // G

    nc = bass.Bass("TRN2", target_bir_lowering=False, detect_race_conditions=False)

    x_h = nc.dram_tensor("x", [T, D], F32, kind="ExternalInput")
    wq_h = nc.dram_tensor("Wq", [D, D], F32, kind="ExternalInput")
    wk_h = nc.dram_tensor("Wk", [D, D], F32, kind="ExternalInput")
    wv_h = nc.dram_tensor("Wv", [D, VD], F32, kind="ExternalInput")
    wg_h = nc.dram_tensor("Wg", [D, VD], F32, kind="ExternalInput")
    wgk1_h = nc.dram_tensor("Wgk1", [D, 16], F32, kind="ExternalInput")
    wgk2_h = nc.dram_tensor("Wgk2", [16, D], F32, kind="ExternalInput")
    bgk2_h = nc.dram_tensor("bgk2", [D], F32, kind="ExternalInput")
    rmsw_h = nc.dram_tensor("rms_w", [DV], F32, kind="ExternalInput")
    wo_h = nc.dram_tensor("Wo", [VD, D], F32, kind="ExternalInput")
    win_h = nc.dram_tensor("W_in", [D, FF], F32, kind="ExternalInput")
    bin_h = nc.dram_tensor("b_in", [FF], F32, kind="ExternalInput")
    wout_h = nc.dram_tensor("W_out", [FF, D], F32, kind="ExternalInput")
    bout_h = nc.dram_tensor("b_out", [D], F32, kind="ExternalInput")
    cons_h = nc.dram_tensor("cons", [128, 54], F32, kind="ExternalInput")
    y_h = nc.dram_tensor("y", [T, D], F32, kind="ExternalOutput")

    _sk = dict(kind="ExternalOutput") if dbg else {}
    xp_d = nc.dram_tensor("xp_d", [T, D], F32, **_sk)
    xl3_d = nc.dram_tensor("xl3_d", [T, D], BF16, **_sk)
    hid_d = nc.dram_tensor("hid_d", [NG, 32, 128, W], BF16, **_sk)
    wgbf_d = nc.dram_tensor("wgbf_d", [8, 128, VD], BF16)
    wvbf_d = nc.dram_tensor("wvbf_d", [8, 128, VD], BF16)
    wqc_d = nc.dram_tensor("wqc_d", [8, 128, 1024], BF16)
    wkc_d = nc.dram_tensor("wkc_d", [8, 128, 1024], BF16)
    wobf_d = nc.dram_tensor("wobf_d", [16, 128, D], BF16)
    wqin_d = nc.dram_tensor("wqin_d", [32, 128, 1024], BF16)
    woutq_d = nc.dram_tensor("woutq_d", [32, 128, D], BF16)
    w2_d = nc.dram_tensor("w2_d", [32, 128, D], BF16)
    ccxi_d = nc.dram_tensor("ccxi_d", [1, 8], F32)
    ccxo_d = nc.dram_tensor("ccxo_d", [1, 8], F32)
    cchi_d = nc.dram_tensor("cchi_d", [1, 8], F32)
    ccho_d = nc.dram_tensor("ccho_d", [1, 8], F32)
    if dbg:
        dsc_d = nc.dram_tensor("dsc_d", [128, 136], F32, kind="ExternalOutput")
        dX_d = nc.dram_tensor("dX_d", [128, 2048], BF16)
        dQG_d = nc.dram_tensor("dQG_d", [128, 2048], BF16)
        dKG_d = nc.dram_tensor("dKG_d", [128, 2048], BF16)
        dV_d = nc.dram_tensor("dV_d", [128, 2048], BF16)
        dSG_d = nc.dram_tensor("dSG_d", [128, 2048], BF16)
        dKT_d = nc.dram_tensor("dKT_d", [128, 1024], BF16)
        dA_d = nc.dram_tensor("dA_d", [128, 512], BF16)
        dOF_d = nc.dram_tensor("dOF_d", [128, 2048], BF16)
        dS_d = nc.dram_tensor("dS_d", [128, 4096], BF16)
        dXS_d = nc.dram_tensor("dXS_d", [NG, 128, 2048], BF16, kind="ExternalOutput")

    A = nc.alloc_sbuf_tensor
    consts = A("consts", [128, 54], F32)
    identb = A("identb", [128, 128], BF16)
    identf = A("identf", [128, 128], F32)
    mask4 = A("mask4", [128, 512], F32)
    onesf = A("onesf", [128, 128], F32)
    onesc = A("onesc", [1, 128], F32)
    bgk2n = A("bgk2n", [128, 8], F32)
    binb = A("binb", [128, 32], F32)
    rms4 = A("rms4", [128, 4], F32)
    boutb = A("boutb", [128, 1024], F32)
    bout_row = A("bout_row", [1, 1024], F32)
    S_bfc = A("S_bfc", [128, 8 * DV], BF16)
    bn12 = A("bn12", [128, 12], F32)
    mvg = A("mvg", [128, 2 * G], F32)
    stA = A("stA", [128, 2], F32)
    stB = A("stB", [128, 2], F32)
    mo8 = A("mo8", [128, 8], F32)
    rsO = A("rsO", [128, 4], F32)
    GL63 = [A(f"GL63_{i}", [128, 8 * G], F32) for i in range(2)]
    GL127 = [A(f"GL127_{i}", [128, 8 * G], F32) for i in range(2)]
    EGLN = [A(f"EGLN_{i}", [128, 8 * G], F32) for i in range(2)]
    EGLS = [A(f"EGLS_{i}", [128, 8 * G], F32) for i in range(2)]
    AMX = A("AMX", [128, 1], F32)
    AMH = A("AMH", [128, 1], F32)
    r1a = A("r1a", [1, 128], F32)
    r1b = A("r1b", [1, 1], F32)
    SXC = A("SXC", [128, 1], F32)
    RSXC = A("RSXC", [128, 1], F32)
    SHC = A("SHC", [128, 1], F32)
    RSHC = A("RSHC", [128, 1], F32)
    ALF = A("ALF", [128, 32], F32)
    AMF = A("AMF", [128, 32], F32)
    SOC = A("SOC", [128, 32], F32)
    RSOC = A("RSOC", [128, 32], F32)
    BET = A("BET", [128, 32], F32)
    ccb = A("ccb", [1, 8], F32)
    Wgk1_s = A("Wgk1_s", [128, 8 * 16], BF16)
    Wgk2_s = A("Wgk2_s", [16, D], BF16)
    wsb = [A(f"wsb{i}", [128, 1024], BF16) for i in range(2)]
    wqb = [A(f"wqb{i}", [128, 1024], BF16) for i in range(2)]
    wkb = [A(f"wkb{i}", [128, 1024], BF16) for i in range(2)]
    wvb = [A(f"wvb{i}", [128, 1024], BF16) for i in range(2)]
    wgb = [A(f"wgb{i}", [128, 1024], BF16) for i in range(2)]
    bnF = [A(f"bnF{i}", [128, 12], F32) for i in range(2)]
    mvF = [A(f"mvF{i}", [128, 2 * G], F32) for i in range(2)]
    stAF = [A(f"stAF{i}", [128, 2], F32) for i in range(2)]
    stBF = [A(f"stBF{i}", [128, 2], F32) for i in range(2)]
    scg = [A(f"scg{i}", [128, 512], F32) for i in range(4)]
    scq = [A(f"scq{i}", [128, 256], F32) for i in range(4)]
    sck = [A(f"sck{i}", [128, 256], F32) for i in range(4)]
    rtA = A("rtA", [128, 6], F32)
    scB2 = A("scB2", [128, 1024], F32)
    z1b = [A(f"z1b{i}", [16, W], BF16) for i in range(2)]
    sc0 = A("sc0", [128, 512], F32)
    sc1 = A("sc1", [128, 512], F32)
    sc2 = A("sc2", [128, 512], F32)
    sci = A("sci", [128, 1024], I32)
    sc3 = A("sc3", [128, 1024], F32)
    sc4 = A("sc4", [128, 1024], F32)
    wpb = [A(f"wpb{i}", [128, 1024], F32) for i in range(2)]
    wacc = [A(f"wacc{i}", [128, 1024], F32) for i in range(2)]
    wqt = A("wqt", [128, 1024], BF16)

    bump = nc._sbuf_addr_for_side('left')
    TOP = 122816
    assert bump <= TOP, f"bump {bump} exceeds overlay base {TOP}"

    AT = nc.alloc_sbuf_tensor_at
    off = [TOP]

    def at(name, shape, dtype, nbytes):
        t = AT(name, shape, dtype, offset=off[0])
        off[0] += nbytes
        assert off[0] <= 229344, f"overlay overflow at {name}"
        return t

    x_gP, X_fmP, QGP, KGP, KTMP, VTP, SGTP, xlntP = [], [], [], [], [], [], [], []
    qg_off, kg_off, xlnt_off = [], [], []
    for pya in range(2):
        x_gP.append(at(f"x_g{pya}", [128, G * 1024], F32, G * 4096))
        X_fmP.append(at(f"X_fm{pya}", [128, 8 * W], BF16, 8 * W * 2))
        qg_off.append(off[0])
        QGP.append(at(f"QG{pya}", [128, 8 * W], BF16, 8 * W * 2))
        kg_off.append(off[0])
        KGP.append(at(f"KG{pya}", [128, 8 * W], BF16, 8 * W * 2))
        KTMP.append([at(f"KTM{pya}_{i}", [128, 1024], BF16, 2048) for i in range(G)])
        VTP.append([at(f"VT{pya}_{i}", [128, VD], BF16, 4096) for i in range(G)])
        SGTP.append([at(f"SGT{pya}_{i}", [128, VD], BF16, 4096) for i in range(G)])
        xlnt_off.append(off[0])
        xlntP.append(at(f"xlnt{pya}", [128, 1024], BF16, 2048))
    OF = at("OF", [128, VD], BF16, 4096)
    OFm = at("OFm", [128, VD], BF16, 4096)
    OF2B = [AT(f"OF2B{i}", [128, VD], BF16, offset=qg_off[i]) for i in range(2)]
    OFKB = [AT(f"OFKB{i}", [128, VD], BF16, offset=kg_off[i]) for i in range(2)]
    ABUF = [AT(f"ABUF{i}", [128, 512], BF16, offset=xlnt_off[i]) for i in range(2)]
    wqt2 = AT("wqt2", [128, 1024], BF16, offset=xlnt_off[0])
    xpb = [at(f"xpb{i}", [128, 1024], F32, 4096) for i in range(2)]
    xl3b = [at(f"xl3b{i}", [128, 1024], BF16, 2048) for i in range(2)]
    attn_end = off[0]

    off = [TOP]
    xsq_base = off[0]
    XSIMQ = [at(f"XSIM{i}", [128, 8 * W], BF16, 8 * W * 2) for i in range(8)]
    wpbX = [AT(f"wpbX{i}", [128, 1024], F32, offset=xsq_base + (4 + i) * 8 * W * 2)
            for i in range(2)]
    xl3l = [at(f"xl3l{i}", [128, 1024], BF16, 2048) for i in range(4)]
    hbb = [at(f"hbb{i}", [128, W], BF16, W * 2) for i in range(4)]

    off = [TOP]
    xpl = [at(f"xpl{i}", [128, 1024], F32, 4096) for i in range(4)]
    outb = [at(f"outb{i}", [128, 1024], F32, 4096) for i in range(2)]
    h2b = [at(f"h2b{i}", [128, W], BF16, W * 2) for i in range(4)]
    hlb = [at(f"hlb{i}", [128, W], BF16, W * 2) for i in range(4)]

    bump = nc._sbuf_addr_for_side('left')
    assert bump <= TOP, f"late bump {bump} exceeds {TOP}"

    PP = [nc.alloc_psum_tensor(f"PP{i}", [128, 1024], F32) for i in range(4)]

    P = Prog(nc)
    LDW = P.dma_stream("ldw")
    LDC = P.dma_stream("ldc")
    GDM = P.dma_stream("gdm")
    WPB = [P.dma_stream(f"wpb{i}") for i in range(4)]
    LDX = [[P.dma_stream(f"ldx{p}_{t}") for t in range(G)] for p in range(2)]
    WQB = [P.dma_stream(f"wqbs{i}") for i in range(2)]
    WKB = [P.dma_stream(f"wkbs{i}") for i in range(2)]
    WVB = [P.dma_stream(f"wvbs{i}") for i in range(2)]
    WGB = [P.dma_stream(f"wgbs{i}") for i in range(2)]
    XPS = [P.dma_stream(f"xps{i}") for i in range(2)]
    X3S = [P.dma_stream(f"x3s{i}") for i in range(2)]
    HIDS = [P.dma_stream(f"hids{i}") for i in range(4)]
    WQIN_SL = [[P.dma_stream(f"wqin_s{i}_{j}") for j in range(2)]
               for i in range(4)]
    WOBF_S = P.dma_stream("wobf_s")
    WOUTQ_S = P.dma_stream("woutq_s")
    W2SL = [P.dma_stream(f"w2s{i}") for i in range(4)]
    LWS = [P.dma_stream(f"lws{i}") for i in range(2)]
    XL3L = [P.dma_stream(f"xl3l{i}") for i in range(4)]
    HLB = [P.dma_stream(f"hlb{i}") for i in range(4)]
    XPL = [P.dma_stream(f"xpl{i}") for i in range(4)]
    STY = [P.dma_stream(f"sty{i}") for i in range(2)]
    DBGS = P.dma_stream("dbgs")

    B = Buf
    b_consts, b_ident, b_mask = B("consts"), B("ident"), B("mask")
    b_onesf, b_onesc = B("onesf"), B("onesc")
    b_vecs = B("vecs")
    b_boutrow = B("boutrow")
    b_Sb = B("Sb")
    b_Sms = [B(f"Sms{i}") for i in range(8)]
    b_bn, b_mv, b_st = B("bn"), B("mv"), B("st")
    b_mo, b_rsO = B("mo"), B("rsO")
    b_GL = B("GL")
    b_AMX, b_AMH = B("AMX"), B("AMH")
    b_r1 = B("r1")
    b_sx, b_sh = B("sx"), B("sh")
    b_ALF, b_AMF = B("ALF"), B("AMF")
    b_SOC, b_BET = B("SOC"), B("BET")
    b_ccb, b_cc = B("ccb"), B("cc")
    b_Wgk = B("Wgk")
    b_wsb = [B() for _ in range(2)]
    b_wqb = [B() for _ in range(2)]
    b_wkb = [B() for _ in range(2)]
    b_wvb = [B() for _ in range(2)]
    b_wgb = [B() for _ in range(2)]
    b_wqc = [B() for _ in range(8)]
    b_wkc = [B() for _ in range(8)]
    b_wvbf = [B() for _ in range(8)]
    b_bnF = [B() for _ in range(2)]
    b_mvF = [B() for _ in range(2)]
    b_stF = [B() for _ in range(2)]
    b_scg = [B(f"scg{i}") for i in range(4)]
    b_scq = [B() for _ in range(4)]
    b_sck = [B() for _ in range(4)]
    b_rtA = B("rtA")
    b_scB1 = B("scB2")
    b_z1 = B("z1")
    b_sc0, b_sc1, b_sc2, b_sci = B("sc0"), B("sc1"), B("sc2"), B("sci")
    b_sc3, b_sc4 = B("sc3"), B("sc4")
    b_wpb = [B() for _ in range(2)]
    b_wacc, b_wqt = B("wacc"), B("wqt")
    b_xg = [[B(f"x_g{p}{t}") for t in range(G)] for p in range(2)]
    b_Xfm = [B() for _ in range(2)]
    b_QG = [B() for _ in range(2)]
    b_KG = [B() for _ in range(2)]
    b_KTM = [[B() for _ in range(G)] for _ in range(2)]
    b_VT = [[B() for _ in range(G)] for _ in range(2)]
    b_SGT = [[B() for _ in range(G)] for _ in range(2)]
    b_xlnt = [B(), B()]
    b_GLP = [B(), B()]
    b_z1P = [B(), B()]
    b_OF, b_OFm = B("OF"), B("OFm")
    b_xpb = [B() for _ in range(2)]
    b_xl3b = [B() for _ in range(2)]
    b_XSIM = [B() for _ in range(8)]
    b_PP0H = B("PP0H")
    b_xl3l = [B() for _ in range(4)]
    b_hbb = [B() for _ in range(4)]
    b_h2b = [B() for _ in range(4)]
    b_hlb = [B() for _ in range(4)]
    b_xpl = [B() for _ in range(4)]
    b_outb = [B() for _ in range(2)]
    b_PP = [B(f"PP{i}") for i in range(4)]
    b_xp_t = [B() for _ in range(NT)]
    b_xl3_t = [B() for _ in range(NT)]
    b_hid = [[B() for _ in range(32)] for _ in range(NG)]
    b_wgbf = [B() for _ in range(8)]
    b_wobf = [B() for _ in range(16)]
    b_wqin = [B() for _ in range(32)]
    b_wqinB = [B() for _ in range(32)]
    b_woutq = [B() for _ in range(32)]
    b_w2s = [B() for _ in range(32)]
    b_w2l = [B() for _ in range(4)]

    C_EPS, C_LN16 = 0, 1

    def cst(i):
        return consts[:, i:i + 1]

    ew = P.emit

    P.dma("y", LDC, out=consts[:, :], in_=cons_h[:, :], dram_final=(b_consts,))
    ew("v", lambda v: v.tensor_copy(bgk2n[:, :], consts[:, 2:10]),
       reads=(b_consts,), writes=(b_vecs,))
    ew("v", lambda v: v.tensor_copy(binb[:, :], consts[:, 10:42]),
       reads=(b_consts,), writes=(b_vecs,))
    ew("v", lambda v: v.tensor_copy(rms4[:, :], consts[:, 42:46]),
       reads=(b_consts,), writes=(b_vecs,))
    ew("g", lambda g: g.memset(onesf[:, :], 1.0), writes=(b_onesf,))
    ew("g", lambda g: g.memset(onesc[:1, :], 1.0), writes=(b_onesc,))
    ew("g", lambda g: g.affine_select(identb[:, :], onesf[:, :], [[1, 128]],
                                      ALU.is_equal, 0.0, base=0,
                                      channel_multiplier=-1),
       reads=(b_onesf,), writes=(b_ident,))
    ew("g", lambda g: g.affine_select(identf[:, :], onesf[:, :], [[1, 128]],
                                      ALU.is_equal, 0.0, base=0,
                                      channel_multiplier=-1),
       reads=(b_onesf,), writes=(b_ident,))
    for hh in range(4):
        ew("g", lambda g, hh=hh: g.affine_select(
            mask4[:, hh * 128:(hh + 1) * 128], onesf[:, :], [[1, 128]],
            ALU.is_ge, 0.0, base=0, channel_multiplier=-1),
           reads=(b_onesf,), writes=(b_mask,))
    ew("v", lambda v: v.tensor_scalar_mul(bgk2n[:, :], bgk2n[:, :], -1.0),
       reads=(b_vecs,), writes=(b_vecs,))
    for k8 in range(8):
        ew("t", lambda t, k8=k8: t.matmul(
            PP[2][:1, k8 * 128:(k8 + 1) * 128], consts[:, 46 + k8:47 + k8],
            identf[:, :], start=True, stop=True),
           reads=(b_consts, b_ident), writes=(b_PP[2],), inc=(k8 == 7))
    ew("v", lambda v: v.tensor_copy(bout_row[:1, :], PP[2][:1, :]),
       reads=(b_PP[2],), writes=(b_boutrow,))
    for ns in range(2):
        ew("t", lambda t, ns=ns: t.matmul(
            PP[3][:, ns * 512:(ns + 1) * 512], onesc[:1, :],
            bout_row[:1, ns * 512:(ns + 1) * 512], start=True, stop=True),
           reads=(b_onesc, b_boutrow), writes=(b_PP[3],), inc=(ns == 1))
    ew("v", lambda v: v.tensor_copy(boutb[:, :], PP[3][:, :]),
       reads=(b_PP[3],), writes=(b_vecs,))
    ew("v", lambda v: v.memset(S_bfc[:, :], 0.0),
       writes=tuple(b_Sms) + (b_Sb,))
    ew("v", lambda v: v.memset(AMX[:, :], 0.0), writes=(b_AMX,))
    ew("v", lambda v: v.memset(AMH[:, :], 0.0), writes=(b_AMH,))

    for kc in range(8):
        P.dma("g", LDW, out=Wgk1_s[:, kc * 16:(kc + 1) * 16],
              in_=wgk1_h[kc * 128:(kc + 1) * 128, :], dram_final=(b_Wgk,))
        P.dma("g", LDW, out=wgbf_d[kc], in_=wg_h[kc * 128:(kc + 1) * 128, :],
              dram_final=(b_wgbf[kc],))
        P.dma("g", LDW, out=wvbf_d[kc], in_=wv_h[kc * 128:(kc + 1) * 128, :],
              dram_final=(b_wvbf[kc],))
        P.dma("g", LDW,
              out=wqc_d[:, :, kc * 128:(kc + 1) * 128],
              in_=wq_h[kc * 128:(kc + 1) * 128, :].rearrange(
                  "p (m c) -> m p c", m=8, c=128),
              dram_final=(b_wqc[kc],))
        P.dma("g", LDW,
              out=wkc_d[:, :, kc * 128:(kc + 1) * 128],
              in_=wk_h[kc * 128:(kc + 1) * 128, :].rearrange(
                  "p (m c) -> m p c", m=8, c=128),
              dram_final=(b_wkc[kc],))
    P.dma("g", LDW, out=Wgk2_s[:16, :], in_=wgk2_h[:, :], dram_final=(b_Wgk,))
    for vc in range(16):
        pb = b_wpb[vc % 2]
        P.dma("y", WPB[vc % 2], out=wpb[vc % 2][:, :],
              in_=wo_h[vc * 128:(vc + 1) * 128, :], writes=(pb,))
        ew("v", lambda v, vc=vc: v.tensor_scalar_mul(
            wqt[:, :], wpb[vc % 2][:, :], rms4[:, vc % 4:vc % 4 + 1]),
           reads=(pb, b_vecs), writes=(b_wqt,))
        P.dma("y", WOBF_S, out=wobf_d[vc], in_=wqt[:, :], reads=(b_wqt,),
              dram_final=(b_wobf[vc],))

    SIMF = os.environ.get("BASS_SIM_FP64") == "1"

    def emit_round(src_ap, src_bufs, scale_ap, scale_buf, out_ap, out_buf, w):
        ew("v", lambda v: v.tensor_scalar_mul(sci[:, :w], src_ap, scale_ap),
           reads=tuple(src_bufs) + (scale_buf,), writes=(b_sci,))
        if not SIMF:
            ew("v", lambda v: v.tensor_copy(out_ap, sci[:, :w]),
               reads=(b_sci,), writes=(out_buf,))
            return
        ew("v", lambda v: v.tensor_scalar_mul(sc0[:, :w], src_ap, scale_ap),
           reads=tuple(src_bufs) + (scale_buf,), writes=(b_sc0,))
        ew("v", lambda v: v.tensor_sub(sc1[:, :w], sc0[:, :w], sci[:, :w]),
           reads=(b_sc0, b_sci), writes=(b_sc1,))
        ew("v", lambda v: v.tensor_single_scalar(sc2[:, :w], sc1[:, :w], 0.5,
                                                 ALU.is_gt),
           reads=(b_sc1,), writes=(b_sc2,))
        ew("v", lambda v: v.tensor_single_scalar(sc1[:, :w], sc1[:, :w], -0.5,
                                                 ALU.is_lt),
           reads=(b_sc1,), writes=(b_sc1,))
        ew("v", lambda v: v.scalar_tensor_tensor(sc2[:, :w], sc1[:, :w], -1.0,
                                                 sc2[:, :w], ALU.mult, ALU.add),
           reads=(b_sc1, b_sc2), writes=(b_sc2,))
        ew("v", lambda v: v.tensor_add(out_ap, sci[:, :w], sc2[:, :w]),
           reads=(b_sci, b_sc2), writes=(out_buf,))

    def emit_round_w(out_buf, dst=None, src=None, bsrc=None):
        if dst is None:
            dst = wqt
        if src is None:
            src, bsrc = sc3, b_sc3
        ew("v", lambda v, src=src: v.tensor_copy(sci[:, :1024], src[:, :]),
           reads=(bsrc,), writes=(b_sci,))
        if not SIMF:
            ew("v", lambda v, dst=dst: v.tensor_copy(dst[:, :], sci[:, :1024]),
               reads=(b_sci,), writes=(out_buf,))
            return
        ew("v", lambda v, src=src: v.tensor_sub(scB2[:, :], src[:, :],
                                                sci[:, :1024]),
           reads=(bsrc, b_sci), writes=(b_scB1,))
        ew("v", lambda v, src=src: v.tensor_single_scalar(
            src[:, :], scB2[:, :], 0.5, ALU.is_gt),
           reads=(b_scB1,), writes=(bsrc,))
        ew("v", lambda v: v.tensor_single_scalar(scB2[:, :], scB2[:, :], -0.5,
                                                 ALU.is_lt),
           reads=(b_scB1,), writes=(b_scB1,))
        ew("v", lambda v, src=src: v.scalar_tensor_tensor(
            src[:, :], scB2[:, :], -1.0, src[:, :], ALU.mult, ALU.add),
           reads=(bsrc, b_scB1), writes=(bsrc,))
        ew("v", lambda v, dst=dst, src=src: v.tensor_add(
            dst[:, :], sci[:, :1024], src[:, :]),
           reads=(b_sci, bsrc), writes=(out_buf,))

    def emit_win_prep(q):
        wp4 = [wpb[0], wpb[1], wpbX[0], wpbX[1]]
        bw4 = [(b_wpb[0],), (b_wpb[1],), (b_XSIM[4], b_KG[0]),
               (b_XSIM[5], b_KTM[0][0], b_KTM[0][1])]
        cl = slice(q * 1024, (q + 1) * 1024)
        for kc in range(8):
            sl = kc % 4
            pb = bw4[sl][0]
            P.dma("y", WPB[sl], out=wp4[sl][:, :],
                  in_=win_h[kc * 128:(kc + 1) * 128, cl], writes=bw4[sl])
            if kc == 0:
                ew("v", lambda v: v.tensor_copy(wacc[0][:, :], wpb[0][:, :]),
                   reads=(pb,), writes=(b_wacc,))
                ew("v", lambda v: v.tensor_copy(wacc[1][:, :], wpb[0][:, :]),
                   reads=(pb,), writes=(b_wacc,))
            else:
                ew("v", lambda v, sl=sl: v.tensor_max(
                    wacc[0][:, :], wacc[0][:, :], wp4[sl][:, :]),
                   reads=(pb, b_wacc), writes=(b_wacc,))
                ew("v", lambda v, sl=sl: v.tensor_tensor(
                    wacc[1][:, :], wacc[1][:, :], wp4[sl][:, :], ALU.min),
                   reads=(pb, b_wacc), writes=(b_wacc,))
            yield
        ew("v", lambda v: v.scalar_tensor_tensor(wacc[0][:, :], wacc[1][:, :],
                                                 -1.0, wacc[0][:, :],
                                                 ALU.mult, ALU.max),
           reads=(b_wacc,), writes=(b_wacc,))
        for sb in range(8):
            s = q * 8 + sb
            ew("t", lambda t, sb=sb: t.matmul(
                PP[1][:, :128], wacc[0][:, sb * 128:(sb + 1) * 128],
                identf[:, :], start=True, stop=True),
               reads=(b_wacc, b_ident), writes=(b_PP[1],))
            ew("v", lambda v, s=s: v.reduce_max(AMF[:, s:s + 1], PP[1][:, :128],
                                                AX.X),
               reads=(b_PP[1],), writes=(b_AMF,))
            yield
        ew("v", lambda v, q=q: v.tensor_single_scalar(
            AMF[:, q * 8:(q + 1) * 8], AMF[:, q * 8:(q + 1) * 8], 1e-8, ALU.max),
           reads=(b_AMF,), writes=(b_AMF,))
        ew("v", lambda v, q=q: v.reciprocal(wacc[1][:, :8],
                                            AMF[:, q * 8:(q + 1) * 8]),
           reads=(b_AMF,), writes=(b_wacc,))
        ew("v", lambda v: v.tensor_single_scalar(wacc[1][:, :8], wacc[1][:, :8],
                                                 127.0, ALU.mult),
           reads=(b_wacc,), writes=(b_wacc,))
        for sb in range(8):
            ew("t", lambda t, sb=sb: t.matmul(
                PP[1][:1, sb * 128:(sb + 1) * 128], wacc[1][:, sb:sb + 1],
                identf[:, :], start=True, stop=True),
               reads=(b_wacc, b_ident), writes=(b_PP[1],), inc=(sb == 7))
        ew("v", lambda v: v.tensor_copy(wacc[0][:1, :1024], PP[1][:1, :]),
           reads=(b_PP[1],), writes=(b_wacc,))
        for nb in range(2):
            ew("t", lambda t, nb=nb: t.matmul(
                PP[1][:, nb * 512:(nb + 1) * 512], onesc[:1, :],
                wacc[0][:1, nb * 512:(nb + 1) * 512], start=True, stop=True),
               reads=(b_onesc, b_wacc), writes=(b_PP[1],), inc=(nb == 1))
        ew("v", lambda v: v.tensor_copy(wacc[1][:, :], PP[1][:, :]),
           reads=(b_PP[1],), writes=(b_wacc,))
        for kc in range(8):
            sl = kc % 4
            pb = bw4[sl][0]
            P.dma("y", WPB[sl], out=wp4[sl][:, :],
                  in_=win_h[kc * 128:(kc + 1) * 128, cl], writes=bw4[sl])
            ew("v", lambda v, sl=sl: v.tensor_mul(wacc[0][:, :], wp4[sl][:, :],
                                                  wacc[1][:, :]),
               reads=(pb, b_wacc), writes=(b_wacc,))
            wq = wqt if kc % 2 == 0 else wqt2
            bwq = b_wqt if kc % 2 == 0 else b_xlnt[0]
            emit_round_w(bwq, dst=wq, src=wacc[0], bsrc=b_wacc)
            j = kc % 2
            bl = b_wqin if j == 0 else b_wqinB
            P.dma("y", WQIN_SL[q][j],
                  out=wqin_d[q * 8:(q + 1) * 8, :, kc * 128:(kc + 1) * 128]
                  .rearrange("s p c -> p s c"),
                  in_=wq.rearrange("q (sb c) -> q sb c", sb=8, c=128),
                  reads=(bwq,),
                  writes=tuple(bl[q * 8 + sb] for sb in range(8)))
            yield

    def emit_wout_prep(c):
        YIELD = True
        pb = b_wpb[c % 2]
        P.dma("y", WPB[c % 2], out=wpb[c % 2][:, :],
              in_=wout_h[c * 128:(c + 1) * 128, :], writes=(pb,))
        ew("v", lambda v, c=c: v.reduce_max(SOC[:, c:c + 1], wpb[c % 2][:, :],
                                            AX.X, apply_absolute_value=True),
           reads=(pb,), writes=(b_SOC,))
        ew("v", lambda v, c=c: v.tensor_single_scalar(
            SOC[:, c:c + 1], SOC[:, c:c + 1], 1e-8, ALU.max),
           reads=(b_SOC,), writes=(b_SOC,))
        ew("v", lambda v, c=c: v.reciprocal(RSOC[:, c:c + 1], SOC[:, c:c + 1]),
           reads=(b_SOC,), writes=(b_SOC,))
        ew("v", lambda v, c=c: v.tensor_scalar(wacc[1][:, :], wpb[c % 2][:, :],
                                               RSOC[:, c:c + 1], 127.0,
                                               ALU.mult, ALU.mult),
           reads=(pb, b_SOC), writes=(b_wacc,))
        emit_round_w(b_wqt, src=wacc[1], bsrc=b_wacc)
        P.dma("y", WOUTQ_S, out=woutq_d[c], in_=wqt[:, :], reads=(b_wqt,),
              dram_final=(b_woutq[c],))
        yield

    def bank(i):
        return PP[2 + i // 2][:, (i % 2) * 512:(i % 2 + 1) * 512]

    def bbank(i):
        return b_PP[2 + i // 2]

    def front(g):
        p = g % 2
        t0g = g * G
        x_g, X_fm, QG, KG = x_gP[p], X_fmP[p], QGP[p], KGP[p]
        KTM, VT, SGT, xlnt = KTMP[p], VTP[p], SGTP[p], xlntP[p]
        bn12f, mvgf = bnF[p], mvF[p]
        stAf, stBf = stAF[p], stBF[p]
        g63, g127, egln, egls = GL63[p], GL127[p], EGLN[p], EGLS[p]

        bbn, bmv, bst = b_bnF[p], b_mvF[p], b_stF[p]
        bgl, bz1 = b_GLP[p], b_z1P[p]

        for t in range(G):
            P.dma("y", LDX[p][t], out=x_g[:, t * 1024:(t + 1) * 1024],
                  in_=x_h[(t0g + t) * 128:(t0g + t + 1) * 128, :],
                  writes=(b_xg[p][t],))
        for t in range(G):
            xs = x_g[:, t * 1024:(t + 1) * 1024]
            ew("v", lambda v, xs=xs, bn12f=bn12f: v.bn_stats(bn12f[:, :6], xs[:, :512]),
               reads=(b_xg[p][t],), writes=(bbn,))
            ew("v", lambda v, xs=xs, bn12f=bn12f: v.bn_stats(bn12f[:, 6:12], xs[:, 512:]),
               reads=(b_xg[p][t],), writes=(bbn,))
            ew("v", lambda v, t=t, bn12f=bn12f, mvgf=mvgf: v.bn_aggr(
                mvgf[:, 2 * t:2 * t + 2], bn12f[:, :]),
               reads=(bbn,), writes=(bmv,))
        ew("s", lambda s, mvgf=mvgf, stAf=stAf: s.activation(
            stAf[:, :G], mvgf[:, 1::2], AF.Ln, bias=cst(C_EPS), scale=1.0),
           reads=(bmv, b_consts), writes=(bst,))
        ew("s", lambda s, stAf=stAf: s.activation(stAf[:, :G], stAf[:, :G],
                                                  AF.Exp, bias=0.0, scale=-0.5),
           reads=(bst,), writes=(bst,))
        ew("v", lambda v, mvgf=mvgf, stBf=stBf: v.tensor_scalar_mul(
            stBf[:, :G], mvgf[:, 0::2], -1.0),
           reads=(bmv,), writes=(bst,))
        for t in range(G):
            ew("v", lambda v, t=t, xlnt=xlnt, x_g=x_g, stAf=stAf, stBf=stBf:
               v.tensor_scalar(xlnt[:, :], x_g[:, t * 1024:(t + 1) * 1024],
                               stBf[:, t:t + 1], stAf[:, t:t + 1],
                               ALU.add, ALU.mult),
               reads=(b_xg[p][t], bst), writes=(b_xlnt[p],))
            for half in range(2):
                pb = PP[0][:, half * 512:(half + 1) * 512]
                for j in range(4):
                    kc = half * 4 + j
                    ew("t", lambda t_, kc=kc, j=j, xlnt=xlnt, pb=pb: t_.matmul(
                        pb[:, j * 128:(j + 1) * 128],
                        xlnt[:, kc * 128:(kc + 1) * 128], identb[:, :],
                        start=True, stop=True),
                       reads=(b_xlnt[p], b_ident), writes=(b_PP[0],),
                       inc=(j == 3))
                dst = X_fm.rearrange("q (k c) -> q k c", k=8, c=W)[
                    :, half * 4:(half + 1) * 4, t * 128:(t + 1) * 128]
                ew("s", lambda s, dst=dst, pb=pb: s.copy(dst, pb[:, :512]),
                   reads=(b_PP[0],), writes=(b_Xfm[p],))
        yield
        for kc in range(8):
            ew("t", lambda t_, kc=kc, X_fm=X_fm: t_.matmul(
                PP[1][:16, :W], Wgk1_s[:, kc * 16:(kc + 1) * 16],
                X_fm[:, kc * W:(kc + 1) * W], start=(kc == 0), stop=(kc == 7)),
               reads=(b_Wgk, b_Xfm[p]), writes=(b_PP[1],), inc=(kc == 7))
        ew("s", lambda s, z1bp=z1b[p]: s.copy(z1bp[:16, :], PP[1][:16, :W]),
           reads=(b_PP[1],), writes=(bz1,))

        def gates(ms):
            sQ = scq[p * 2 + ms % 2]
            sK = sck[p * 2 + ms % 2]
            bQ = b_scq[p * 2 + ms % 2]
            bK = b_sck[p * 2 + ms % 2]
            sA = scg[2 * p + ms % 2]
            bA = b_scg[2 * p + ms % 2]
            ew("t", lambda t_, ms=ms, z1bp=z1b[p]: t_.matmul(
                PP[0][:, :W], Wgk2_s[:16, ms * 128:(ms + 1) * 128],
                z1bp[:16, :], start=True, stop=True),
               reads=(b_Wgk, bz1), writes=(b_PP[0],))
            ew("v", lambda v, ms=ms, sA=sA: v.tensor_scalar(
                sA[:, :W], PP[0][:, :W], bgk2n[:, ms:ms + 1], -1.0,
                ALU.subtract, ALU.mult),
               reads=(b_PP[0], b_vecs), writes=(bA,))
            ew("v", lambda v, sA=sA, sK=sK: v.tensor_single_scalar(
                sK[:, :W], sA[:, :W], 0.0, ALU.max),
               reads=(bA,), writes=(bK,))
            ew("s", lambda s, sA=sA, sQ=sQ: s.activation(sQ[:, :W], sA[:, :W],
                                                         AF.Abs),
               reads=(bA,), writes=(bQ,))
            ew("s", lambda s, sQ=sQ: s.activation(sQ[:, :W], sQ[:, :W], AF.Exp,
                                                  bias=0.0, scale=-1.0),
               reads=(bQ,), writes=(bQ,))
            ew("s", lambda s, sQ=sQ: s.activation(sQ[:, :W], sQ[:, :W], AF.Ln,
                                                  bias=1.0, scale=1.0),
               reads=(bQ,), writes=(bQ,))
            ew("v", lambda v, sA=sA, sQ=sQ, sK=sK: v.tensor_add(
                sA[:, :W], sK[:, :W], sQ[:, :W]),
               reads=(bK, bQ), writes=(bA,))
            ew("s", lambda s, sA=sA, sQ=sQ: s.activation(
                sQ[:, :W], sA[:, :W], AF.Exp, bias=cst(C_LN16), scale=-1.0 / 16),
               reads=(bA, b_consts), writes=(bQ,))
            ew("s", lambda s, sA=sA, sK=sK: s.activation(
                sK[:, :W], sA[:, :W], AF.Exp, bias=0.0, scale=1.0 / 16),
               reads=(bA,), writes=(bK,))
            for t in range(G):
                ew("v", lambda v, ms=ms, t=t, sK=sK, g63=g63: v.tensor_copy(
                    g63[:, ms * G + t:ms * G + t + 1],
                    sK[:, t * 128 + 63:t * 128 + 64]),
                   reads=(bK,), writes=(bgl,))
                ew("v", lambda v, ms=ms, t=t, sK=sK, g127=g127: v.tensor_copy(
                    g127[:, ms * G + t:ms * G + t + 1],
                    sK[:, t * 128 + 127:t * 128 + 128]),
                   reads=(bK,), writes=(bgl,))
            ew("v", lambda v, ms=ms, egln=egln, g63=g63: v.reciprocal(
                egln[:, ms * G:(ms + 1) * G], g63[:, ms * G:(ms + 1) * G]),
               reads=(bgl,), writes=(bgl,))
            ew("v", lambda v, ms=ms, egls=egls, g63=g63, g127=g127: v.tensor_mul(
                egls[:, ms * G:(ms + 1) * G], g63[:, ms * G:(ms + 1) * G],
                g127[:, ms * G:(ms + 1) * G]),
               reads=(bgl,), writes=(bgl,))
            ew("v", lambda v, ms=ms, egls=egls: v.reciprocal(
                egls[:, ms * G:(ms + 1) * G], egls[:, ms * G:(ms + 1) * G]),
               reads=(bgl,), writes=(bgl,))
            for t in range(G):
                h2 = slice(t * 128 + 64, t * 128 + 128)
                ew("v", lambda v, ms=ms, t=t, h2=h2, sQ=sQ, egln=egln:
                   v.tensor_scalar_mul(sQ[:, h2], sQ[:, h2],
                                       egln[:, ms * G + t:ms * G + t + 1]),
                   reads=(bgl,), writes=(bQ,))
                ew("v", lambda v, ms=ms, t=t, h2=h2, sK=sK, g63=g63:
                   v.tensor_scalar_mul(sK[:, h2], sK[:, h2],
                                       g63[:, ms * G + t:ms * G + t + 1]),
                   reads=(bgl,), writes=(bK,))

        def qkproj(ms):
            sQ = scq[p * 2 + ms % 2]
            sK = sck[p * 2 + ms % 2]
            bQ = b_scq[p * 2 + ms % 2]
            bK = b_sck[p * 2 + ms % 2]
            wq_ = wqb[ms % 2]
            bwq = b_wqb[ms % 2]
            wk_ = wkb[ms % 2]
            bwk = b_wkb[ms % 2]
            for kc in range(8):
                ew("t", lambda t_, ms=ms, kc=kc, wq_=wq_, X_fm=X_fm: t_.matmul(
                    PP[1][:, :W], wq_[:, kc * 128:(kc + 1) * 128],
                    X_fm[:, kc * W:(kc + 1) * W], start=(kc == 0), stop=(kc == 7)),
                   reads=(bwq, b_Xfm[p]), writes=(b_PP[1],), inc=(kc == 7))
            ew("v", lambda v, ms=ms, QG=QG, sQ=sQ: v.tensor_mul(
                QG[:, ms * W:(ms + 1) * W], PP[1][:, :W], sQ[:, :W]),
               reads=(b_PP[1], bQ), writes=(b_QG[p],))
            for kc in range(8):
                ew("t", lambda t_, ms=ms, kc=kc, wk_=wk_, X_fm=X_fm: t_.matmul(
                    PP[1][:, 512:512 + W], wk_[:, kc * 128:(kc + 1) * 128],
                    X_fm[:, kc * W:(kc + 1) * W], start=(kc == 0), stop=(kc == 7)),
                   reads=(bwk, b_Xfm[p]), writes=(b_PP[1],), inc=(kc == 7))
            ew("v", lambda v, ms=ms, KG=KG, sK=sK: v.tensor_mul(
                KG[:, ms * W:(ms + 1) * W], PP[1][:, 512:512 + W], sK[:, :W]),
               reads=(b_PP[1], bK), writes=(b_KG[p],))

        for ms in range(8):
            P.dma("y", WQB[ms % 2], out=wqb[ms % 2][:, :], in_=wqc_d[ms],
                  reads=tuple(b_wqc), writes=(b_wqb[ms % 2],))
            P.dma("y", WKB[ms % 2], out=wkb[ms % 2][:, :], in_=wkc_d[ms],
                  reads=tuple(b_wkc), writes=(b_wkb[ms % 2],))
            gates(ms)
            if ms >= 1:
                qkproj(ms - 1)
            yield
        qkproj(7)
        yield
        for t in range(G):
            for half in range(2):
                pb = PP[0][:, half * 512:(half + 1) * 512]
                for j in range(4):
                    ms = half * 4 + j
                    ew("t", lambda t_, ms=ms, j=j, t=t, KG=KG, pb=pb: t_.matmul(
                        pb[:, j * 128:(j + 1) * 128],
                        KG[:, ms * W + t * 128: ms * W + (t + 1) * 128],
                        identb[:, :], start=True, stop=True),
                       reads=(b_KG[p], b_ident), writes=(b_PP[0],), inc=(j == 3))
                ew("s", lambda s, half=half, t=t, KTM=KTM, pb=pb: s.copy(
                    KTM[t][:, half * 512:(half + 1) * 512], pb[:, :512]),
                   reads=(b_PP[0],), writes=(b_KTM[p][t],))
            yield
        for which, wbuf, bwbuf, WSTR, src_d, b_src, dstP, bdstP in (
                ("gt", [wgb[0], wgb[1], wkb[0], wkb[1]],
                 [b_wgb[0], b_wgb[1], b_wkb[0], b_wkb[1]],
                 [WGB[0], WGB[1], WKB[0], WKB[1]],
                 wgbf_d, b_wgbf, SGT, b_SGT[p]),
                ("v", [wvb[0], wvb[1], wqb[0], wqb[1]],
                 [b_wvb[0], b_wvb[1], b_wqb[0], b_wqb[1]],
                 [WVB[0], WVB[1], WQB[0], WQB[1]],
                 wvbf_d, b_wvbf, VT, b_VT[p])):
            for ns2 in range(2):
                for kc in range(8):
                    wb = wbuf[kc % 4]
                    bwb = bwbuf[kc % 4]
                    P.dma("y", WSTR[kc % 4], out=wb[:, :],
                          in_=src_d[kc][:, ns2 * 1024:(ns2 + 1) * 1024],
                          reads=(b_src[kc],), writes=(bwb,))
                    for t in range(G):
                        for nsi in range(2):
                            bk = PP[t][:, nsi * 512:(nsi + 1) * 512]
                            ew("t", lambda t_, kc=kc, t=t, nsi=nsi, wb=wb,
                               bk=bk, X_fm=X_fm: t_.matmul(
                                   bk,
                                   X_fm[:, kc * W + t * 128: kc * W + (t + 1) * 128],
                                   wb[:, nsi * 512:(nsi + 1) * 512],
                                   start=(kc == 0), stop=(kc == 7)),
                               reads=(b_Xfm[p], bwb), writes=(b_PP[t],),
                               inc=(kc == 7 or (t == G - 1 and nsi == 1)))
                yield
                for t in range(G):
                    for nsi in range(2):
                        q4 = ns2 * 2 + nsi
                        bk = PP[t][:, nsi * 512:(nsi + 1) * 512]
                        if which == "v":
                            ew("s", lambda s, t=t, q4=q4, bk=bk, dstP=dstP: s.copy(
                                dstP[t][:, q4 * 512:(q4 + 1) * 512], bk),
                               reads=(b_PP[t],), writes=(bdstP[t],))
                        else:
                            sS = scg[2 * p + (t * 2 + nsi) % 2]
                            bS = b_scg[2 * p + (t * 2 + nsi) % 2]
                            ew("s", lambda s, bk=bk, sS=sS: s.activation(
                                sS[:, :512], bk, AF.Exp, bias=0.0, scale=-1.0),
                               reads=(b_PP[t],), writes=(bS,))
                            ew("v", lambda v, sS=sS: v.tensor_single_scalar(
                                sS[:, :512], sS[:, :512], 1e30, ALU.min),
                               reads=(bS,), writes=(bS,))
                            ew("s", lambda s, sS=sS: s.activation(
                                sS[:, :512], sS[:, :512], AF.Ln, bias=1.0,
                                scale=1.0),
                               reads=(bS,), writes=(bS,))
                            ew("s", lambda s, sS=sS: s.activation(
                                sS[:, :512], sS[:, :512], AF.Exp, bias=0.0,
                                scale=-1.0),
                               reads=(bS,), writes=(bS,))
                            ew("v", lambda v, t=t, q4=q4, bk=bk, sS=sS,
                               dstP=dstP: v.tensor_mul(
                                   dstP[t][:, q4 * 512:(q4 + 1) * 512], bk,
                                   sS[:, :512]),
                               reads=(b_PP[t], bS), writes=(bdstP[t],))

        yield

    def scanback(g):
        p = g % 2
        t0g = g * G
        x_g, QG, KG = x_gP[p], QGP[p], KGP[p]
        KTM, VT, SGT = KTMP[p], VTP[p], SGTP[p]
        g63, g127, egln, egls = GL63[p], GL127[p], EGLN[p], EGLS[p]
        bgl = b_GLP[p]
        AB = ABUF[p]
        bAB = b_xlnt[p]
        OF_l = [OF, OF2B[p]]
        bOF_l = [b_OF, b_QG[p]]
        OFm_l = [OFm, OFKB[p]]
        bOFm_l = [b_OFm, b_KG[p]]
        for t in range(G):
            tile = t0g + t
            OFt = OF_l[t]
            bOFt = bOF_l[t]
            for h in range(4):
                for ci in range(2):
                    ms = 2 * h + ci
                    ew("t", lambda t_, h=h, ms=ms, ci=ci, t=t, KG=KG, QG=QG:
                       t_.matmul(bank(0)[:, h * 128:(h + 1) * 128],
                                 KG[:, ms * W + t * 128: ms * W + (t + 1) * 128],
                                 QG[:, ms * W + t * 128: ms * W + (t + 1) * 128],
                                 start=(ci == 0), stop=(ci == 1)),
                       reads=(b_KG[p], b_QG[p]), writes=(bbank(0),),
                       inc=(h == 3 and ci == 1))
            ew("v", lambda v, AB=AB: v.tensor_mul(AB[:, :512], bank(0),
                                                  mask4[:, :]),
               reads=(bbank(0), b_mask), writes=(bAB,))
            yield
            obk = [1, 2, 3, 0]
            for h in range(4):
                bi = obk[h]
                for ci in range(2):
                    ms = 2 * h + ci
                    ew("t", lambda t_, ms=ms, t=t, ci=ci, bi=bi, QG=QG: t_.matmul(
                        bank(bi), QG[:, ms * W + t * 128: ms * W + (t + 1) * 128],
                        S_bfc[:, ms * 512:(ms + 1) * 512],
                        start=(ci == 0), stop=False),
                       reads=(b_QG[p], b_Sms[ms]), writes=(bbank(bi),),
                       inc=(ci == 1))
                ew("t", lambda t_, h=h, t=t, bi=bi, VT=VT, AB=AB: t_.matmul(
                    bank(bi), AB[:, h * 128:(h + 1) * 128],
                    VT[t][:, h * 512:(h + 1) * 512], start=False, stop=True),
                   reads=(bAB, b_VT[p][t]), writes=(bbank(bi),), inc=True)
            yield
            for h in range(4):
                bi = obk[h]
                ew("v", lambda v, bi=bi: v.bn_stats(bn12[:, :6], bank(bi)),
                   reads=(bbank(bi),), writes=(b_bn,))
                ew("v", lambda v, h=h: v.bn_aggr(mo8[:, 2 * h:2 * h + 2],
                                                 bn12[:, :6]),
                   reads=(b_bn,), writes=(b_mo,))
            ew("v", lambda v: v.scalar_tensor_tensor(
                rsO[:, :4], mo8[:, 0::2], 1.0, mo8[:, 0::2], ALU.mult, ALU.mult),
               reads=(b_mo,), writes=(b_rsO,))
            ew("v", lambda v: v.tensor_add(rsO[:, :4], rsO[:, :4], mo8[:, 1::2]),
               reads=(b_mo,), writes=(b_rsO,))
            ew("s", lambda s: s.activation(rsO[:, :4], rsO[:, :4], AF.Ln,
                                           bias=cst(C_EPS), scale=1.0),
               reads=(b_consts,), writes=(b_rsO,))
            ew("s", lambda s: s.activation(rsO[:, :4], rsO[:, :4], AF.Exp,
                                           bias=0.0, scale=-0.5),
               reads=(), writes=(b_rsO,))
            yield
            for h in range(4):
                bi = obk[h]
                ew("v", lambda v, h=h, bi=bi, t=t, SGT=SGT, OFt=OFt:
                   v.scalar_tensor_tensor(
                       OFt[:, h * 512:(h + 1) * 512], bank(bi), rsO[:, h:h + 1],
                       SGT[t][:, h * 512:(h + 1) * 512], ALU.mult, ALU.mult),
                   reads=(bbank(bi), b_rsO, b_SGT[p][t]), writes=(bOFt,))
                yield
            for h in range(4):
                for ci in range(2):
                    ms = 2 * h + ci
                    ew("t", lambda t_, ms=ms, t=t, ci=ci, KTM=KTM, VT=VT: t_.matmul(
                        bank(2 + ci), KTM[t][:, ms * 128:(ms + 1) * 128],
                        VT[t][:, (ms // 2) * 512:(ms // 2 + 1) * 512],
                        start=True, stop=True),
                       reads=(b_KTM[p][t], b_VT[p][t]), writes=(bbank(2 + ci),),
                       inc=True)
                ew("v", lambda v, h=h: v.tensor_add(
                    scB2[:, :], S_bfc[:, h * 1024:(h + 1) * 1024], PP[3][:, :]),
                   reads=(b_Sms[2 * h], b_Sms[2 * h + 1], b_PP[3]),
                   writes=(b_scB1,))
                for ci in range(2):
                    ms = 2 * h + ci
                    ew("v", lambda v, ms=ms, t=t, ci=ci, egls=egls:
                       v.tensor_scalar_mul(
                           S_bfc[:, ms * 512:(ms + 1) * 512],
                           scB2[:, ci * 512:(ci + 1) * 512],
                           egls[:, ms * G + t:ms * G + t + 1]),
                       reads=(b_scB1, bgl), writes=(b_Sms[ms],))
                yield
            yield
        for t in range(G):
            OFt, bOFt = OF_l[t], bOF_l[t]
            OMt, bOMt = OFm_l[t], bOFm_l[t]
            for half in range(4):
                bi = 3 if half % 2 == 0 else 0
                for j in range(4):
                    vc = half * 4 + j
                    ew("t", lambda t_, vc=vc, j=j, bi=bi, OFt=OFt: t_.matmul(
                        bank(bi)[:, j * 128:(j + 1) * 128],
                        OFt[:, vc * 128:(vc + 1) * 128], identb[:, :],
                        start=True, stop=True),
                       reads=(bOFt, b_ident), writes=(bbank(bi),), inc=(j == 3))
                ew("s", lambda s, half=half, bi=bi, OMt=OMt: s.copy(
                    OMt[:, half * 512:(half + 1) * 512], bank(bi)),
                   reads=(bbank(bi),), writes=(bOMt,))
            yield
        WOL = [wsb[0], wsb[1], wgb[0], wgb[1]]
        bWOL = [b_wsb[0], b_wsb[1], b_wgb[0], b_wgb[1]]
        SOL = [LWS[0], LWS[1], WGB[0], WGB[1]]
        for vc in range(16):
            wb = WOL[vc % 4]
            bwb = bWOL[vc % 4]
            P.dma("y", SOL[vc % 4], out=wb[:, :1024], in_=wobf_d[vc],
                  reads=(b_wobf[vc],), writes=(bwb,))
            for t in range(G):
                ew("t", lambda t_, vc=vc, wb=wb, t=t: t_.matmul(
                    PP[2 + t][:, 0:512],
                    OFm_l[t][:, vc * 128:(vc + 1) * 128],
                    wb[:, 0:512],
                    start=(vc == 0), stop=(vc == 15)),
                   reads=(bOFm_l[t], bwb), writes=(b_PP[2 + t],), inc=False)
                ew("t", lambda t_, vc=vc, wb=wb, t=t: t_.matmul(
                    PP[2 + t][:, 512:1024],
                    OFm_l[t][:, vc * 128:(vc + 1) * 128],
                    wb[:, 512:1024],
                    start=(vc == 0), stop=(vc == 15)),
                   reads=(bOFm_l[t], bwb), writes=(b_PP[2 + t],), inc=True)
            if vc % 4 == 3:
                yield
        ew("s", lambda s: s.copy(sc3[:, :], PP[2][:, :]),
           reads=(b_PP[2],), writes=(b_sc3,))
        ew("s", lambda s: s.copy(sc4[:, :], PP[3][:, :]),
           reads=(b_PP[3],), writes=(b_sc4,))

        def ln_chain(t):
            tile = t0g + t
            if t == 0:
                bnT, mvT, stAT, stBT, scT = bn12, mvg, stA, stB, sc3
                bbnT, bmvT, bstT, bscT = b_bn, b_mv, b_st, b_sc3
            else:
                bnT, mvT, stAT, stBT, scT = bnF[p], mvF[p], stAF[p], stBF[p], sc4
                bbnT, bmvT, bstT, bscT = b_bnF[p], b_mvF[p], b_stF[p], b_sc4
            ew("v", lambda v, scT=scT, bnT=bnT: v.bn_stats(
                bnT[:, :6], scT[:, :512]), reads=(bscT,), writes=(bbnT,))
            ew("v", lambda v, scT=scT, bnT=bnT: v.bn_stats(
                bnT[:, 6:12], scT[:, 512:]), reads=(bscT,), writes=(bbnT,))
            yield
            ew("v", lambda v, bnT=bnT, mvT=mvT: v.bn_aggr(mvT[:, :2], bnT[:, :]),
               reads=(bbnT,), writes=(bmvT,))
            yield
            ew("s", lambda s, mvT=mvT, stAT=stAT: s.activation(
                stAT[:, :1], mvT[:, 1:2], AF.Ln, bias=cst(C_EPS), scale=1.0),
               reads=(bmvT, b_consts), writes=(bstT,))
            ew("s", lambda s, stAT=stAT: s.activation(
                stAT[:, :1], stAT[:, :1], AF.Exp, bias=0.0, scale=-0.5),
               reads=(), writes=(bstT,))
            ew("v", lambda v, mvT=mvT, stBT=stBT: v.tensor_scalar_mul(
                stBT[:, :1], mvT[:, 0:1], -1.0),
               reads=(bmvT,), writes=(bstT,))
            yield
            ew("v", lambda v, scT=scT, stAT=stAT, stBT=stBT:
               v.tensor_scalar(scT[:, :], scT[:, :], stBT[:, :1], stAT[:, :1],
                               ALU.add, ALU.mult),
               reads=(bstT,), writes=(bscT,))
            yield
            xb = xpb[t % 2]
            bxb = b_xpb[t % 2]
            ew("v", lambda v, t=t, xb=xb, x_g=x_g, scT=scT: v.tensor_add(
                xb[:, :], scT[:, :], x_g[:, t * 1024:(t + 1) * 1024]),
               reads=(bscT, b_xg[p][t]), writes=(bxb,))
            P.dma("y", XPS[t % 2], out=xp_d[tile * 128:(tile + 1) * 128, :],
                  in_=xb[:, :], reads=(bxb,), dram_final=(b_xp_t[tile],))
            yield
            ew("v", lambda v, xb=xb, bnT=bnT: v.bn_stats(bnT[:, :6], xb[:, :512]),
               reads=(bxb,), writes=(bbnT,))
            ew("v", lambda v, xb=xb, bnT=bnT: v.bn_stats(bnT[:, 6:12], xb[:, 512:]),
               reads=(bxb,), writes=(bbnT,))
            yield
            ew("v", lambda v, bnT=bnT, mvT=mvT: v.bn_aggr(mvT[:, :2], bnT[:, :]),
               reads=(bbnT,), writes=(bmvT,))
            yield
            ew("s", lambda s, mvT=mvT, stAT=stAT: s.activation(
                stAT[:, :1], mvT[:, 1:2], AF.Ln, bias=cst(C_EPS), scale=1.0),
               reads=(bmvT, b_consts), writes=(bstT,))
            ew("s", lambda s, stAT=stAT: s.activation(
                stAT[:, :1], stAT[:, :1], AF.Exp, bias=0.0, scale=-0.5),
               reads=(), writes=(bstT,))
            ew("v", lambda v, mvT=mvT, stBT=stBT: v.tensor_scalar_mul(
                stBT[:, :1], mvT[:, 0:1], -1.0),
               reads=(bmvT,), writes=(bstT,))
            yield
            x3 = xl3b[t % 2]
            bx3 = b_xl3b[t % 2]
            ew("v", lambda v, xb=xb, x3=x3, stAT=stAT, stBT=stBT:
               v.tensor_scalar(x3[:, :], xb[:, :], stBT[:, :1], stAT[:, :1],
                               ALU.add, ALU.mult),
               reads=(bxb, bstT), writes=(bx3,))
            yield
            ew("v", lambda v, x3=x3, t=t: v.reduce_max(
                rtA[:, t % 2:t % 2 + 1], x3[:, :], AX.X,
                apply_absolute_value=True),
               reads=(bx3,), writes=(b_rtA,))
            ew("v", lambda v, t=t: v.tensor_max(AMX[:, :], AMX[:, :],
                                                rtA[:, t % 2:t % 2 + 1]),
               reads=(b_rtA,), writes=(b_AMX,))
            P.dma("y", X3S[t % 2], out=xl3_d[tile * 128:(tile + 1) * 128, :],
                  in_=x3[:, :], reads=(bx3,), dram_final=(b_xl3_t[tile],))
            yield

        chains = [ln_chain(0), ln_chain(1)]
        while chains:
            for c in list(chains):
                try:
                    next(c)
                except StopIteration:
                    chains.remove(c)
            yield

    def emit_allreduce(amx_buf, amx_t, cin_d, cout_d, b_scol, sx_col, rsx_col):
        ew("t", lambda t_: t_.matmul(PP[1][:1, :128], amx_t[:, :], identf[:, :],
                                     start=True, stop=True),
           reads=(amx_buf, b_ident), writes=(b_PP[1],))
        ew("v", lambda v: v.tensor_copy(r1a[:1, :], PP[1][:1, :128]),
           reads=(b_PP[1],), writes=(b_r1,))
        ew("v", lambda v: v.reduce_max(r1b[:1, :1], r1a[:1, :], AX.X),
           reads=(b_r1,), writes=(b_r1,))
        ew("v", lambda v: v.tensor_single_scalar(r1b[:1, :1], r1b[:1, :1],
                                                 1e-8, ALU.max),
           reads=(b_r1,), writes=(b_r1,))
        ew("v", lambda v: v.tensor_copy(ccb[:1, :1], r1b[:1, :1]),
           reads=(b_r1,), writes=(b_ccb,))
        P.dma("g", GDM, out=cin_d[:, :1], in_=ccb[:1, :1], reads=(b_ccb,),
              writes=(b_cc,))
        ew("g", lambda g: g.collective_compute(
            "AllReduce", ALU.max, replica_groups=[list(range(n_cores))],
            ins=[cin_d[:, :1].opt()], outs=[cout_d[:, :1].opt()]),
           reads=(b_cc,), writes=(b_cc,))
        P.dma("g", GDM, out=ccb[:1, :1], in_=cout_d[:, :1], reads=(b_cc,),
              writes=(b_ccb,))
        ew("v", lambda v: v.reciprocal(r1b[:1, :1], ccb[:1, :1]),
           reads=(b_ccb,), writes=(b_r1,))
        ew("v", lambda v: v.tensor_single_scalar(r1b[:1, :1], r1b[:1, :1],
                                                 32767.0, ALU.mult),
           reads=(), writes=(b_r1,))
        ew("t", lambda t_: t_.matmul(PP[1][:, 512:513], onesc[:1, :],
                                     r1b[:1, :1], start=True, stop=True),
           reads=(b_onesc, b_r1), writes=(b_PP[1],))
        ew("v", lambda v: v.tensor_copy(sx_col[:, :], PP[1][:, 512:513]),
           reads=(b_PP[1],), writes=(b_scol,))
        ew("v", lambda v: v.tensor_single_scalar(r1b[:1, :1], ccb[:1, :1],
                                                 1.0 / 32767.0, ALU.mult),
           reads=(b_ccb,), writes=(b_r1,))
        ew("t", lambda t_: t_.matmul(PP[1][:, 512:513], onesc[:1, :],
                                     r1b[:1, :1], start=True, stop=True),
           reads=(b_onesc, b_r1), writes=(b_PP[1],))
        ew("v", lambda v: v.tensor_copy(rsx_col[:, :], PP[1][:, 512:513]),
           reads=(b_PP[1],), writes=(b_scol,))

    SG1 = min(4, NG)

    def f1_pro(u):
        par = (u % 2) * SG1
        for gi in range(SG1):
            g = u * SG1 + gi
            XSIM = XSIMQ[par + gi]
            t0g = g * G
            for t in range(G):
                sl = (gi * G + t) % 4
                lb = xl3l[sl]
                blb = b_xl3l[sl]
                P.dma("y", XL3L[sl], out=lb[:, :],
                      in_=xl3_d[(t0g + t) * 128:(t0g + t + 1) * 128, :],
                      reads=(b_xl3_t[t0g + t],), writes=(blb,))
                for half in range(2):
                    for j in range(4):
                        kc = half * 4 + j
                        ew("t", lambda t_, kc=kc, j=j, lb=lb: t_.matmul(
                            PP[0][:, 512 + j * 128:512 + (j + 1) * 128],
                            lb[:, kc * 128:(kc + 1) * 128], identb[:, :],
                            start=True, stop=True),
                           reads=(blb, b_ident), writes=(b_PP0H,),
                           inc=(j == 3))
                    ew("s", lambda s: s.copy(sc3[:, :512], PP[0][:, 512:1024]),
                       reads=(b_PP0H,), writes=(b_sc3,))
                    xdst = XSIM.rearrange("q (k c) -> q k c", k=8, c=W)[
                        :, half * 4:(half + 1) * 4, t * 128:(t + 1) * 128]
                    emit_round(sc3[:, :512], (b_sc3,), SXC[:, :1], b_sx,
                               xdst, b_XSIM[par + gi], 512)
                    yield
            if dbg:
                P.dma("y", DBGS, out=dXS_d[g], in_=XSIM[:, :],
                      reads=(b_XSIM[par + gi],), writes=(Buf(),))

    def f1_mms(u):
        par = (u % 2) * SG1
        WSL = [wsb[0], wsb[1], wqb[0], wqb[1]]
        bWSL = [b_wsb[0], b_wsb[1], b_wqb[0], b_wqb[1]]
        STL = [LWS[0], LWS[1], WQB[0], WQB[1]]
        for s in range(32):
            wb = WSL[s % 4]
            bwb = bWSL[s % 4]
            P.dma("y", STL[s % 4], out=wb[:, :1024], in_=wqin_d[s],
                  reads=(b_wqin[s], b_wqinB[s]), writes=(bwb,))
            for gi in range(SG1):
                g = u * SG1 + gi
                pp = PP[gi]
                bpp = b_PP[gi]
                for kc in range(8):
                    ew("t", lambda t_, kc=kc, wb=wb, pp=pp, gi=gi, par=par:
                       t_.matmul(
                           pp[:, :W], wb[:, kc * 128:(kc + 1) * 128],
                           XSIMQ[par + gi][:, kc * W:(kc + 1) * W],
                           start=(kc == 0), stop=(kc == 7)),
                       reads=(bwb, b_XSIM[par + gi]), writes=(bpp,),
                       inc=(kc == 7))
                hb = hbb[gi]
                bhb = b_hbb[gi]
                ew("s", lambda s_, s=s, pp=pp, hb=hb: s_.activation(
                    hb[:, :], pp[:, :W], AF.Gelu, bias=binb[:, s:s + 1],
                    scale=ALF[:, s:s + 1]),
                   reads=(bpp, b_ALF, b_vecs), writes=(bhb,))
                P.dma("y", HIDS[gi], out=hid_d[g, s], in_=hb[:, :],
                      reads=(bhb,), dram_final=(b_hid[g][s],))
            for gi in range(SG1):
                ew("v", lambda v, gi=gi: v.reduce_max(
                    rtA[:, gi:gi + 1], hbb[gi][:, :], AX.X,
                    apply_absolute_value=True),
                   reads=(b_hbb[gi],), writes=(b_rtA,))
            ew("v", lambda v: v.reduce_max(
                rtA[:, 4:5], rtA[:, :SG1], AX.X),
               reads=(b_rtA,), writes=(b_rtA,))
            ew("v", lambda v: v.tensor_max(AMH[:, :], AMH[:, :], rtA[:, 4:5]),
               reads=(b_rtA,), writes=(b_AMH,))
            yield

    SG2 = min(2, NG)

    def f2_super(u):
        t0g = u * SG2 * G
        for ti in range(SG2 * G):
            P.dma("y", XPL[ti], out=xpl[ti][:, :],
                  in_=xp_d[(t0g + ti) * 128:(t0g + ti + 1) * 128, :],
                  reads=(b_xp_t[t0g + ti],), writes=(b_xpl[ti],))
        WSL = [wsb[0], wsb[1], wqb[0], wqb[1]]
        bWSL = [b_wsb[0], b_wsb[1], b_wqb[0], b_wqb[1]]
        STL = [LWS[0], LWS[1], WQB[0], WQB[1]]
        for s in range(32):
            wb = WSL[s % 4]
            bwb = bWSL[s % 4]
            P.dma("y", STL[s % 4], out=wb[:, :1024], in_=w2_d[s],
                  reads=(b_w2s[s],), writes=(bwb,))
            for gi in range(SG2):
                g = u * SG2 + gi
                sl = (s * SG2 + gi) % 4
                hl = hlb[sl]
                bhl = b_hlb[sl]
                P.dma("y", HLB[sl], out=hl[:, :], in_=hid_d[g, s],
                      reads=(b_hid[g][s],), writes=(bhl,))
                h2 = h2b[sl]
                bh2 = b_h2b[sl]
                emit_round(hl[:, :], (bhl,), SHC[:, :1], b_sh, h2[:, :],
                           bh2, W)
                for t in range(G):
                    for ns in range(2):
                        ew("t", lambda t_, t=t, ns=ns, h2=h2, wb=wb, gi=gi,
                           s=s: t_.matmul(
                            PP[gi * G + t][:, ns * 512:(ns + 1) * 512],
                            h2[:, t * 128:(t + 1) * 128],
                            wb[:, ns * 512:(ns + 1) * 512],
                            start=(s == 0), stop=(s == 31)),
                           reads=(bh2, bwb), writes=(b_PP[gi * G + t],),
                           inc=(ns == 1))
            yield
        def ln4_chain(ti):
            tile = t0g + ti
            pp = PP[ti]
            bpp = b_PP[ti]
            if ti % 2 == 0:
                bnT, mvT, stAT, stBT = bn12, mvg, stA, stB
                scT, sc4T = sc3, sc4
                bbnT, bmvT, bstT = b_bn, b_mv, b_st
                bscT, bsc4T = b_sc3, b_sc4
            else:
                bnT, mvT, stAT, stBT = bnF[0], mvF[0], stAF[0], stBF[0]
                scT, sc4T = scB2, wacc[0]
                bbnT, bmvT, bstT = b_bnF[0], b_mvF[0], b_stF[0]
                bscT, bsc4T = b_scB1, b_wacc
            ew("v", lambda v, pp=pp, scT=scT: v.tensor_add(
                scT[:, :], pp[:, :], boutb[:, :]),
               reads=(bpp, b_vecs), writes=(bscT,))
            yield
            ew("v", lambda v, scT=scT, bnT=bnT: v.bn_stats(
                bnT[:, :6], scT[:, :512]), reads=(bscT,), writes=(bbnT,))
            ew("v", lambda v, scT=scT, bnT=bnT: v.bn_stats(
                bnT[:, 6:12], scT[:, 512:]), reads=(bscT,), writes=(bbnT,))
            yield
            ew("v", lambda v, bnT=bnT, mvT=mvT: v.bn_aggr(mvT[:, :2], bnT[:, :]),
               reads=(bbnT,), writes=(bmvT,))
            yield
            ew("s", lambda s, mvT=mvT, stAT=stAT: s.activation(
                stAT[:, :1], mvT[:, 1:2], AF.Ln, bias=cst(C_EPS), scale=1.0),
               reads=(bmvT, b_consts), writes=(bstT,))
            ew("s", lambda s, stAT=stAT: s.activation(
                stAT[:, :1], stAT[:, :1], AF.Exp, bias=0.0, scale=-0.5),
               reads=(), writes=(bstT,))
            ew("v", lambda v, mvT=mvT, stBT=stBT: v.tensor_scalar_mul(
                stBT[:, :1], mvT[:, 0:1], -1.0),
               reads=(bmvT,), writes=(bstT,))
            yield
            ew("v", lambda v, scT=scT, sc4T=sc4T, stAT=stAT, stBT=stBT:
               v.tensor_scalar(sc4T[:, :], scT[:, :], stBT[:, :1], stAT[:, :1],
                               ALU.add, ALU.mult),
               reads=(bscT, bstT), writes=(bsc4T,))
            yield
            ob = outb[ti % 2]
            bob = b_outb[ti % 2]
            ew("v", lambda v, ob=ob, ti=ti, sc4T=sc4T: v.tensor_add(
                ob[:, :], sc4T[:, :], xpl[ti][:, :]),
               reads=(bsc4T, b_xpl[ti]), writes=(bob,))
            P.dma("y", STY[ti % 2], out=y_h[tile * 128:(tile + 1) * 128, :],
                  in_=ob[:, :], reads=(bob,), writes=(Buf(),))
            yield

        for base in range(0, SG2 * G, 2):
            chains = [ln4_chain(base)]
            if base + 1 < SG2 * G:
                chains.append(ln4_chain(base + 1))
            while chains:
                for c in list(chains):
                    try:
                        next(c)
                    except StopIteration:
                        chains.remove(c)
                yield

    def drive(*gens, bg=None):
        live = [gg for gg in gens if gg is not None]
        while live:
            for gg in list(live):
                try:
                    next(gg)
                except StopIteration:
                    live.remove(gg)
            if bg is not None:
                try:
                    next(bg)
                except StopIteration:
                    bg = None

    def prep_stream():
        for q in range(4):
            for _ in emit_win_prep(q):
                yield
        for c in range(32):
            for _ in emit_wout_prep(c):
                yield

    def wout_stream():
        for c in range(32):
            for _ in emit_wout_prep(c):
                yield

    def win_stream():
        for q in range(4):
            for _ in emit_win_prep(q):
                yield

    def _chain(*gens):
        for gg in gens:
            for _ in gg:
                yield

    wouts = wout_stream()
    preps = _chain(wouts, win_stream())
    for g in range(NG):
        drive(front(g), scanback(g - 1) if g >= 1 else None, bg=wouts)
    drive(scanback(NG - 1), bg=preps)
    drive(preps)

    emit_allreduce(b_AMX, AMX, ccxi_d, ccxo_d, b_sx, SXC, RSXC)
    ew("v", lambda v: v.tensor_single_scalar(ALF[:, :], AMF[:, :], 1.0 / 127.0,
                                             ALU.mult),
       reads=(b_AMF,), writes=(b_ALF,))
    ew("v", lambda v: v.tensor_scalar_mul(ALF[:, :], ALF[:, :], RSXC[:, :1]),
       reads=(b_sx,), writes=(b_ALF,))

    NU1 = NG // SG1
    drive(f1_pro(0))
    for u in range(NU1):
        drive(f1_mms(u), f1_pro(u + 1) if u + 1 < NU1 else None)

    emit_allreduce(b_AMH, AMH, cchi_d, ccho_d, b_sh, SHC, RSHC)
    ew("v", lambda v: v.tensor_single_scalar(BET[:, :], SOC[:, :], 1.0 / 127.0,
                                             ALU.mult),
       reads=(b_SOC,), writes=(b_BET,))
    ew("v", lambda v: v.tensor_scalar_mul(BET[:, :], BET[:, :], RSHC[:, :1]),
       reads=(b_sh,), writes=(b_BET,))
    def w2_prep():
        for c in range(32):
            sl = c % 4
            src_ = S_bfc[:, sl * 1024:(sl + 1) * 1024]
            P.dma("y", XL3L[sl], out=src_, in_=woutq_d[c],
                  reads=(b_woutq[c],),
                  writes=(b_w2l[sl], b_Sms[2 * sl], b_Sms[2 * sl + 1]))
            ew("v", lambda v, c=c, src_=src_: v.tensor_scalar_mul(
                wqt[:, :], src_, BET[:, c:c + 1]),
               reads=(b_w2l[sl], b_BET), writes=(b_wqt,))
            stq = W2SL[sl]
            if c >= 4:
                P.raw("y", lambda o, stq=stq, v=16 * (c // 4): o.wait_ge(
                    P.dma_sems[stq][0], v))
            P.dma("y", stq, out=w2_d[c], in_=wqt[:, :], reads=(b_wqt,),
                  writes=(b_w2s[c],))
            yield

    w2p = w2_prep()
    for _ in range(6):
        next(w2p, None)
    for u in range(NG // SG2):
        drive(f2_super(u), bg=w2p)
    drive(w2p)

    if dbg:
        ew("v", lambda v: v.tensor_copy(sc4[:, 0:1], SXC[:, :]), reads=(b_sx,), writes=(b_sc4,))
        ew("v", lambda v: v.tensor_copy(sc4[:, 1:2], SHC[:, :]), reads=(b_sh,), writes=(b_sc4,))
        ew("v", lambda v: v.tensor_copy(sc4[:, 2:3], AMX[:, :]), reads=(b_AMX,), writes=(b_sc4,))
        ew("v", lambda v: v.tensor_copy(sc4[:, 3:4], AMH[:, :]), reads=(b_AMH,), writes=(b_sc4,))
        ew("v", lambda v: v.tensor_copy(sc4[:, 4:36], ALF[:, :]), reads=(b_ALF,), writes=(b_sc4,))
        ew("v", lambda v: v.tensor_copy(sc4[:, 36:68], BET[:, :]), reads=(b_BET,), writes=(b_sc4,))
        ew("v", lambda v: v.tensor_copy(sc4[:, 68:100], AMF[:, :]), reads=(b_AMF,), writes=(b_sc4,))
        ew("v", lambda v: v.tensor_copy(sc4[:, 100:132], SOC[:, :]), reads=(b_SOC,), writes=(b_sc4,))
        P.dma("y", DBGS, out=dsc_d[:, :136], in_=sc4[:, :136],
              reads=(b_sc4,), writes=(Buf(),))
    for _sty in STY:
        P.raw("y", lambda o, _sty=_sty: o.wait_ge(
            P.dma_sems[_sty][0], P.dma_sems[_sty][1]))

    with nc.Block() as block:
        P.build(block)
    return nc


def _cons_table(bgk2, b_in, rms_w, b_out):
    cons = np.zeros((128, 54), dtype=np.float32)
    cons[:, 0] = 1e-5
    cons[:, 1] = LN16
    cons[:, 2:10] = np.asarray(bgk2, np.float32).reshape(8, 128).T
    cons[:, 10:42] = np.asarray(b_in, np.float32).reshape(32, 128).T
    cons[:, 42:46] = np.asarray(rms_w, np.float32).reshape(4, 128).T
    cons[:, 46:54] = np.asarray(b_out, np.float32).reshape(8, 128).T
    return cons


def make_in_maps(inputs_by_name, B):
    w = {k: np.ascontiguousarray(v, np.float32)
         for k, v in inputs_by_name.items() if k != "x"}
    w["cons"] = _cons_table(w["bgk2"], w["b_in"], w["rms_w"], w["b_out"])
    x = inputs_by_name["x"]
    return [dict(w, x=np.ascontiguousarray(x[b])) for b in range(B)]


def kernel(L0_attn_hidden_states, L0_attn_Wq, L0_attn_Wk, L0_attn_Wv,
           L0_attn_Wg, L0_attn_Wgk1, L0_attn_Wgk2, L0_attn_bgk2,
           L0_attn_rms_w, L0_attn_Wo, L0_ffn_W_in, L0_ffn_b_in,
           L0_ffn_W_out, L0_ffn_b_out):
    x = np.ascontiguousarray(np.asarray(L0_attn_hidden_states, dtype=np.float32))
    B, T, _ = x.shape
    nc = build(n_cores=B, T=T)
    in_maps = make_in_maps(dict(
        x=x, Wq=L0_attn_Wq, Wk=L0_attn_Wk, Wv=L0_attn_Wv, Wg=L0_attn_Wg,
        Wgk1=L0_attn_Wgk1, Wgk2=L0_attn_Wgk2, bgk2=L0_attn_bgk2,
        rms_w=L0_attn_rms_w, Wo=L0_attn_Wo, W_in=L0_ffn_W_in,
        b_in=L0_ffn_b_in, W_out=L0_ffn_W_out, b_out=L0_ffn_b_out), B)
    res = run_bass_kernel_spmd(nc, in_maps, core_ids=list(range(B)))
    out = np.stack([np.asarray(res.results[b]["y"]) for b in range(B)], axis=0)
    return out.astype(np.float32)


# version 24
# speedup vs baseline: 1.1044x; 1.1044x over previous
import os
import numpy as np
import concourse.bass as bass
import concourse.mybir as mybir
from concourse.bass_utils import run_bass_kernel_spmd

F32 = mybir.dt.float32
BF16 = mybir.dt.bfloat16
I32 = mybir.dt.int32
AF = mybir.ActivationFunctionType
ALU = mybir.AluOpType
AX = mybir.AxisListType

D = 1024
H = 4
DV = 512
VD = 2048
FF = 4096
LN16 = -2.772588722239781



class Buf:
    __slots__ = ("name", "w", "readers")

    def __init__(self, name=""):
        self.name = name
        self.w = None
        self.readers = []


class Prog:
    ENGS = ("v", "s", "t", "g", "y")

    def __init__(self, nc):
        self.nc = nc
        self.plan = {e: [] for e in self.ENGS}
        self.cnt = {e: 0 for e in self.ENGS}
        self.done = {}
        self.dma_sems = {}
        self.waited = {e: {} for e in self.ENGS}
        self._dma_names = []

    def dma_stream(self, name):
        self._dma_names.append(name)
        self.dma_sems[name] = [None, 0]
        return name

    def _wait(self, eng, tok):
        if tok is None:
            return
        kind, key, val = tok
        if kind == "o":
            assert key == eng, f"cross-engine dep on unfenced group ({key}->{eng})"
            return
        if kind == "df":
            if self.waited[eng].get(("df", key)):
                return
            self.waited[eng][("df", key)] = True
            self.waited[eng][("d", key)] = 10 ** 9
            ds = self.dma_sems

            def thunk(obj, key=key):
                obj.wait_ge(ds[key][0], ds[key][1])
            self.plan[eng].append(thunk)
            return
        cur = self.waited[eng].get((kind, key), 0)
        if val <= cur:
            return
        self.waited[eng][(kind, key)] = val
        if kind == "e":
            done = self.done

            def thunk(obj, key=key, val=val):
                obj.wait_ge(done[key], val)
        else:
            ds = self.dma_sems

            def thunk(obj, key=key, val=val):
                obj.wait_ge(ds[key][0], val)
        self.plan[eng].append(thunk)

    def _deps(self, reads, writes):
        toks = []
        for b in reads:
            if b.w is not None:
                toks.append(b.w)
        for b in writes:
            if b.w is not None:
                toks.append(b.w)
            toks.extend(b.readers)
        return toks

    def emit(self, eng, fn, reads=(), writes=(), inc=True):
        for tok in self._deps(reads, writes):
            self._wait(eng, tok)
        if inc:
            self.cnt[eng] += 1
            val = self.cnt[eng]
            tok = ("e", eng, val)
            done = self.done

            def thunk(obj, fn=fn, eng=eng):
                fn(obj).then_inc(done[eng], 1)
        else:
            tok = None

            def thunk(obj, fn=fn):
                fn(obj)
        self.plan[eng].append(thunk)
        for b in reads:
            if tok is not None:
                b.readers.append(tok)
        for b in writes:
            b.w = tok if tok is not None else ("o", eng, None)
            b.readers = []
        return tok

    def dma(self, eng, stream, out, in_, reads=(), writes=(), slow=False,
            dram_final=()):
        toks = self._deps(reads, writes)
        for b in dram_final:
            toks.extend(b.readers)
        for tok in toks:
            self._wait(eng, tok)
        ds = self.dma_sems[stream]
        ds[1] += 16
        val = ds[1]
        dsd = self.dma_sems

        def thunk(obj, out=out, in_=in_, stream=stream, slow=slow):
            kw = dict(allow_slow_non_contiguous=True) if slow else {}
            obj.dma_start(out=out, in_=in_, **kw).then_inc(dsd[stream][0], 16)
        self.plan[eng].append(thunk)
        tok = ("d", stream, val)
        for b in reads:
            b.readers.append(tok)
        for b in writes:
            b.w = tok
            b.readers = []
        for b in dram_final:
            b.w = ("df", stream, None)
            b.readers = []
        return tok

    def raw(self, eng, fn):
        self.plan[eng].append(fn)

    def build(self, block):
        import contextlib
        with contextlib.ExitStack() as st:
            for e in self.ENGS:
                self.done[e] = st.enter_context(self.nc.semaphore("done_" + e))
            for n in self._dma_names:
                self.dma_sems[n][0] = st.enter_context(self.nc.semaphore("dma_" + n))

            @block.sync
            def _(o):
                for f in self.plan["y"]:
                    f(o)

            @block.vector
            def _(o):
                for f in self.plan["v"]:
                    f(o)

            @block.scalar
            def _(o):
                for f in self.plan["s"]:
                    f(o)

            @block.tensor
            def _(o):
                for f in self.plan["t"]:
                    f(o)

            @block.gpsimd
            def _(o):
                for f in self.plan["g"]:
                    f(o)


def build(n_cores=8, T=4096, dbg=False):
    NT = T // 128
    G = 2
    W = G * 128
    NG = NT // G

    nc = bass.Bass("TRN2", target_bir_lowering=False, detect_race_conditions=False)

    x_h = nc.dram_tensor("x", [T, D], F32, kind="ExternalInput")
    wq_h = nc.dram_tensor("Wq", [D, D], F32, kind="ExternalInput")
    wk_h = nc.dram_tensor("Wk", [D, D], F32, kind="ExternalInput")
    wv_h = nc.dram_tensor("Wv", [D, VD], F32, kind="ExternalInput")
    wg_h = nc.dram_tensor("Wg", [D, VD], F32, kind="ExternalInput")
    wgk1_h = nc.dram_tensor("Wgk1", [D, 16], F32, kind="ExternalInput")
    wgk2_h = nc.dram_tensor("Wgk2", [16, D], F32, kind="ExternalInput")
    bgk2_h = nc.dram_tensor("bgk2", [D], F32, kind="ExternalInput")
    rmsw_h = nc.dram_tensor("rms_w", [DV], F32, kind="ExternalInput")
    wo_h = nc.dram_tensor("Wo", [VD, D], F32, kind="ExternalInput")
    win_h = nc.dram_tensor("W_in", [D, FF], F32, kind="ExternalInput")
    bin_h = nc.dram_tensor("b_in", [FF], F32, kind="ExternalInput")
    wout_h = nc.dram_tensor("W_out", [FF, D], F32, kind="ExternalInput")
    bout_h = nc.dram_tensor("b_out", [D], F32, kind="ExternalInput")
    cons_h = nc.dram_tensor("cons", [128, 54], F32, kind="ExternalInput")
    y_h = nc.dram_tensor("y", [T, D], F32, kind="ExternalOutput")

    _sk = dict(kind="ExternalOutput") if dbg else {}
    xp_d = nc.dram_tensor("xp_d", [T, D], F32, **_sk)
    xl3_d = nc.dram_tensor("xl3_d", [T, D], BF16, **_sk)
    hid_d = nc.dram_tensor("hid_d", [NG, 32, 128, W], BF16, **_sk)
    wgbf_d = nc.dram_tensor("wgbf_d", [8, 128, VD], BF16)
    wvbf_d = nc.dram_tensor("wvbf_d", [8, 128, VD], BF16)
    wqc_d = nc.dram_tensor("wqc_d", [8, 128, 1024], BF16)
    wkc_d = nc.dram_tensor("wkc_d", [8, 128, 1024], BF16)
    wobf_d = nc.dram_tensor("wobf_d", [16, 128, D], BF16)
    wqin_d = nc.dram_tensor("wqin_d", [32, 128, 1024], BF16)
    woutq_d = nc.dram_tensor("woutq_d", [32, 128, D], BF16)
    w2_d = nc.dram_tensor("w2_d", [32, 128, D], BF16)
    ccxi_d = nc.dram_tensor("ccxi_d", [1, 8], F32)
    ccxo_d = nc.dram_tensor("ccxo_d", [1, 8], F32)
    cchi_d = nc.dram_tensor("cchi_d", [1, 8], F32)
    ccho_d = nc.dram_tensor("ccho_d", [1, 8], F32)
    if dbg:
        dsc_d = nc.dram_tensor("dsc_d", [128, 136], F32, kind="ExternalOutput")
        dX_d = nc.dram_tensor("dX_d", [128, 2048], BF16)
        dQG_d = nc.dram_tensor("dQG_d", [128, 2048], BF16)
        dKG_d = nc.dram_tensor("dKG_d", [128, 2048], BF16)
        dV_d = nc.dram_tensor("dV_d", [128, 2048], BF16)
        dSG_d = nc.dram_tensor("dSG_d", [128, 2048], BF16)
        dKT_d = nc.dram_tensor("dKT_d", [128, 1024], BF16)
        dA_d = nc.dram_tensor("dA_d", [128, 512], BF16)
        dOF_d = nc.dram_tensor("dOF_d", [128, 2048], BF16)
        dS_d = nc.dram_tensor("dS_d", [128, 4096], BF16)
        dXS_d = nc.dram_tensor("dXS_d", [NG, 128, 2048], BF16, kind="ExternalOutput")

    A = nc.alloc_sbuf_tensor
    consts = A("consts", [128, 54], F32)
    identb = A("identb", [128, 128], BF16)
    identf = A("identf", [128, 128], F32)
    mask4 = A("mask4", [128, 512], F32)
    onesf = A("onesf", [128, 128], F32)
    onesc = A("onesc", [1, 128], F32)
    bgk2n = A("bgk2n", [128, 8], F32)
    binb = A("binb", [128, 32], F32)
    rms4 = A("rms4", [128, 4], F32)
    boutb = A("boutb", [128, 1024], F32)
    bout_row = A("bout_row", [1, 1024], F32)
    S_bfc = A("S_bfc", [128, 8 * DV], BF16)
    bn12 = A("bn12", [128, 12], F32)
    mvg = A("mvg", [128, 2 * G], F32)
    stA = A("stA", [128, 2], F32)
    stB = A("stB", [128, 2], F32)
    mo8 = A("mo8", [128, 8], F32)
    rsO = A("rsO", [128, 4], F32)
    GL63 = [A(f"GL63_{i}", [128, 8 * G], F32) for i in range(2)]
    GL127 = [A(f"GL127_{i}", [128, 8 * G], F32) for i in range(2)]
    EGLN = [A(f"EGLN_{i}", [128, 8 * G], F32) for i in range(2)]
    EGLS = [A(f"EGLS_{i}", [128, 8 * G], F32) for i in range(2)]
    AMX = A("AMX", [128, 1], F32)
    AMH = A("AMH", [128, 1], F32)
    r1a = A("r1a", [1, 128], F32)
    r1b = A("r1b", [1, 1], F32)
    SXC = A("SXC", [128, 1], F32)
    RSXC = A("RSXC", [128, 1], F32)
    SHC = A("SHC", [128, 1], F32)
    RSHC = A("RSHC", [128, 1], F32)
    ALF = A("ALF", [128, 32], F32)
    AMF = A("AMF", [128, 32], F32)
    SOC = A("SOC", [128, 32], F32)
    RSOC = A("RSOC", [128, 32], F32)
    BET = A("BET", [128, 32], F32)
    ccb = A("ccb", [1, 8], F32)
    Wgk1_s = A("Wgk1_s", [128, 8 * 16], BF16)
    Wgk2_s = A("Wgk2_s", [16, D], BF16)
    wsb = [A(f"wsb{i}", [128, 1024], BF16) for i in range(2)]
    wqb = [A(f"wqb{i}", [128, 1024], BF16) for i in range(2)]
    wkb = [A(f"wkb{i}", [128, 1024], BF16) for i in range(2)]
    wvb = [A(f"wvb{i}", [128, 1024], BF16) for i in range(2)]
    wgb = [A(f"wgb{i}", [128, 1024], BF16) for i in range(2)]
    bnF = [A(f"bnF{i}", [128, 12], F32) for i in range(2)]
    mvF = [A(f"mvF{i}", [128, 2 * G], F32) for i in range(2)]
    stAF = [A(f"stAF{i}", [128, 2], F32) for i in range(2)]
    stBF = [A(f"stBF{i}", [128, 2], F32) for i in range(2)]
    scg = [A(f"scg{i}", [128, 512], F32) for i in range(4)]
    scq = [A(f"scq{i}", [128, 256], F32) for i in range(4)]
    sck = [A(f"sck{i}", [128, 256], F32) for i in range(4)]
    rtA = A("rtA", [128, 6], F32)
    scB2 = A("scB2", [128, 1024], F32)
    z1b = [A(f"z1b{i}", [16, W], BF16) for i in range(2)]
    sc0 = A("sc0", [128, 512], F32)
    sc1 = A("sc1", [128, 512], F32)
    sc2 = A("sc2", [128, 512], F32)
    sci = A("sci", [128, 1024], I32)
    sc3 = A("sc3", [128, 1024], F32)
    sc4 = A("sc4", [128, 1024], F32)
    wpb = [A(f"wpb{i}", [128, 1024], F32) for i in range(2)]
    wacc = [A(f"wacc{i}", [128, 1024], F32) for i in range(2)]
    wqt = A("wqt", [128, 1024], BF16)

    bump = nc._sbuf_addr_for_side('left')
    TOP = 122816
    assert bump <= TOP, f"bump {bump} exceeds overlay base {TOP}"

    AT = nc.alloc_sbuf_tensor_at
    off = [TOP]

    def at(name, shape, dtype, nbytes):
        t = AT(name, shape, dtype, offset=off[0])
        off[0] += nbytes
        assert off[0] <= 229344, f"overlay overflow at {name}"
        return t

    x_gP, X_fmP, QGP, KGP, KTMP, VTP, SGTP, xlntP = [], [], [], [], [], [], [], []
    qg_off, kg_off, xlnt_off = [], [], []
    for pya in range(2):
        x_gP.append(at(f"x_g{pya}", [128, G * 1024], F32, G * 4096))
        X_fmP.append(at(f"X_fm{pya}", [128, 8 * W], BF16, 8 * W * 2))
        qg_off.append(off[0])
        QGP.append(at(f"QG{pya}", [128, 8 * W], BF16, 8 * W * 2))
        kg_off.append(off[0])
        KGP.append(at(f"KG{pya}", [128, 8 * W], BF16, 8 * W * 2))
        KTMP.append([at(f"KTM{pya}_{i}", [128, 1024], BF16, 2048) for i in range(G)])
        VTP.append([at(f"VT{pya}_{i}", [128, VD], BF16, 4096) for i in range(G)])
        SGTP.append([at(f"SGT{pya}_{i}", [128, VD], BF16, 4096) for i in range(G)])
        xlnt_off.append(off[0])
        xlntP.append(at(f"xlnt{pya}", [128, 1024], BF16, 2048))
    OF = at("OF", [128, VD], BF16, 4096)
    OFm = at("OFm", [128, VD], BF16, 4096)
    OF2B = [AT(f"OF2B{i}", [128, VD], BF16, offset=qg_off[i]) for i in range(2)]
    OFKB = [AT(f"OFKB{i}", [128, VD], BF16, offset=kg_off[i]) for i in range(2)]
    ABUF = [AT(f"ABUF{i}", [128, 512], BF16, offset=xlnt_off[i]) for i in range(2)]
    wqt2 = AT("wqt2", [128, 1024], BF16, offset=xlnt_off[0])
    xpb = [at(f"xpb{i}", [128, 1024], F32, 4096) for i in range(2)]
    xl3b = [at(f"xl3b{i}", [128, 1024], BF16, 2048) for i in range(2)]
    attn_end = off[0]

    off = [TOP]
    xsq_base = off[0]
    XSIMQ = [at(f"XSIM{i}", [128, 8 * W], BF16, 8 * W * 2) for i in range(8)]
    wpbX = [AT(f"wpbX{i}", [128, 1024], F32, offset=xsq_base + (4 + i) * 8 * W * 2)
            for i in range(2)]
    xl3l = [at(f"xl3l{i}", [128, 1024], BF16, 2048) for i in range(4)]
    hbb = [at(f"hbb{i}", [128, W], BF16, W * 2) for i in range(4)]

    off = [TOP]
    xpl = [at(f"xpl{i}", [128, 1024], F32, 4096) for i in range(4)]
    outb = [at(f"outb{i}", [128, 1024], F32, 4096) for i in range(2)]
    h2b_off = off[0]
    h2b = [at(f"h2b{i}", [128, W], BF16, W * 2) for i in range(4)]
    hlb_off = off[0]
    hlb = [at(f"hlb{i}", [128, W], BF16, W * 2) for i in range(4)]
    h2b2 = [AT(f"h2b2_{j}", [128, 2 * W], BF16, offset=h2b_off + j * 2 * W * 2)
            for j in range(2)]
    hlb2 = [AT(f"hlb2_{j}", [128, 2 * W], BF16, offset=hlb_off + j * 2 * W * 2)
            for j in range(2)]

    bump = nc._sbuf_addr_for_side('left')
    assert bump <= TOP, f"late bump {bump} exceeds {TOP}"

    PP = [nc.alloc_psum_tensor(f"PP{i}", [128, 1024], F32) for i in range(4)]

    P = Prog(nc)
    LDW = P.dma_stream("ldw")
    LDC = P.dma_stream("ldc")
    GDM = P.dma_stream("gdm")
    WPB = [P.dma_stream(f"wpb{i}") for i in range(4)]
    LDX = [[P.dma_stream(f"ldx{p}_{t}") for t in range(G)] for p in range(2)]
    WQB = [P.dma_stream(f"wqbs{i}") for i in range(2)]
    WKB = [P.dma_stream(f"wkbs{i}") for i in range(2)]
    WVB = [P.dma_stream(f"wvbs{i}") for i in range(2)]
    WGB = [P.dma_stream(f"wgbs{i}") for i in range(2)]
    XPS = [P.dma_stream(f"xps{i}") for i in range(2)]
    X3S = [P.dma_stream(f"x3s{i}") for i in range(2)]
    HIDS = [P.dma_stream(f"hids{i}") for i in range(4)]
    WQIN_SL = [[P.dma_stream(f"wqin_s{i}_{j}") for j in range(2)]
               for i in range(4)]
    WOBF_S = P.dma_stream("wobf_s")
    WOUTQ_S = P.dma_stream("woutq_s")
    W2SL = [P.dma_stream(f"w2s{i}") for i in range(4)]
    LWS = [P.dma_stream(f"lws{i}") for i in range(2)]
    XL3L = [P.dma_stream(f"xl3l{i}") for i in range(4)]
    HLB = [P.dma_stream(f"hlb{i}") for i in range(4)]
    XPL = [P.dma_stream(f"xpl{i}") for i in range(4)]
    STY = [P.dma_stream(f"sty{i}") for i in range(2)]
    DBGS = P.dma_stream("dbgs")

    B = Buf
    b_consts, b_ident, b_mask = B("consts"), B("ident"), B("mask")
    b_onesf, b_onesc = B("onesf"), B("onesc")
    b_vecs = B("vecs")
    b_boutrow = B("boutrow")
    b_Sb = B("Sb")
    b_Sms = [B(f"Sms{i}") for i in range(8)]
    b_bn, b_mv, b_st = B("bn"), B("mv"), B("st")
    b_mo, b_rsO = B("mo"), B("rsO")
    b_GL = B("GL")
    b_AMX, b_AMH = B("AMX"), B("AMH")
    b_r1 = B("r1")
    b_sx, b_sh = B("sx"), B("sh")
    b_ALF, b_AMF = B("ALF"), B("AMF")
    b_SOC, b_BET = B("SOC"), B("BET")
    b_ccb, b_cc = B("ccb"), B("cc")
    b_Wgk = B("Wgk")
    b_wsb = [B() for _ in range(2)]
    b_wqb = [B() for _ in range(2)]
    b_wkb = [B() for _ in range(2)]
    b_wvb = [B() for _ in range(2)]
    b_wgb = [B() for _ in range(2)]
    b_wqc = [B() for _ in range(8)]
    b_wkc = [B() for _ in range(8)]
    b_wvbf = [B() for _ in range(8)]
    b_bnF = [B() for _ in range(2)]
    b_mvF = [B() for _ in range(2)]
    b_stF = [B() for _ in range(2)]
    b_scg = [B(f"scg{i}") for i in range(4)]
    b_scq = [B() for _ in range(4)]
    b_sck = [B() for _ in range(4)]
    b_rtA = B("rtA")
    b_scB1 = B("scB2")
    b_z1 = B("z1")
    b_sc0, b_sc1, b_sc2, b_sci = B("sc0"), B("sc1"), B("sc2"), B("sci")
    b_sc3, b_sc4 = B("sc3"), B("sc4")
    b_wpb = [B() for _ in range(2)]
    b_wacc, b_wqt = B("wacc"), B("wqt")
    b_xg = [[B(f"x_g{p}{t}") for t in range(G)] for p in range(2)]
    b_Xfm = [B() for _ in range(2)]
    b_QG = [B() for _ in range(2)]
    b_KG = [B() for _ in range(2)]
    b_KTM = [[B() for _ in range(G)] for _ in range(2)]
    b_VT = [[B() for _ in range(G)] for _ in range(2)]
    b_SGT = [[B() for _ in range(G)] for _ in range(2)]
    b_xlnt = [B(), B()]
    b_GLP = [B(), B()]
    b_z1P = [B(), B()]
    b_OF, b_OFm = B("OF"), B("OFm")
    b_xpb = [B() for _ in range(2)]
    b_xl3b = [B() for _ in range(2)]
    b_XSIM = [B() for _ in range(8)]
    b_PP0H = B("PP0H")
    b_xl3l = [B() for _ in range(4)]
    b_hbb = [B() for _ in range(4)]
    b_h2b = [B() for _ in range(4)]
    b_hlb = [B() for _ in range(4)]
    b_xpl = [B() for _ in range(4)]
    b_outb = [B() for _ in range(2)]
    b_PP = [B(f"PP{i}") for i in range(4)]
    b_xp_t = [B() for _ in range(NT)]
    b_xl3_t = [B() for _ in range(NT)]
    b_hid = [[B() for _ in range(32)] for _ in range(NG)]
    b_wgbf = [B() for _ in range(8)]
    b_wobf = [B() for _ in range(16)]
    b_wqin = [B() for _ in range(32)]
    b_wqinB = [B() for _ in range(32)]
    b_woutq = [B() for _ in range(32)]
    b_w2s = [B() for _ in range(32)]
    b_w2l = [B() for _ in range(4)]

    C_EPS, C_LN16 = 0, 1

    def cst(i):
        return consts[:, i:i + 1]

    ew = P.emit

    P.dma("y", LDC, out=consts[:, :], in_=cons_h[:, :], dram_final=(b_consts,))
    ew("v", lambda v: v.tensor_copy(bgk2n[:, :], consts[:, 2:10]),
       reads=(b_consts,), writes=(b_vecs,))
    ew("v", lambda v: v.tensor_copy(binb[:, :], consts[:, 10:42]),
       reads=(b_consts,), writes=(b_vecs,))
    ew("v", lambda v: v.tensor_copy(rms4[:, :], consts[:, 42:46]),
       reads=(b_consts,), writes=(b_vecs,))
    ew("g", lambda g: g.memset(onesf[:, :], 1.0), writes=(b_onesf,))
    ew("g", lambda g: g.memset(onesc[:1, :], 1.0), writes=(b_onesc,))
    ew("g", lambda g: g.affine_select(identb[:, :], onesf[:, :], [[1, 128]],
                                      ALU.is_equal, 0.0, base=0,
                                      channel_multiplier=-1),
       reads=(b_onesf,), writes=(b_ident,))
    ew("g", lambda g: g.affine_select(identf[:, :], onesf[:, :], [[1, 128]],
                                      ALU.is_equal, 0.0, base=0,
                                      channel_multiplier=-1),
       reads=(b_onesf,), writes=(b_ident,))
    for hh in range(4):
        ew("g", lambda g, hh=hh: g.affine_select(
            mask4[:, hh * 128:(hh + 1) * 128], onesf[:, :], [[1, 128]],
            ALU.is_ge, 0.0, base=0, channel_multiplier=-1),
           reads=(b_onesf,), writes=(b_mask,))
    ew("v", lambda v: v.tensor_scalar_mul(bgk2n[:, :], bgk2n[:, :], -1.0),
       reads=(b_vecs,), writes=(b_vecs,))
    for k8 in range(8):
        ew("t", lambda t, k8=k8: t.matmul(
            PP[2][:1, k8 * 128:(k8 + 1) * 128], consts[:, 46 + k8:47 + k8],
            identf[:, :], start=True, stop=True),
           reads=(b_consts, b_ident), writes=(b_PP[2],), inc=(k8 == 7))
    ew("v", lambda v: v.tensor_copy(bout_row[:1, :], PP[2][:1, :]),
       reads=(b_PP[2],), writes=(b_boutrow,))
    for ns in range(2):
        ew("t", lambda t, ns=ns: t.matmul(
            PP[3][:, ns * 512:(ns + 1) * 512], onesc[:1, :],
            bout_row[:1, ns * 512:(ns + 1) * 512], start=True, stop=True),
           reads=(b_onesc, b_boutrow), writes=(b_PP[3],), inc=(ns == 1))
    ew("v", lambda v: v.tensor_copy(boutb[:, :], PP[3][:, :]),
       reads=(b_PP[3],), writes=(b_vecs,))
    ew("v", lambda v: v.memset(S_bfc[:, :], 0.0),
       writes=tuple(b_Sms) + (b_Sb,))
    ew("v", lambda v: v.memset(AMX[:, :], 0.0), writes=(b_AMX,))
    ew("v", lambda v: v.memset(AMH[:, :], 0.0), writes=(b_AMH,))

    for kc in range(8):
        P.dma("g", LDW, out=Wgk1_s[:, kc * 16:(kc + 1) * 16],
              in_=wgk1_h[kc * 128:(kc + 1) * 128, :], dram_final=(b_Wgk,))
        P.dma("g", LDW, out=wgbf_d[kc], in_=wg_h[kc * 128:(kc + 1) * 128, :],
              dram_final=(b_wgbf[kc],))
        P.dma("g", LDW, out=wvbf_d[kc], in_=wv_h[kc * 128:(kc + 1) * 128, :],
              dram_final=(b_wvbf[kc],))
        P.dma("g", LDW,
              out=wqc_d[:, :, kc * 128:(kc + 1) * 128],
              in_=wq_h[kc * 128:(kc + 1) * 128, :].rearrange(
                  "p (m c) -> m p c", m=8, c=128),
              dram_final=(b_wqc[kc],))
        P.dma("g", LDW,
              out=wkc_d[:, :, kc * 128:(kc + 1) * 128],
              in_=wk_h[kc * 128:(kc + 1) * 128, :].rearrange(
                  "p (m c) -> m p c", m=8, c=128),
              dram_final=(b_wkc[kc],))
    P.dma("g", LDW, out=Wgk2_s[:16, :], in_=wgk2_h[:, :], dram_final=(b_Wgk,))
    for vc in range(16):
        pb = b_wpb[vc % 2]
        P.dma("y", WPB[vc % 2], out=wpb[vc % 2][:, :],
              in_=wo_h[vc * 128:(vc + 1) * 128, :], writes=(pb,))
        ew("v", lambda v, vc=vc: v.tensor_scalar_mul(
            wqt[:, :], wpb[vc % 2][:, :], rms4[:, vc % 4:vc % 4 + 1]),
           reads=(pb, b_vecs), writes=(b_wqt,))
        P.dma("y", WOBF_S, out=wobf_d[vc], in_=wqt[:, :], reads=(b_wqt,),
              dram_final=(b_wobf[vc],))

    SIMF = os.environ.get("BASS_SIM_FP64") == "1"

    def emit_round(src_ap, src_bufs, scale_ap, scale_buf, out_ap, out_buf, w):
        outs = out_buf if isinstance(out_buf, tuple) else (out_buf,)
        ew("v", lambda v: v.tensor_scalar_mul(sci[:, :w], src_ap, scale_ap),
           reads=tuple(src_bufs) + (scale_buf,), writes=(b_sci,))
        if not SIMF:
            ew("v", lambda v: v.tensor_copy(out_ap, sci[:, :w]),
               reads=(b_sci,), writes=outs)
            return
        ew("v", lambda v: v.tensor_scalar_mul(sc0[:, :w], src_ap, scale_ap),
           reads=tuple(src_bufs) + (scale_buf,), writes=(b_sc0,))
        ew("v", lambda v: v.tensor_sub(sc1[:, :w], sc0[:, :w], sci[:, :w]),
           reads=(b_sc0, b_sci), writes=(b_sc1,))
        ew("v", lambda v: v.tensor_single_scalar(sc2[:, :w], sc1[:, :w], 0.5,
                                                 ALU.is_gt),
           reads=(b_sc1,), writes=(b_sc2,))
        ew("v", lambda v: v.tensor_single_scalar(sc1[:, :w], sc1[:, :w], -0.5,
                                                 ALU.is_lt),
           reads=(b_sc1,), writes=(b_sc1,))
        ew("v", lambda v: v.scalar_tensor_tensor(sc2[:, :w], sc1[:, :w], -1.0,
                                                 sc2[:, :w], ALU.mult, ALU.add),
           reads=(b_sc1, b_sc2), writes=(b_sc2,))
        ew("v", lambda v: v.tensor_add(out_ap, sci[:, :w], sc2[:, :w]),
           reads=(b_sci, b_sc2), writes=outs)

    def emit_round_w(out_buf, dst=None, src=None, bsrc=None):
        if dst is None:
            dst = wqt
        if src is None:
            src, bsrc = sc3, b_sc3
        ew("v", lambda v, src=src: v.tensor_copy(sci[:, :1024], src[:, :]),
           reads=(bsrc,), writes=(b_sci,))
        if not SIMF:
            ew("v", lambda v, dst=dst: v.tensor_copy(dst[:, :], sci[:, :1024]),
               reads=(b_sci,), writes=(out_buf,))
            return
        ew("v", lambda v, src=src: v.tensor_sub(scB2[:, :], src[:, :],
                                                sci[:, :1024]),
           reads=(bsrc, b_sci), writes=(b_scB1,))
        ew("v", lambda v, src=src: v.tensor_single_scalar(
            src[:, :], scB2[:, :], 0.5, ALU.is_gt),
           reads=(b_scB1,), writes=(bsrc,))
        ew("v", lambda v: v.tensor_single_scalar(scB2[:, :], scB2[:, :], -0.5,
                                                 ALU.is_lt),
           reads=(b_scB1,), writes=(b_scB1,))
        ew("v", lambda v, src=src: v.scalar_tensor_tensor(
            src[:, :], scB2[:, :], -1.0, src[:, :], ALU.mult, ALU.add),
           reads=(bsrc, b_scB1), writes=(bsrc,))
        ew("v", lambda v, dst=dst, src=src: v.tensor_add(
            dst[:, :], sci[:, :1024], src[:, :]),
           reads=(b_sci, bsrc), writes=(out_buf,))

    def emit_win_prep(q):
        wp4 = [wpb[0], wpb[1], wpbX[0], wpbX[1]]
        bw4 = [(b_wpb[0],), (b_wpb[1],), (b_XSIM[4], b_KG[0]),
               (b_XSIM[5], b_KTM[0][0], b_KTM[0][1])]
        cl = slice(q * 1024, (q + 1) * 1024)
        for kc in range(8):
            sl = kc % 4
            pb = bw4[sl][0]
            P.dma("y", WPB[sl], out=wp4[sl][:, :],
                  in_=win_h[kc * 128:(kc + 1) * 128, cl], writes=bw4[sl])
            if kc == 0:
                ew("v", lambda v: v.tensor_copy(wacc[0][:, :], wpb[0][:, :]),
                   reads=(pb,), writes=(b_wacc,))
                ew("v", lambda v: v.tensor_copy(wacc[1][:, :], wpb[0][:, :]),
                   reads=(pb,), writes=(b_wacc,))
            else:
                ew("v", lambda v, sl=sl: v.tensor_max(
                    wacc[0][:, :], wacc[0][:, :], wp4[sl][:, :]),
                   reads=(pb, b_wacc), writes=(b_wacc,))
                ew("v", lambda v, sl=sl: v.tensor_tensor(
                    wacc[1][:, :], wacc[1][:, :], wp4[sl][:, :], ALU.min),
                   reads=(pb, b_wacc), writes=(b_wacc,))
            yield
        ew("v", lambda v: v.scalar_tensor_tensor(wacc[0][:, :], wacc[1][:, :],
                                                 -1.0, wacc[0][:, :],
                                                 ALU.mult, ALU.max),
           reads=(b_wacc,), writes=(b_wacc,))
        for sb in range(8):
            s = q * 8 + sb
            ew("t", lambda t, sb=sb: t.matmul(
                PP[1][:, :128], wacc[0][:, sb * 128:(sb + 1) * 128],
                identf[:, :], start=True, stop=True),
               reads=(b_wacc, b_ident), writes=(b_PP[1],))
            ew("v", lambda v, s=s: v.reduce_max(AMF[:, s:s + 1], PP[1][:, :128],
                                                AX.X),
               reads=(b_PP[1],), writes=(b_AMF,))
            yield
        ew("v", lambda v, q=q: v.tensor_single_scalar(
            AMF[:, q * 8:(q + 1) * 8], AMF[:, q * 8:(q + 1) * 8], 1e-8, ALU.max),
           reads=(b_AMF,), writes=(b_AMF,))
        ew("v", lambda v, q=q: v.reciprocal(wacc[1][:, :8],
                                            AMF[:, q * 8:(q + 1) * 8]),
           reads=(b_AMF,), writes=(b_wacc,))
        ew("v", lambda v: v.tensor_single_scalar(wacc[1][:, :8], wacc[1][:, :8],
                                                 127.0, ALU.mult),
           reads=(b_wacc,), writes=(b_wacc,))
        for sb in range(8):
            ew("t", lambda t, sb=sb: t.matmul(
                PP[1][:1, sb * 128:(sb + 1) * 128], wacc[1][:, sb:sb + 1],
                identf[:, :], start=True, stop=True),
               reads=(b_wacc, b_ident), writes=(b_PP[1],), inc=(sb == 7))
        ew("v", lambda v: v.tensor_copy(wacc[0][:1, :1024], PP[1][:1, :]),
           reads=(b_PP[1],), writes=(b_wacc,))
        for nb in range(2):
            ew("t", lambda t, nb=nb: t.matmul(
                PP[1][:, nb * 512:(nb + 1) * 512], onesc[:1, :],
                wacc[0][:1, nb * 512:(nb + 1) * 512], start=True, stop=True),
               reads=(b_onesc, b_wacc), writes=(b_PP[1],), inc=(nb == 1))
        ew("v", lambda v: v.tensor_copy(wacc[1][:, :], PP[1][:, :]),
           reads=(b_PP[1],), writes=(b_wacc,))
        for kc in range(8):
            sl = kc % 4
            pb = bw4[sl][0]
            P.dma("y", WPB[sl], out=wp4[sl][:, :],
                  in_=win_h[kc * 128:(kc + 1) * 128, cl], writes=bw4[sl])
            ew("v", lambda v, sl=sl: v.tensor_mul(wacc[0][:, :], wp4[sl][:, :],
                                                  wacc[1][:, :]),
               reads=(pb, b_wacc), writes=(b_wacc,))
            wq = wqt if kc % 2 == 0 else wqt2
            bwq = b_wqt if kc % 2 == 0 else b_xlnt[0]
            emit_round_w(bwq, dst=wq, src=wacc[0], bsrc=b_wacc)
            j = kc % 2
            bl = b_wqin if j == 0 else b_wqinB
            P.dma("y", WQIN_SL[q][j],
                  out=wqin_d[q * 8:(q + 1) * 8, :, kc * 128:(kc + 1) * 128]
                  .rearrange("s p c -> p s c"),
                  in_=wq.rearrange("q (sb c) -> q sb c", sb=8, c=128),
                  reads=(bwq,),
                  writes=tuple(bl[q * 8 + sb] for sb in range(8)))
            yield

    def emit_wout_prep(c):
        YIELD = True
        pb = b_wpb[c % 2]
        P.dma("y", WPB[c % 2], out=wpb[c % 2][:, :],
              in_=wout_h[c * 128:(c + 1) * 128, :], writes=(pb,))
        ew("v", lambda v, c=c: v.reduce_max(SOC[:, c:c + 1], wpb[c % 2][:, :],
                                            AX.X, apply_absolute_value=True),
           reads=(pb,), writes=(b_SOC,))
        ew("v", lambda v, c=c: v.tensor_single_scalar(
            SOC[:, c:c + 1], SOC[:, c:c + 1], 1e-8, ALU.max),
           reads=(b_SOC,), writes=(b_SOC,))
        ew("v", lambda v, c=c: v.reciprocal(RSOC[:, c:c + 1], SOC[:, c:c + 1]),
           reads=(b_SOC,), writes=(b_SOC,))
        ew("v", lambda v, c=c: v.tensor_scalar(wacc[1][:, :], wpb[c % 2][:, :],
                                               RSOC[:, c:c + 1], 127.0,
                                               ALU.mult, ALU.mult),
           reads=(pb, b_SOC), writes=(b_wacc,))
        emit_round_w(b_wqt, src=wacc[1], bsrc=b_wacc)
        P.dma("y", WOUTQ_S, out=woutq_d[c], in_=wqt[:, :], reads=(b_wqt,),
              dram_final=(b_woutq[c],))
        yield

    def bank(i):
        return PP[2 + i // 2][:, (i % 2) * 512:(i % 2 + 1) * 512]

    def bbank(i):
        return b_PP[2 + i // 2]

    def front(g):
        p = g % 2
        t0g = g * G
        x_g, X_fm, QG, KG = x_gP[p], X_fmP[p], QGP[p], KGP[p]
        KTM, VT, SGT, xlnt = KTMP[p], VTP[p], SGTP[p], xlntP[p]
        bn12f, mvgf = bnF[p], mvF[p]
        stAf, stBf = stAF[p], stBF[p]
        g63, g127, egln, egls = GL63[p], GL127[p], EGLN[p], EGLS[p]

        bbn, bmv, bst = b_bnF[p], b_mvF[p], b_stF[p]
        bgl, bz1 = b_GLP[p], b_z1P[p]

        for t in range(G):
            P.dma("y", LDX[p][t], out=x_g[:, t * 1024:(t + 1) * 1024],
                  in_=x_h[(t0g + t) * 128:(t0g + t + 1) * 128, :],
                  writes=(b_xg[p][t],))
        for t in range(G):
            xs = x_g[:, t * 1024:(t + 1) * 1024]
            ew("v", lambda v, xs=xs, bn12f=bn12f: v.bn_stats(bn12f[:, :6], xs[:, :512]),
               reads=(b_xg[p][t],), writes=(bbn,))
            ew("v", lambda v, xs=xs, bn12f=bn12f: v.bn_stats(bn12f[:, 6:12], xs[:, 512:]),
               reads=(b_xg[p][t],), writes=(bbn,))
            ew("v", lambda v, t=t, bn12f=bn12f, mvgf=mvgf: v.bn_aggr(
                mvgf[:, 2 * t:2 * t + 2], bn12f[:, :]),
               reads=(bbn,), writes=(bmv,))
        ew("s", lambda s, mvgf=mvgf, stAf=stAf: s.activation(
            stAf[:, :G], mvgf[:, 1::2], AF.Ln, bias=cst(C_EPS), scale=1.0),
           reads=(bmv, b_consts), writes=(bst,))
        ew("s", lambda s, stAf=stAf: s.activation(stAf[:, :G], stAf[:, :G],
                                                  AF.Exp, bias=0.0, scale=-0.5),
           reads=(bst,), writes=(bst,))
        ew("v", lambda v, mvgf=mvgf, stBf=stBf: v.tensor_scalar_mul(
            stBf[:, :G], mvgf[:, 0::2], -1.0),
           reads=(bmv,), writes=(bst,))
        for t in range(G):
            ew("v", lambda v, t=t, xlnt=xlnt, x_g=x_g, stAf=stAf, stBf=stBf:
               v.tensor_scalar(xlnt[:, :], x_g[:, t * 1024:(t + 1) * 1024],
                               stBf[:, t:t + 1], stAf[:, t:t + 1],
                               ALU.add, ALU.mult),
               reads=(b_xg[p][t], bst), writes=(b_xlnt[p],))
            for half in range(2):
                pb = PP[0][:, half * 512:(half + 1) * 512]
                for j in range(4):
                    kc = half * 4 + j
                    ew("t", lambda t_, kc=kc, j=j, xlnt=xlnt, pb=pb: t_.matmul(
                        pb[:, j * 128:(j + 1) * 128],
                        xlnt[:, kc * 128:(kc + 1) * 128], identb[:, :],
                        start=True, stop=True),
                       reads=(b_xlnt[p], b_ident), writes=(b_PP[0],),
                       inc=(j == 3))
                dst = X_fm.rearrange("q (k c) -> q k c", k=8, c=W)[
                    :, half * 4:(half + 1) * 4, t * 128:(t + 1) * 128]
                ew("s", lambda s, dst=dst, pb=pb: s.copy(dst, pb[:, :512]),
                   reads=(b_PP[0],), writes=(b_Xfm[p],))
        yield
        for kc in range(8):
            ew("t", lambda t_, kc=kc, X_fm=X_fm: t_.matmul(
                PP[1][:16, :W], Wgk1_s[:, kc * 16:(kc + 1) * 16],
                X_fm[:, kc * W:(kc + 1) * W], start=(kc == 0), stop=(kc == 7)),
               reads=(b_Wgk, b_Xfm[p]), writes=(b_PP[1],), inc=(kc == 7))
        ew("s", lambda s, z1bp=z1b[p]: s.copy(z1bp[:16, :], PP[1][:16, :W]),
           reads=(b_PP[1],), writes=(bz1,))

        def gates(ms):
            sQ = scq[p * 2 + ms % 2]
            sK = sck[p * 2 + ms % 2]
            bQ = b_scq[p * 2 + ms % 2]
            bK = b_sck[p * 2 + ms % 2]
            sA = scg[2 * p + ms % 2]
            bA = b_scg[2 * p + ms % 2]
            ew("t", lambda t_, ms=ms, z1bp=z1b[p]: t_.matmul(
                PP[0][:, :W], Wgk2_s[:16, ms * 128:(ms + 1) * 128],
                z1bp[:16, :], start=True, stop=True),
               reads=(b_Wgk, bz1), writes=(b_PP[0],))
            ew("v", lambda v, ms=ms, sA=sA: v.tensor_scalar(
                sA[:, :W], PP[0][:, :W], bgk2n[:, ms:ms + 1], -1.0,
                ALU.subtract, ALU.mult),
               reads=(b_PP[0], b_vecs), writes=(bA,))
            ew("v", lambda v, sA=sA, sK=sK: v.tensor_single_scalar(
                sK[:, :W], sA[:, :W], 0.0, ALU.max),
               reads=(bA,), writes=(bK,))
            ew("s", lambda s, sA=sA, sQ=sQ: s.activation(sQ[:, :W], sA[:, :W],
                                                         AF.Abs),
               reads=(bA,), writes=(bQ,))
            ew("s", lambda s, sQ=sQ: s.activation(sQ[:, :W], sQ[:, :W], AF.Exp,
                                                  bias=0.0, scale=-1.0),
               reads=(bQ,), writes=(bQ,))
            ew("s", lambda s, sQ=sQ: s.activation(sQ[:, :W], sQ[:, :W], AF.Ln,
                                                  bias=1.0, scale=1.0),
               reads=(bQ,), writes=(bQ,))
            ew("v", lambda v, sA=sA, sQ=sQ, sK=sK: v.tensor_add(
                sA[:, :W], sK[:, :W], sQ[:, :W]),
               reads=(bK, bQ), writes=(bA,))
            ew("s", lambda s, sA=sA, sQ=sQ: s.activation(
                sQ[:, :W], sA[:, :W], AF.Exp, bias=cst(C_LN16), scale=-1.0 / 16),
               reads=(bA, b_consts), writes=(bQ,))
            ew("s", lambda s, sA=sA, sK=sK: s.activation(
                sK[:, :W], sA[:, :W], AF.Exp, bias=0.0, scale=1.0 / 16),
               reads=(bA,), writes=(bK,))
            for t in range(G):
                ew("v", lambda v, ms=ms, t=t, sK=sK, g63=g63: v.tensor_copy(
                    g63[:, ms * G + t:ms * G + t + 1],
                    sK[:, t * 128 + 63:t * 128 + 64]),
                   reads=(bK,), writes=(bgl,))
                ew("v", lambda v, ms=ms, t=t, sK=sK, g127=g127: v.tensor_copy(
                    g127[:, ms * G + t:ms * G + t + 1],
                    sK[:, t * 128 + 127:t * 128 + 128]),
                   reads=(bK,), writes=(bgl,))
            ew("v", lambda v, ms=ms, egln=egln, g63=g63: v.reciprocal(
                egln[:, ms * G:(ms + 1) * G], g63[:, ms * G:(ms + 1) * G]),
               reads=(bgl,), writes=(bgl,))
            ew("v", lambda v, ms=ms, egls=egls, g63=g63, g127=g127: v.tensor_mul(
                egls[:, ms * G:(ms + 1) * G], g63[:, ms * G:(ms + 1) * G],
                g127[:, ms * G:(ms + 1) * G]),
               reads=(bgl,), writes=(bgl,))
            ew("v", lambda v, ms=ms, egls=egls: v.reciprocal(
                egls[:, ms * G:(ms + 1) * G], egls[:, ms * G:(ms + 1) * G]),
               reads=(bgl,), writes=(bgl,))
            for t in range(G):
                h2 = slice(t * 128 + 64, t * 128 + 128)
                ew("v", lambda v, ms=ms, t=t, h2=h2, sQ=sQ, egln=egln:
                   v.tensor_scalar_mul(sQ[:, h2], sQ[:, h2],
                                       egln[:, ms * G + t:ms * G + t + 1]),
                   reads=(bgl,), writes=(bQ,))
                ew("v", lambda v, ms=ms, t=t, h2=h2, sK=sK, g63=g63:
                   v.tensor_scalar_mul(sK[:, h2], sK[:, h2],
                                       g63[:, ms * G + t:ms * G + t + 1]),
                   reads=(bgl,), writes=(bK,))

        def qkproj(ms):
            sQ = scq[p * 2 + ms % 2]
            sK = sck[p * 2 + ms % 2]
            bQ = b_scq[p * 2 + ms % 2]
            bK = b_sck[p * 2 + ms % 2]
            wq_ = wqb[ms % 2]
            bwq = b_wqb[ms % 2]
            wk_ = wkb[ms % 2]
            bwk = b_wkb[ms % 2]
            for kc in range(8):
                ew("t", lambda t_, ms=ms, kc=kc, wq_=wq_, X_fm=X_fm: t_.matmul(
                    PP[1][:, :W], wq_[:, kc * 128:(kc + 1) * 128],
                    X_fm[:, kc * W:(kc + 1) * W], start=(kc == 0), stop=(kc == 7)),
                   reads=(bwq, b_Xfm[p]), writes=(b_PP[1],), inc=(kc == 7))
            ew("v", lambda v, ms=ms, QG=QG, sQ=sQ: v.tensor_mul(
                QG[:, ms * W:(ms + 1) * W], PP[1][:, :W], sQ[:, :W]),
               reads=(b_PP[1], bQ), writes=(b_QG[p],))
            for kc in range(8):
                ew("t", lambda t_, ms=ms, kc=kc, wk_=wk_, X_fm=X_fm: t_.matmul(
                    PP[1][:, 512:512 + W], wk_[:, kc * 128:(kc + 1) * 128],
                    X_fm[:, kc * W:(kc + 1) * W], start=(kc == 0), stop=(kc == 7)),
                   reads=(bwk, b_Xfm[p]), writes=(b_PP[1],), inc=(kc == 7))
            ew("v", lambda v, ms=ms, KG=KG, sK=sK: v.tensor_mul(
                KG[:, ms * W:(ms + 1) * W], PP[1][:, 512:512 + W], sK[:, :W]),
               reads=(b_PP[1], bK), writes=(b_KG[p],))

        for ms in range(8):
            P.dma("y", WQB[ms % 2], out=wqb[ms % 2][:, :], in_=wqc_d[ms],
                  reads=tuple(b_wqc), writes=(b_wqb[ms % 2],))
            P.dma("y", WKB[ms % 2], out=wkb[ms % 2][:, :], in_=wkc_d[ms],
                  reads=tuple(b_wkc), writes=(b_wkb[ms % 2],))
            gates(ms)
            if ms >= 1:
                qkproj(ms - 1)
            yield
        qkproj(7)
        yield
        for t in range(G):
            for half in range(2):
                pb = PP[0][:, half * 512:(half + 1) * 512]
                for j in range(4):
                    ms = half * 4 + j
                    ew("t", lambda t_, ms=ms, j=j, t=t, KG=KG, pb=pb: t_.matmul(
                        pb[:, j * 128:(j + 1) * 128],
                        KG[:, ms * W + t * 128: ms * W + (t + 1) * 128],
                        identb[:, :], start=True, stop=True),
                       reads=(b_KG[p], b_ident), writes=(b_PP[0],), inc=(j == 3))
                ew("s", lambda s, half=half, t=t, KTM=KTM, pb=pb: s.copy(
                    KTM[t][:, half * 512:(half + 1) * 512], pb[:, :512]),
                   reads=(b_PP[0],), writes=(b_KTM[p][t],))
            yield
        for which, wbuf, bwbuf, WSTR, src_d, b_src, dstP, bdstP in (
                ("gt", [wgb[0], wgb[1], wkb[0], wkb[1]],
                 [b_wgb[0], b_wgb[1], b_wkb[0], b_wkb[1]],
                 [WGB[0], WGB[1], WKB[0], WKB[1]],
                 wgbf_d, b_wgbf, SGT, b_SGT[p]),
                ("v", [wvb[0], wvb[1], wqb[0], wqb[1]],
                 [b_wvb[0], b_wvb[1], b_wqb[0], b_wqb[1]],
                 [WVB[0], WVB[1], WQB[0], WQB[1]],
                 wvbf_d, b_wvbf, VT, b_VT[p])):
            for ns2 in range(2):
                for kc in range(8):
                    wb = wbuf[kc % 4]
                    bwb = bwbuf[kc % 4]
                    P.dma("y", WSTR[kc % 4], out=wb[:, :],
                          in_=src_d[kc][:, ns2 * 1024:(ns2 + 1) * 1024],
                          reads=(b_src[kc],), writes=(bwb,))
                    for t in range(G):
                        for nsi in range(2):
                            bk = PP[t][:, nsi * 512:(nsi + 1) * 512]
                            ew("t", lambda t_, kc=kc, t=t, nsi=nsi, wb=wb,
                               bk=bk, X_fm=X_fm: t_.matmul(
                                   bk,
                                   X_fm[:, kc * W + t * 128: kc * W + (t + 1) * 128],
                                   wb[:, nsi * 512:(nsi + 1) * 512],
                                   start=(kc == 0), stop=(kc == 7)),
                               reads=(b_Xfm[p], bwb), writes=(b_PP[t],),
                               inc=(kc == 7 or (t == G - 1 and nsi == 1)))
                yield
                for t in range(G):
                    for nsi in range(2):
                        q4 = ns2 * 2 + nsi
                        bk = PP[t][:, nsi * 512:(nsi + 1) * 512]
                        if which == "v":
                            ew("s", lambda s, t=t, q4=q4, bk=bk, dstP=dstP: s.copy(
                                dstP[t][:, q4 * 512:(q4 + 1) * 512], bk),
                               reads=(b_PP[t],), writes=(bdstP[t],))
                        else:
                            sS = scg[2 * p + (t * 2 + nsi) % 2]
                            bS = b_scg[2 * p + (t * 2 + nsi) % 2]
                            ew("s", lambda s, bk=bk, sS=sS: s.activation(
                                sS[:, :512], bk, AF.Exp, bias=0.0, scale=-1.0),
                               reads=(b_PP[t],), writes=(bS,))
                            ew("v", lambda v, sS=sS: v.tensor_single_scalar(
                                sS[:, :512], sS[:, :512], 1e30, ALU.min),
                               reads=(bS,), writes=(bS,))
                            ew("s", lambda s, sS=sS: s.activation(
                                sS[:, :512], sS[:, :512], AF.Ln, bias=1.0,
                                scale=1.0),
                               reads=(bS,), writes=(bS,))
                            ew("s", lambda s, sS=sS: s.activation(
                                sS[:, :512], sS[:, :512], AF.Exp, bias=0.0,
                                scale=-1.0),
                               reads=(bS,), writes=(bS,))
                            ew("v", lambda v, t=t, q4=q4, bk=bk, sS=sS,
                               dstP=dstP: v.tensor_mul(
                                   dstP[t][:, q4 * 512:(q4 + 1) * 512], bk,
                                   sS[:, :512]),
                               reads=(b_PP[t], bS), writes=(bdstP[t],))

        yield

    def scanback(g):
        p = g % 2
        t0g = g * G
        x_g, QG, KG = x_gP[p], QGP[p], KGP[p]
        KTM, VT, SGT = KTMP[p], VTP[p], SGTP[p]
        g63, g127, egln, egls = GL63[p], GL127[p], EGLN[p], EGLS[p]
        bgl = b_GLP[p]
        AB = ABUF[p]
        bAB = b_xlnt[p]
        OF_l = [OF, OF2B[p]]
        bOF_l = [b_OF, b_QG[p]]
        OFm_l = [OFm, OFKB[p]]
        bOFm_l = [b_OFm, b_KG[p]]
        for t in range(G):
            tile = t0g + t
            OFt = OF_l[t]
            bOFt = bOF_l[t]
            for h in range(4):
                for ci in range(2):
                    ms = 2 * h + ci
                    ew("t", lambda t_, h=h, ms=ms, ci=ci, t=t, KG=KG, QG=QG:
                       t_.matmul(bank(0)[:, h * 128:(h + 1) * 128],
                                 KG[:, ms * W + t * 128: ms * W + (t + 1) * 128],
                                 QG[:, ms * W + t * 128: ms * W + (t + 1) * 128],
                                 start=(ci == 0), stop=(ci == 1)),
                       reads=(b_KG[p], b_QG[p]), writes=(bbank(0),),
                       inc=(h == 3 and ci == 1))
            ew("v", lambda v, AB=AB: v.tensor_mul(AB[:, :512], bank(0),
                                                  mask4[:, :]),
               reads=(bbank(0), b_mask), writes=(bAB,))
            yield
            obk = [1, 2, 3, 0]
            for h in range(4):
                bi = obk[h]
                for ci in range(2):
                    ms = 2 * h + ci
                    ew("t", lambda t_, ms=ms, t=t, ci=ci, bi=bi, QG=QG: t_.matmul(
                        bank(bi), QG[:, ms * W + t * 128: ms * W + (t + 1) * 128],
                        S_bfc[:, ms * 512:(ms + 1) * 512],
                        start=(ci == 0), stop=False),
                       reads=(b_QG[p], b_Sms[ms]), writes=(bbank(bi),),
                       inc=(ci == 1))
                ew("t", lambda t_, h=h, t=t, bi=bi, VT=VT, AB=AB: t_.matmul(
                    bank(bi), AB[:, h * 128:(h + 1) * 128],
                    VT[t][:, h * 512:(h + 1) * 512], start=False, stop=True),
                   reads=(bAB, b_VT[p][t]), writes=(bbank(bi),), inc=True)
            yield
            for h in range(4):
                bi = obk[h]
                ew("v", lambda v, bi=bi: v.bn_stats(bn12[:, :6], bank(bi)),
                   reads=(bbank(bi),), writes=(b_bn,))
                ew("v", lambda v, h=h: v.bn_aggr(mo8[:, 2 * h:2 * h + 2],
                                                 bn12[:, :6]),
                   reads=(b_bn,), writes=(b_mo,))
            ew("v", lambda v: v.scalar_tensor_tensor(
                rsO[:, :4], mo8[:, 0::2], 1.0, mo8[:, 0::2], ALU.mult, ALU.mult),
               reads=(b_mo,), writes=(b_rsO,))
            ew("v", lambda v: v.tensor_add(rsO[:, :4], rsO[:, :4], mo8[:, 1::2]),
               reads=(b_mo,), writes=(b_rsO,))
            ew("s", lambda s: s.activation(rsO[:, :4], rsO[:, :4], AF.Ln,
                                           bias=cst(C_EPS), scale=1.0),
               reads=(b_consts,), writes=(b_rsO,))
            ew("s", lambda s: s.activation(rsO[:, :4], rsO[:, :4], AF.Exp,
                                           bias=0.0, scale=-0.5),
               reads=(), writes=(b_rsO,))
            yield
            for h in range(4):
                bi = obk[h]
                ew("v", lambda v, h=h, bi=bi, t=t, SGT=SGT, OFt=OFt:
                   v.scalar_tensor_tensor(
                       OFt[:, h * 512:(h + 1) * 512], bank(bi), rsO[:, h:h + 1],
                       SGT[t][:, h * 512:(h + 1) * 512], ALU.mult, ALU.mult),
                   reads=(bbank(bi), b_rsO, b_SGT[p][t]), writes=(bOFt,))
                yield
            for h in range(4):
                for ci in range(2):
                    ms = 2 * h + ci
                    ew("t", lambda t_, ms=ms, t=t, ci=ci, KTM=KTM, VT=VT: t_.matmul(
                        bank(2 + ci), KTM[t][:, ms * 128:(ms + 1) * 128],
                        VT[t][:, (ms // 2) * 512:(ms // 2 + 1) * 512],
                        start=True, stop=True),
                       reads=(b_KTM[p][t], b_VT[p][t]), writes=(bbank(2 + ci),),
                       inc=True)
                ew("v", lambda v, h=h: v.tensor_add(
                    scB2[:, :], S_bfc[:, h * 1024:(h + 1) * 1024], PP[3][:, :]),
                   reads=(b_Sms[2 * h], b_Sms[2 * h + 1], b_PP[3]),
                   writes=(b_scB1,))
                for ci in range(2):
                    ms = 2 * h + ci
                    ew("v", lambda v, ms=ms, t=t, ci=ci, egls=egls:
                       v.tensor_scalar_mul(
                           S_bfc[:, ms * 512:(ms + 1) * 512],
                           scB2[:, ci * 512:(ci + 1) * 512],
                           egls[:, ms * G + t:ms * G + t + 1]),
                       reads=(b_scB1, bgl), writes=(b_Sms[ms],))
                yield
            yield
        for t in range(G):
            OFt, bOFt = OF_l[t], bOF_l[t]
            OMt, bOMt = OFm_l[t], bOFm_l[t]
            for half in range(4):
                bi = 3 if half % 2 == 0 else 0
                for j in range(4):
                    vc = half * 4 + j
                    ew("t", lambda t_, vc=vc, j=j, bi=bi, OFt=OFt: t_.matmul(
                        bank(bi)[:, j * 128:(j + 1) * 128],
                        OFt[:, vc * 128:(vc + 1) * 128], identb[:, :],
                        start=True, stop=True),
                       reads=(bOFt, b_ident), writes=(bbank(bi),), inc=(j == 3))
                ew("s", lambda s, half=half, bi=bi, OMt=OMt: s.copy(
                    OMt[:, half * 512:(half + 1) * 512], bank(bi)),
                   reads=(bbank(bi),), writes=(bOMt,))
            yield
        WOL = [wsb[0], wsb[1], wgb[0], wgb[1]]
        bWOL = [b_wsb[0], b_wsb[1], b_wgb[0], b_wgb[1]]
        SOL = [LWS[0], LWS[1], WGB[0], WGB[1]]
        for vc in range(16):
            wb = WOL[vc % 4]
            bwb = bWOL[vc % 4]
            P.dma("y", SOL[vc % 4], out=wb[:, :1024], in_=wobf_d[vc],
                  reads=(b_wobf[vc],), writes=(bwb,))
            for t in range(G):
                ew("t", lambda t_, vc=vc, wb=wb, t=t: t_.matmul(
                    PP[2 + t][:, 0:512],
                    OFm_l[t][:, vc * 128:(vc + 1) * 128],
                    wb[:, 0:512],
                    start=(vc == 0), stop=(vc == 15)),
                   reads=(bOFm_l[t], bwb), writes=(b_PP[2 + t],), inc=False)
                ew("t", lambda t_, vc=vc, wb=wb, t=t: t_.matmul(
                    PP[2 + t][:, 512:1024],
                    OFm_l[t][:, vc * 128:(vc + 1) * 128],
                    wb[:, 512:1024],
                    start=(vc == 0), stop=(vc == 15)),
                   reads=(bOFm_l[t], bwb), writes=(b_PP[2 + t],), inc=True)
            if vc % 4 == 3:
                yield
        ew("s", lambda s: s.copy(sc3[:, :], PP[2][:, :]),
           reads=(b_PP[2],), writes=(b_sc3,))
        ew("s", lambda s: s.copy(sc4[:, :], PP[3][:, :]),
           reads=(b_PP[3],), writes=(b_sc4,))

        def ln_chain(t):
            tile = t0g + t
            if t == 0:
                bnT, mvT, stAT, stBT, scT = bn12, mvg, stA, stB, sc3
                bbnT, bmvT, bstT, bscT = b_bn, b_mv, b_st, b_sc3
            else:
                bnT, mvT, stAT, stBT, scT = bnF[p], mvF[p], stAF[p], stBF[p], sc4
                bbnT, bmvT, bstT, bscT = b_bnF[p], b_mvF[p], b_stF[p], b_sc4
            ew("v", lambda v, scT=scT, bnT=bnT: v.bn_stats(
                bnT[:, :6], scT[:, :512]), reads=(bscT,), writes=(bbnT,))
            ew("v", lambda v, scT=scT, bnT=bnT: v.bn_stats(
                bnT[:, 6:12], scT[:, 512:]), reads=(bscT,), writes=(bbnT,))
            yield
            ew("v", lambda v, bnT=bnT, mvT=mvT: v.bn_aggr(mvT[:, :2], bnT[:, :]),
               reads=(bbnT,), writes=(bmvT,))
            yield
            ew("s", lambda s, mvT=mvT, stAT=stAT: s.activation(
                stAT[:, :1], mvT[:, 1:2], AF.Ln, bias=cst(C_EPS), scale=1.0),
               reads=(bmvT, b_consts), writes=(bstT,))
            ew("s", lambda s, stAT=stAT: s.activation(
                stAT[:, :1], stAT[:, :1], AF.Exp, bias=0.0, scale=-0.5),
               reads=(), writes=(bstT,))
            ew("v", lambda v, mvT=mvT, stBT=stBT: v.tensor_scalar_mul(
                stBT[:, :1], mvT[:, 0:1], -1.0),
               reads=(bmvT,), writes=(bstT,))
            yield
            ew("v", lambda v, scT=scT, stAT=stAT, stBT=stBT:
               v.tensor_scalar(scT[:, :], scT[:, :], stBT[:, :1], stAT[:, :1],
                               ALU.add, ALU.mult),
               reads=(bstT,), writes=(bscT,))
            yield
            xb = xpb[t % 2]
            bxb = b_xpb[t % 2]
            ew("v", lambda v, t=t, xb=xb, x_g=x_g, scT=scT: v.tensor_add(
                xb[:, :], scT[:, :], x_g[:, t * 1024:(t + 1) * 1024]),
               reads=(bscT, b_xg[p][t]), writes=(bxb,))
            P.dma("y", XPS[t % 2], out=xp_d[tile * 128:(tile + 1) * 128, :],
                  in_=xb[:, :], reads=(bxb,), dram_final=(b_xp_t[tile],))
            yield
            ew("v", lambda v, xb=xb, bnT=bnT: v.bn_stats(bnT[:, :6], xb[:, :512]),
               reads=(bxb,), writes=(bbnT,))
            ew("v", lambda v, xb=xb, bnT=bnT: v.bn_stats(bnT[:, 6:12], xb[:, 512:]),
               reads=(bxb,), writes=(bbnT,))
            yield
            ew("v", lambda v, bnT=bnT, mvT=mvT: v.bn_aggr(mvT[:, :2], bnT[:, :]),
               reads=(bbnT,), writes=(bmvT,))
            yield
            ew("s", lambda s, mvT=mvT, stAT=stAT: s.activation(
                stAT[:, :1], mvT[:, 1:2], AF.Ln, bias=cst(C_EPS), scale=1.0),
               reads=(bmvT, b_consts), writes=(bstT,))
            ew("s", lambda s, stAT=stAT: s.activation(
                stAT[:, :1], stAT[:, :1], AF.Exp, bias=0.0, scale=-0.5),
               reads=(), writes=(bstT,))
            ew("v", lambda v, mvT=mvT, stBT=stBT: v.tensor_scalar_mul(
                stBT[:, :1], mvT[:, 0:1], -1.0),
               reads=(bmvT,), writes=(bstT,))
            yield
            x3 = xl3b[t % 2]
            bx3 = b_xl3b[t % 2]
            ew("v", lambda v, xb=xb, x3=x3, stAT=stAT, stBT=stBT:
               v.tensor_scalar(x3[:, :], xb[:, :], stBT[:, :1], stAT[:, :1],
                               ALU.add, ALU.mult),
               reads=(bxb, bstT), writes=(bx3,))
            yield
            ew("v", lambda v, x3=x3, t=t: v.reduce_max(
                rtA[:, t % 2:t % 2 + 1], x3[:, :], AX.X,
                apply_absolute_value=True),
               reads=(bx3,), writes=(b_rtA,))
            ew("v", lambda v, t=t: v.tensor_max(AMX[:, :], AMX[:, :],
                                                rtA[:, t % 2:t % 2 + 1]),
               reads=(b_rtA,), writes=(b_AMX,))
            P.dma("y", X3S[t % 2], out=xl3_d[tile * 128:(tile + 1) * 128, :],
                  in_=x3[:, :], reads=(bx3,), dram_final=(b_xl3_t[tile],))
            yield

        chains = [ln_chain(0), ln_chain(1)]
        while chains:
            for c in list(chains):
                try:
                    next(c)
                except StopIteration:
                    chains.remove(c)
            yield

    def emit_allreduce(amx_buf, amx_t, cin_d, cout_d, b_scol, sx_col, rsx_col):
        ew("t", lambda t_: t_.matmul(PP[1][:1, :128], amx_t[:, :], identf[:, :],
                                     start=True, stop=True),
           reads=(amx_buf, b_ident), writes=(b_PP[1],))
        ew("v", lambda v: v.tensor_copy(r1a[:1, :], PP[1][:1, :128]),
           reads=(b_PP[1],), writes=(b_r1,))
        ew("v", lambda v: v.reduce_max(r1b[:1, :1], r1a[:1, :], AX.X),
           reads=(b_r1,), writes=(b_r1,))
        ew("v", lambda v: v.tensor_single_scalar(r1b[:1, :1], r1b[:1, :1],
                                                 1e-8, ALU.max),
           reads=(b_r1,), writes=(b_r1,))
        ew("v", lambda v: v.tensor_copy(ccb[:1, :1], r1b[:1, :1]),
           reads=(b_r1,), writes=(b_ccb,))
        P.dma("g", GDM, out=cin_d[:, :1], in_=ccb[:1, :1], reads=(b_ccb,),
              writes=(b_cc,))
        ew("g", lambda g: g.collective_compute(
            "AllReduce", ALU.max, replica_groups=[list(range(n_cores))],
            ins=[cin_d[:, :1].opt()], outs=[cout_d[:, :1].opt()]),
           reads=(b_cc,), writes=(b_cc,))
        P.dma("g", GDM, out=ccb[:1, :1], in_=cout_d[:, :1], reads=(b_cc,),
              writes=(b_ccb,))
        ew("v", lambda v: v.reciprocal(r1b[:1, :1], ccb[:1, :1]),
           reads=(b_ccb,), writes=(b_r1,))
        ew("v", lambda v: v.tensor_single_scalar(r1b[:1, :1], r1b[:1, :1],
                                                 32767.0, ALU.mult),
           reads=(), writes=(b_r1,))
        ew("t", lambda t_: t_.matmul(PP[1][:, 512:513], onesc[:1, :],
                                     r1b[:1, :1], start=True, stop=True),
           reads=(b_onesc, b_r1), writes=(b_PP[1],))
        ew("v", lambda v: v.tensor_copy(sx_col[:, :], PP[1][:, 512:513]),
           reads=(b_PP[1],), writes=(b_scol,))
        ew("v", lambda v: v.tensor_single_scalar(r1b[:1, :1], ccb[:1, :1],
                                                 1.0 / 32767.0, ALU.mult),
           reads=(b_ccb,), writes=(b_r1,))
        ew("t", lambda t_: t_.matmul(PP[1][:, 512:513], onesc[:1, :],
                                     r1b[:1, :1], start=True, stop=True),
           reads=(b_onesc, b_r1), writes=(b_PP[1],))
        ew("v", lambda v: v.tensor_copy(rsx_col[:, :], PP[1][:, 512:513]),
           reads=(b_PP[1],), writes=(b_scol,))

    SG1 = min(4, NG)

    def f1_pro(u):
        par = (u % 2) * SG1
        for gi in range(SG1):
            g = u * SG1 + gi
            XSIM = XSIMQ[par + gi]
            t0g = g * G
            for t in range(G):
                sl = (gi * G + t) % 4
                lb = xl3l[sl]
                blb = b_xl3l[sl]
                P.dma("y", XL3L[sl], out=lb[:, :],
                      in_=xl3_d[(t0g + t) * 128:(t0g + t + 1) * 128, :],
                      reads=(b_xl3_t[t0g + t],), writes=(blb,))
                for half in range(2):
                    for j in range(4):
                        kc = half * 4 + j
                        ew("t", lambda t_, kc=kc, j=j, lb=lb: t_.matmul(
                            PP[0][:, 512 + j * 128:512 + (j + 1) * 128],
                            lb[:, kc * 128:(kc + 1) * 128], identb[:, :],
                            start=True, stop=True),
                           reads=(blb, b_ident), writes=(b_PP0H,),
                           inc=(j == 3))
                    ew("s", lambda s: s.copy(sc3[:, :512], PP[0][:, 512:1024]),
                       reads=(b_PP0H,), writes=(b_sc3,))
                    xdst = XSIM.rearrange("q (k c) -> q k c", k=8, c=W)[
                        :, half * 4:(half + 1) * 4, t * 128:(t + 1) * 128]
                    emit_round(sc3[:, :512], (b_sc3,), SXC[:, :1], b_sx,
                               xdst, b_XSIM[par + gi], 512)
                    yield
            if dbg:
                P.dma("y", DBGS, out=dXS_d[g], in_=XSIM[:, :],
                      reads=(b_XSIM[par + gi],), writes=(Buf(),))

    def f1_mms(u):
        par = (u % 2) * SG1
        WSL = [wsb[0], wsb[1], wqb[0], wqb[1]]
        bWSL = [b_wsb[0], b_wsb[1], b_wqb[0], b_wqb[1]]
        STL = [LWS[0], LWS[1], WQB[0], WQB[1]]
        for s in range(32):
            wb = WSL[s % 4]
            bwb = bWSL[s % 4]
            P.dma("y", STL[s % 4], out=wb[:, :1024], in_=wqin_d[s],
                  reads=(b_wqin[s], b_wqinB[s]), writes=(bwb,))
            for gi in range(SG1):
                g = u * SG1 + gi
                pp = PP[gi]
                bpp = b_PP[gi]
                for kc in range(8):
                    ew("t", lambda t_, kc=kc, wb=wb, pp=pp, gi=gi, par=par:
                       t_.matmul(
                           pp[:, :W], wb[:, kc * 128:(kc + 1) * 128],
                           XSIMQ[par + gi][:, kc * W:(kc + 1) * W],
                           start=(kc == 0), stop=(kc == 7)),
                       reads=(bwb, b_XSIM[par + gi]), writes=(bpp,),
                       inc=(kc == 7))
                hb = hbb[gi]
                bhb = b_hbb[gi]
                ew("s", lambda s_, s=s, pp=pp, hb=hb: s_.activation(
                    hb[:, :], pp[:, :W], AF.Gelu, bias=binb[:, s:s + 1],
                    scale=ALF[:, s:s + 1]),
                   reads=(bpp, b_ALF, b_vecs), writes=(bhb,))
                P.dma("s", HIDS[gi], out=hid_d[g, s], in_=hb[:, :],
                      reads=(bhb,), dram_final=(b_hid[g][s],))
            for gi in range(SG1):
                ew("v", lambda v, gi=gi: v.reduce_max(
                    rtA[:, gi:gi + 1], hbb[gi][:, :], AX.X,
                    apply_absolute_value=True),
                   reads=(b_hbb[gi],), writes=(b_rtA,))
            ew("v", lambda v: v.reduce_max(
                rtA[:, 4:5], rtA[:, :SG1], AX.X),
               reads=(b_rtA,), writes=(b_rtA,))
            ew("v", lambda v: v.tensor_max(AMH[:, :], AMH[:, :], rtA[:, 4:5]),
               reads=(b_rtA,), writes=(b_AMH,))
            yield

    SG2 = min(2, NG)

    def f2_super(u):
        t0g = u * SG2 * G
        for ti in range(SG2 * G):
            P.dma("y", XPL[ti], out=xpl[ti][:, :],
                  in_=xp_d[(t0g + ti) * 128:(t0g + ti + 1) * 128, :],
                  reads=(b_xp_t[t0g + ti],), writes=(b_xpl[ti],))
        WSL = [wsb[0], wsb[1], wqb[0], wqb[1]]
        bWSL = [b_wsb[0], b_wsb[1], b_wqb[0], b_wqb[1]]
        STL = [LWS[0], LWS[1], WQB[0], WQB[1]]
        for s in range(32):
            wb = WSL[s % 4]
            bwb = bWSL[s % 4]
            P.dma("y", STL[s % 4], out=wb[:, :1024], in_=w2_d[s],
                  reads=(b_w2s[s],), writes=(bwb,))
            j = s % 2
            hl2 = hlb2[j]
            bhl2 = (b_hlb[2 * j], b_hlb[2 * j + 1])
            P.dma("y", HLB[2 * j],
                  out=hl2.rearrange("p (g c) -> p g c", g=SG2, c=W),
                  in_=hid_d[u * SG2:(u + 1) * SG2, s]
                  .rearrange("g p c -> p g c"),
                  reads=tuple(b_hid[u * SG2 + gi][s] for gi in range(SG2)),
                  writes=bhl2)
            h22 = h2b2[j]
            bh22 = (b_h2b[2 * j], b_h2b[2 * j + 1])
            emit_round(hl2[:, :], bhl2, SHC[:, :1], b_sh, h22[:, :],
                       bh22, 2 * W)
            for gi in range(SG2):
                for t in range(G):
                    for ns in range(2):
                        ew("t", lambda t_, t=t, ns=ns, h22=h22, wb=wb, gi=gi,
                           s=s: t_.matmul(
                            PP[gi * G + t][:, ns * 512:(ns + 1) * 512],
                            h22[:, gi * W + t * 128:gi * W + (t + 1) * 128],
                            wb[:, ns * 512:(ns + 1) * 512],
                            start=(s == 0), stop=(s == 31)),
                           reads=(bh22[0], bh22[1], bwb),
                           writes=(b_PP[gi * G + t],),
                           inc=(ns == 1))
            yield
        def ln4_chain(ti):
            tile = t0g + ti
            pp = PP[ti]
            bpp = b_PP[ti]
            if ti % 2 == 0:
                bnT, mvT, stAT, stBT = bn12, mvg, stA, stB
                scT, sc4T = sc3, sc4
                bbnT, bmvT, bstT = b_bn, b_mv, b_st
                bscT, bsc4T = b_sc3, b_sc4
            else:
                bnT, mvT, stAT, stBT = bnF[0], mvF[0], stAF[0], stBF[0]
                scT, sc4T = scB2, wacc[0]
                bbnT, bmvT, bstT = b_bnF[0], b_mvF[0], b_stF[0]
                bscT, bsc4T = b_scB1, b_wacc
            ew("v", lambda v, pp=pp, scT=scT: v.tensor_add(
                scT[:, :], pp[:, :], boutb[:, :]),
               reads=(bpp, b_vecs), writes=(bscT,))
            yield
            ew("v", lambda v, scT=scT, bnT=bnT: v.bn_stats(
                bnT[:, :6], scT[:, :512]), reads=(bscT,), writes=(bbnT,))
            ew("v", lambda v, scT=scT, bnT=bnT: v.bn_stats(
                bnT[:, 6:12], scT[:, 512:]), reads=(bscT,), writes=(bbnT,))
            yield
            ew("v", lambda v, bnT=bnT, mvT=mvT: v.bn_aggr(mvT[:, :2], bnT[:, :]),
               reads=(bbnT,), writes=(bmvT,))
            yield
            ew("s", lambda s, mvT=mvT, stAT=stAT: s.activation(
                stAT[:, :1], mvT[:, 1:2], AF.Ln, bias=cst(C_EPS), scale=1.0),
               reads=(bmvT, b_consts), writes=(bstT,))
            ew("s", lambda s, stAT=stAT: s.activation(
                stAT[:, :1], stAT[:, :1], AF.Exp, bias=0.0, scale=-0.5),
               reads=(), writes=(bstT,))
            ew("v", lambda v, mvT=mvT, stBT=stBT: v.tensor_scalar_mul(
                stBT[:, :1], mvT[:, 0:1], -1.0),
               reads=(bmvT,), writes=(bstT,))
            yield
            ew("v", lambda v, scT=scT, sc4T=sc4T, stAT=stAT, stBT=stBT:
               v.tensor_scalar(sc4T[:, :], scT[:, :], stBT[:, :1], stAT[:, :1],
                               ALU.add, ALU.mult),
               reads=(bscT, bstT), writes=(bsc4T,))
            yield
            ob = outb[ti % 2]
            bob = b_outb[ti % 2]
            ew("v", lambda v, ob=ob, ti=ti, sc4T=sc4T: v.tensor_add(
                ob[:, :], sc4T[:, :], xpl[ti][:, :]),
               reads=(bsc4T, b_xpl[ti]), writes=(bob,))
            P.dma("s", STY[ti % 2], out=y_h[tile * 128:(tile + 1) * 128, :],
                  in_=ob[:, :], reads=(bob,), writes=(Buf(),))
            yield

        for base in range(0, SG2 * G, 2):
            chains = [ln4_chain(base)]
            if base + 1 < SG2 * G:
                chains.append(ln4_chain(base + 1))
            while chains:
                for c in list(chains):
                    try:
                        next(c)
                    except StopIteration:
                        chains.remove(c)
                yield

    def drive(*gens, bg=None):
        live = [gg for gg in gens if gg is not None]
        while live:
            for gg in list(live):
                try:
                    next(gg)
                except StopIteration:
                    live.remove(gg)
            if bg is not None:
                try:
                    next(bg)
                except StopIteration:
                    bg = None

    def prep_stream():
        for q in range(4):
            for _ in emit_win_prep(q):
                yield
        for c in range(32):
            for _ in emit_wout_prep(c):
                yield

    def wout_stream():
        for c in range(32):
            for _ in emit_wout_prep(c):
                yield

    def win_stream():
        for q in range(4):
            for _ in emit_win_prep(q):
                yield

    def _chain(*gens):
        for gg in gens:
            for _ in gg:
                yield

    wouts = wout_stream()
    preps = _chain(wouts, win_stream())
    for g in range(NG):
        drive(front(g), scanback(g - 1) if g >= 1 else None, bg=wouts)
    drive(scanback(NG - 1), bg=preps)
    drive(preps)

    emit_allreduce(b_AMX, AMX, ccxi_d, ccxo_d, b_sx, SXC, RSXC)
    ew("v", lambda v: v.tensor_single_scalar(ALF[:, :], AMF[:, :], 1.0 / 127.0,
                                             ALU.mult),
       reads=(b_AMF,), writes=(b_ALF,))
    ew("v", lambda v: v.tensor_scalar_mul(ALF[:, :], ALF[:, :], RSXC[:, :1]),
       reads=(b_sx,), writes=(b_ALF,))

    NU1 = NG // SG1
    drive(f1_pro(0))
    for u in range(NU1):
        drive(f1_mms(u), f1_pro(u + 1) if u + 1 < NU1 else None)

    emit_allreduce(b_AMH, AMH, cchi_d, ccho_d, b_sh, SHC, RSHC)
    ew("v", lambda v: v.tensor_single_scalar(BET[:, :], SOC[:, :], 1.0 / 127.0,
                                             ALU.mult),
       reads=(b_SOC,), writes=(b_BET,))
    ew("v", lambda v: v.tensor_scalar_mul(BET[:, :], BET[:, :], RSHC[:, :1]),
       reads=(b_sh,), writes=(b_BET,))
    def w2_prep():
        for c in range(32):
            sl = c % 4
            src_ = S_bfc[:, sl * 1024:(sl + 1) * 1024]
            P.dma("y", XL3L[sl], out=src_, in_=woutq_d[c],
                  reads=(b_woutq[c],),
                  writes=(b_w2l[sl], b_Sms[2 * sl], b_Sms[2 * sl + 1]))
            ew("v", lambda v, c=c, src_=src_: v.tensor_scalar_mul(
                wqt[:, :], src_, BET[:, c:c + 1]),
               reads=(b_w2l[sl], b_BET), writes=(b_wqt,))
            stq = W2SL[sl]
            if c >= 4:
                P.raw("y", lambda o, stq=stq, v=16 * (c // 4): o.wait_ge(
                    P.dma_sems[stq][0], v))
            P.dma("y", stq, out=w2_d[c], in_=wqt[:, :], reads=(b_wqt,),
                  writes=(b_w2s[c],))
            yield

    w2p = w2_prep()
    for _ in range(6):
        next(w2p, None)
    for u in range(NG // SG2):
        drive(f2_super(u), bg=w2p)
    drive(w2p)

    if dbg:
        ew("v", lambda v: v.tensor_copy(sc4[:, 0:1], SXC[:, :]), reads=(b_sx,), writes=(b_sc4,))
        ew("v", lambda v: v.tensor_copy(sc4[:, 1:2], SHC[:, :]), reads=(b_sh,), writes=(b_sc4,))
        ew("v", lambda v: v.tensor_copy(sc4[:, 2:3], AMX[:, :]), reads=(b_AMX,), writes=(b_sc4,))
        ew("v", lambda v: v.tensor_copy(sc4[:, 3:4], AMH[:, :]), reads=(b_AMH,), writes=(b_sc4,))
        ew("v", lambda v: v.tensor_copy(sc4[:, 4:36], ALF[:, :]), reads=(b_ALF,), writes=(b_sc4,))
        ew("v", lambda v: v.tensor_copy(sc4[:, 36:68], BET[:, :]), reads=(b_BET,), writes=(b_sc4,))
        ew("v", lambda v: v.tensor_copy(sc4[:, 68:100], AMF[:, :]), reads=(b_AMF,), writes=(b_sc4,))
        ew("v", lambda v: v.tensor_copy(sc4[:, 100:132], SOC[:, :]), reads=(b_SOC,), writes=(b_sc4,))
        P.dma("y", DBGS, out=dsc_d[:, :136], in_=sc4[:, :136],
              reads=(b_sc4,), writes=(Buf(),))
    for _sty in STY:
        P.raw("y", lambda o, _sty=_sty: o.wait_ge(
            P.dma_sems[_sty][0], P.dma_sems[_sty][1]))

    with nc.Block() as block:
        P.build(block)
    return nc


def _cons_table(bgk2, b_in, rms_w, b_out):
    cons = np.zeros((128, 54), dtype=np.float32)
    cons[:, 0] = 1e-5
    cons[:, 1] = LN16
    cons[:, 2:10] = np.asarray(bgk2, np.float32).reshape(8, 128).T
    cons[:, 10:42] = np.asarray(b_in, np.float32).reshape(32, 128).T
    cons[:, 42:46] = np.asarray(rms_w, np.float32).reshape(4, 128).T
    cons[:, 46:54] = np.asarray(b_out, np.float32).reshape(8, 128).T
    return cons


def make_in_maps(inputs_by_name, B):
    w = {k: np.ascontiguousarray(v, np.float32)
         for k, v in inputs_by_name.items() if k != "x"}
    w["cons"] = _cons_table(w["bgk2"], w["b_in"], w["rms_w"], w["b_out"])
    x = inputs_by_name["x"]
    return [dict(w, x=np.ascontiguousarray(x[b])) for b in range(B)]


def kernel(L0_attn_hidden_states, L0_attn_Wq, L0_attn_Wk, L0_attn_Wv,
           L0_attn_Wg, L0_attn_Wgk1, L0_attn_Wgk2, L0_attn_bgk2,
           L0_attn_rms_w, L0_attn_Wo, L0_ffn_W_in, L0_ffn_b_in,
           L0_ffn_W_out, L0_ffn_b_out):
    x = np.ascontiguousarray(np.asarray(L0_attn_hidden_states, dtype=np.float32))
    B, T, _ = x.shape
    nc = build(n_cores=B, T=T)
    in_maps = make_in_maps(dict(
        x=x, Wq=L0_attn_Wq, Wk=L0_attn_Wk, Wv=L0_attn_Wv, Wg=L0_attn_Wg,
        Wgk1=L0_attn_Wgk1, Wgk2=L0_attn_Wgk2, bgk2=L0_attn_bgk2,
        rms_w=L0_attn_rms_w, Wo=L0_attn_Wo, W_in=L0_ffn_W_in,
        b_in=L0_ffn_b_in, W_out=L0_ffn_W_out, b_out=L0_ffn_b_out), B)
    res = run_bass_kernel_spmd(nc, in_maps, core_ids=list(range(B)))
    out = np.stack([np.asarray(res.results[b]["y"]) for b in range(B)], axis=0)
    return out.astype(np.float32)


# version 25 (final)
# speedup vs baseline: 1.1172x; 1.0116x over previous
import os
import numpy as np
import concourse.bass as bass
import concourse.mybir as mybir
from concourse.bass_utils import run_bass_kernel_spmd

F32 = mybir.dt.float32
BF16 = mybir.dt.bfloat16
I32 = mybir.dt.int32
AF = mybir.ActivationFunctionType
ALU = mybir.AluOpType
AX = mybir.AxisListType

D = 1024
H = 4
DV = 512
VD = 2048
FF = 4096
LN16 = -2.772588722239781



class Buf:
    __slots__ = ("name", "w", "readers")

    def __init__(self, name=""):
        self.name = name
        self.w = None
        self.readers = []


class Prog:
    ENGS = ("v", "s", "t", "g", "y")

    def __init__(self, nc):
        self.nc = nc
        self.plan = {e: [] for e in self.ENGS}
        self.cnt = {e: 0 for e in self.ENGS}
        self.done = {}
        self.dma_sems = {}
        self.waited = {e: {} for e in self.ENGS}
        self._dma_names = []

    def dma_stream(self, name):
        self._dma_names.append(name)
        self.dma_sems[name] = [None, 0]
        return name

    def _wait(self, eng, tok):
        if tok is None:
            return
        kind, key, val = tok
        if kind == "o":
            assert key == eng, f"cross-engine dep on unfenced group ({key}->{eng})"
            return
        if kind == "df":
            if self.waited[eng].get(("df", key)):
                return
            self.waited[eng][("df", key)] = True
            self.waited[eng][("d", key)] = 10 ** 9
            ds = self.dma_sems

            def thunk(obj, key=key):
                obj.wait_ge(ds[key][0], ds[key][1])
            self.plan[eng].append(thunk)
            return
        cur = self.waited[eng].get((kind, key), 0)
        if val <= cur:
            return
        self.waited[eng][(kind, key)] = val
        if kind == "e":
            done = self.done

            def thunk(obj, key=key, val=val):
                obj.wait_ge(done[key], val)
        else:
            ds = self.dma_sems

            def thunk(obj, key=key, val=val):
                obj.wait_ge(ds[key][0], val)
        self.plan[eng].append(thunk)

    def _deps(self, reads, writes):
        toks = []
        for b in reads:
            if b.w is not None:
                toks.append(b.w)
        for b in writes:
            if b.w is not None:
                toks.append(b.w)
            toks.extend(b.readers)
        return toks

    def emit(self, eng, fn, reads=(), writes=(), inc=True):
        for tok in self._deps(reads, writes):
            self._wait(eng, tok)
        if inc:
            self.cnt[eng] += 1
            val = self.cnt[eng]
            tok = ("e", eng, val)
            done = self.done

            def thunk(obj, fn=fn, eng=eng):
                fn(obj).then_inc(done[eng], 1)
        else:
            tok = None

            def thunk(obj, fn=fn):
                fn(obj)
        self.plan[eng].append(thunk)
        for b in reads:
            if tok is not None:
                b.readers.append(tok)
        for b in writes:
            b.w = tok if tok is not None else ("o", eng, None)
            b.readers = []
        return tok

    def dma(self, eng, stream, out, in_, reads=(), writes=(), slow=False,
            dram_final=()):
        toks = self._deps(reads, writes)
        for b in dram_final:
            toks.extend(b.readers)
        for tok in toks:
            self._wait(eng, tok)
        ds = self.dma_sems[stream]
        ds[1] += 16
        val = ds[1]
        dsd = self.dma_sems

        def thunk(obj, out=out, in_=in_, stream=stream, slow=slow):
            kw = dict(allow_slow_non_contiguous=True) if slow else {}
            obj.dma_start(out=out, in_=in_, **kw).then_inc(dsd[stream][0], 16)
        self.plan[eng].append(thunk)
        tok = ("d", stream, val)
        for b in reads:
            b.readers.append(tok)
        for b in writes:
            b.w = tok
            b.readers = []
        for b in dram_final:
            b.w = ("df", stream, None)
            b.readers = []
        return tok

    def raw(self, eng, fn):
        self.plan[eng].append(fn)

    def build(self, block):
        import contextlib
        with contextlib.ExitStack() as st:
            for e in self.ENGS:
                self.done[e] = st.enter_context(self.nc.semaphore("done_" + e))
            for n in self._dma_names:
                self.dma_sems[n][0] = st.enter_context(self.nc.semaphore("dma_" + n))

            @block.sync
            def _(o):
                for f in self.plan["y"]:
                    f(o)

            @block.vector
            def _(o):
                for f in self.plan["v"]:
                    f(o)

            @block.scalar
            def _(o):
                for f in self.plan["s"]:
                    f(o)

            @block.tensor
            def _(o):
                for f in self.plan["t"]:
                    f(o)

            @block.gpsimd
            def _(o):
                for f in self.plan["g"]:
                    f(o)


def build(n_cores=8, T=4096, dbg=False):
    NT = T // 128
    G = 2
    W = G * 128
    NG = NT // G

    nc = bass.Bass("TRN2", target_bir_lowering=False, detect_race_conditions=False)

    x_h = nc.dram_tensor("x", [T, D], F32, kind="ExternalInput")
    wq_h = nc.dram_tensor("Wq", [D, D], F32, kind="ExternalInput")
    wk_h = nc.dram_tensor("Wk", [D, D], F32, kind="ExternalInput")
    wv_h = nc.dram_tensor("Wv", [D, VD], F32, kind="ExternalInput")
    wg_h = nc.dram_tensor("Wg", [D, VD], F32, kind="ExternalInput")
    wgk1_h = nc.dram_tensor("Wgk1", [D, 16], F32, kind="ExternalInput")
    wgk2_h = nc.dram_tensor("Wgk2", [16, D], F32, kind="ExternalInput")
    bgk2_h = nc.dram_tensor("bgk2", [D], F32, kind="ExternalInput")
    rmsw_h = nc.dram_tensor("rms_w", [DV], F32, kind="ExternalInput")
    wo_h = nc.dram_tensor("Wo", [VD, D], F32, kind="ExternalInput")
    win_h = nc.dram_tensor("W_in", [D, FF], F32, kind="ExternalInput")
    bin_h = nc.dram_tensor("b_in", [FF], F32, kind="ExternalInput")
    wout_h = nc.dram_tensor("W_out", [FF, D], F32, kind="ExternalInput")
    bout_h = nc.dram_tensor("b_out", [D], F32, kind="ExternalInput")
    cons_h = nc.dram_tensor("cons", [128, 54], F32, kind="ExternalInput")
    y_h = nc.dram_tensor("y", [T, D], F32, kind="ExternalOutput")

    _sk = dict(kind="ExternalOutput") if dbg else {}
    xp_d = nc.dram_tensor("xp_d", [T, D], F32, **_sk)
    xl3_d = nc.dram_tensor("xl3_d", [T, D], BF16, **_sk)
    hid_d = nc.dram_tensor("hid_d", [NG, 32, 128, W], BF16, **_sk)
    wgbf_d = nc.dram_tensor("wgbf_d", [8, 128, VD], BF16)
    wvbf_d = nc.dram_tensor("wvbf_d", [8, 128, VD], BF16)
    wqc_d = nc.dram_tensor("wqc_d", [8, 128, 1024], BF16)
    wkc_d = nc.dram_tensor("wkc_d", [8, 128, 1024], BF16)
    wobf_d = nc.dram_tensor("wobf_d", [16, 128, D], BF16)
    wqin_d = nc.dram_tensor("wqin_d", [32, 128, 1024], BF16)
    woutq_d = nc.dram_tensor("woutq_d", [32, 128, D], BF16)
    w2_d = nc.dram_tensor("w2_d", [32, 128, D], BF16)
    ccxi_d = nc.dram_tensor("ccxi_d", [1, 8], F32)
    ccxo_d = nc.dram_tensor("ccxo_d", [1, 8], F32)
    cchi_d = nc.dram_tensor("cchi_d", [1, 8], F32)
    ccho_d = nc.dram_tensor("ccho_d", [1, 8], F32)
    if dbg:
        dsc_d = nc.dram_tensor("dsc_d", [128, 136], F32, kind="ExternalOutput")
        dX_d = nc.dram_tensor("dX_d", [128, 2048], BF16)
        dQG_d = nc.dram_tensor("dQG_d", [128, 2048], BF16)
        dKG_d = nc.dram_tensor("dKG_d", [128, 2048], BF16)
        dV_d = nc.dram_tensor("dV_d", [128, 2048], BF16)
        dSG_d = nc.dram_tensor("dSG_d", [128, 2048], BF16)
        dKT_d = nc.dram_tensor("dKT_d", [128, 1024], BF16)
        dA_d = nc.dram_tensor("dA_d", [128, 512], BF16)
        dOF_d = nc.dram_tensor("dOF_d", [128, 2048], BF16)
        dS_d = nc.dram_tensor("dS_d", [128, 4096], BF16)
        dXS_d = nc.dram_tensor("dXS_d", [NG, 128, 2048], BF16, kind="ExternalOutput")

    A = nc.alloc_sbuf_tensor
    consts = A("consts", [128, 54], F32)
    identb = A("identb", [128, 128], BF16)
    identf = A("identf", [128, 128], F32)
    mask4 = A("mask4", [128, 512], F32)
    onesf = A("onesf", [128, 128], F32)
    onesc = A("onesc", [1, 128], F32)
    bgk2n = A("bgk2n", [128, 8], F32)
    binb = A("binb", [128, 32], F32)
    rms4 = A("rms4", [128, 4], F32)
    boutb = A("boutb", [128, 1024], F32)
    bout_row = A("bout_row", [1, 1024], F32)
    S_bfc = A("S_bfc", [128, 8 * DV], BF16)
    bn12 = A("bn12", [128, 12], F32)
    mvg = A("mvg", [128, 2 * G], F32)
    stA = A("stA", [128, 2], F32)
    stB = A("stB", [128, 2], F32)
    mo8 = A("mo8", [128, 8], F32)
    rsO = A("rsO", [128, 4], F32)
    GL63 = [A(f"GL63_{i}", [128, 8 * G], F32) for i in range(2)]
    GL127 = [A(f"GL127_{i}", [128, 8 * G], F32) for i in range(2)]
    EGLN = [A(f"EGLN_{i}", [128, 8 * G], F32) for i in range(2)]
    EGLS = [A(f"EGLS_{i}", [128, 8 * G], F32) for i in range(2)]
    AMX = A("AMX", [128, 1], F32)
    AMH = A("AMH", [128, 1], F32)
    r1a = A("r1a", [1, 128], F32)
    r1b = A("r1b", [1, 1], F32)
    SXC = A("SXC", [128, 1], F32)
    RSXC = A("RSXC", [128, 1], F32)
    SHC = A("SHC", [128, 1], F32)
    RSHC = A("RSHC", [128, 1], F32)
    ALF = A("ALF", [128, 32], F32)
    AMF = A("AMF", [128, 32], F32)
    SOC = A("SOC", [128, 32], F32)
    RSOC = A("RSOC", [128, 32], F32)
    BET = A("BET", [128, 32], F32)
    ccb = A("ccb", [1, 8], F32)
    Wgk1_s = A("Wgk1_s", [128, 8 * 16], BF16)
    Wgk2_s = A("Wgk2_s", [16, D], BF16)
    wsb = [A(f"wsb{i}", [128, 1024], BF16) for i in range(2)]
    wqb = [A(f"wqb{i}", [128, 1024], BF16) for i in range(2)]
    wkb = [A(f"wkb{i}", [128, 1024], BF16) for i in range(2)]
    wvb = [A(f"wvb{i}", [128, 1024], BF16) for i in range(2)]
    wgb = [A(f"wgb{i}", [128, 1024], BF16) for i in range(2)]
    bnF = [A(f"bnF{i}", [128, 12], F32) for i in range(2)]
    mvF = [A(f"mvF{i}", [128, 2 * G], F32) for i in range(2)]
    stAF = [A(f"stAF{i}", [128, 2], F32) for i in range(2)]
    stBF = [A(f"stBF{i}", [128, 2], F32) for i in range(2)]
    scg = [A(f"scg{i}", [128, 512], F32) for i in range(4)]
    scq = [A(f"scq{i}", [128, 256], F32) for i in range(4)]
    sck = [A(f"sck{i}", [128, 256], F32) for i in range(4)]
    rtA = A("rtA", [128, 6], F32)
    scB2 = A("scB2", [128, 1024], F32)
    z1b = [A(f"z1b{i}", [16, W], BF16) for i in range(2)]
    sc0 = A("sc0", [128, 512], F32)
    sc1 = A("sc1", [128, 512], F32)
    sc2 = A("sc2", [128, 512], F32)
    sci = A("sci", [128, 1024], I32)
    sc3 = A("sc3", [128, 1024], F32)
    sc4 = A("sc4", [128, 1024], F32)
    wpb = [A(f"wpb{i}", [128, 1024], F32) for i in range(2)]
    wacc = [A(f"wacc{i}", [128, 1024], F32) for i in range(2)]
    wqt = A("wqt", [128, 1024], BF16)

    bump = nc._sbuf_addr_for_side('left')
    TOP = 122816
    assert bump <= TOP, f"bump {bump} exceeds overlay base {TOP}"

    AT = nc.alloc_sbuf_tensor_at
    off = [TOP]

    def at(name, shape, dtype, nbytes):
        t = AT(name, shape, dtype, offset=off[0])
        off[0] += nbytes
        assert off[0] <= 229344, f"overlay overflow at {name}"
        return t

    x_gP, X_fmP, QGP, KGP, KTMP, VTP, SGTP, xlntP = [], [], [], [], [], [], [], []
    qg_off, kg_off, xlnt_off = [], [], []
    for pya in range(2):
        x_gP.append(at(f"x_g{pya}", [128, G * 1024], F32, G * 4096))
        X_fmP.append(at(f"X_fm{pya}", [128, 8 * W], BF16, 8 * W * 2))
        qg_off.append(off[0])
        QGP.append(at(f"QG{pya}", [128, 8 * W], BF16, 8 * W * 2))
        kg_off.append(off[0])
        KGP.append(at(f"KG{pya}", [128, 8 * W], BF16, 8 * W * 2))
        KTMP.append([at(f"KTM{pya}_{i}", [128, 1024], BF16, 2048) for i in range(G)])
        VTP.append([at(f"VT{pya}_{i}", [128, VD], BF16, 4096) for i in range(G)])
        SGTP.append([at(f"SGT{pya}_{i}", [128, VD], BF16, 4096) for i in range(G)])
        xlnt_off.append(off[0])
        xlntP.append(at(f"xlnt{pya}", [128, 1024], BF16, 2048))
    OF = at("OF", [128, VD], BF16, 4096)
    OFm = at("OFm", [128, VD], BF16, 4096)
    OF2B = [AT(f"OF2B{i}", [128, VD], BF16, offset=qg_off[i]) for i in range(2)]
    OFKB = [AT(f"OFKB{i}", [128, VD], BF16, offset=kg_off[i]) for i in range(2)]
    ABUF = [AT(f"ABUF{i}", [128, 512], BF16, offset=xlnt_off[i]) for i in range(2)]
    wqt2 = AT("wqt2", [128, 1024], BF16, offset=xlnt_off[0])
    xpb = [at(f"xpb{i}", [128, 1024], F32, 4096) for i in range(2)]
    xl3b = [at(f"xl3b{i}", [128, 1024], BF16, 2048) for i in range(2)]
    attn_end = off[0]

    off = [TOP]
    xsq_base = off[0]
    XSIMQ = [at(f"XSIM{i}", [128, 8 * W], BF16, 8 * W * 2) for i in range(8)]
    wpbX = [AT(f"wpbX{i}", [128, 1024], F32, offset=xsq_base + (4 + i) * 8 * W * 2)
            for i in range(2)]
    xl3l = [at(f"xl3l{i}", [128, 1024], BF16, 2048) for i in range(4)]
    hbb = [at(f"hbb{i}", [128, W], BF16, W * 2) for i in range(4)]

    off = [TOP]
    xpl = [at(f"xpl{i}", [128, 1024], F32, 4096) for i in range(4)]
    outb = [at(f"outb{i}", [128, 1024], F32, 4096) for i in range(2)]
    h2b_off = off[0]
    h2b = [at(f"h2b{i}", [128, W], BF16, W * 2) for i in range(4)]
    hlb_off = off[0]
    hlb = [at(f"hlb{i}", [128, W], BF16, W * 2) for i in range(4)]
    h2b2 = [AT(f"h2b2_{j}", [128, 2 * W], BF16, offset=h2b_off + j * 2 * W * 2)
            for j in range(2)]
    hlb2 = [AT(f"hlb2_{j}", [128, 2 * W], BF16, offset=hlb_off + j * 2 * W * 2)
            for j in range(2)]

    bump = nc._sbuf_addr_for_side('left')
    assert bump <= TOP, f"late bump {bump} exceeds {TOP}"

    PP = [nc.alloc_psum_tensor(f"PP{i}", [128, 1024], F32) for i in range(4)]

    P = Prog(nc)
    LDW = P.dma_stream("ldw")
    LDC = P.dma_stream("ldc")
    GDM = P.dma_stream("gdm")
    WPB = [P.dma_stream(f"wpb{i}") for i in range(4)]
    LDX = [[P.dma_stream(f"ldx{p}_{t}") for t in range(G)] for p in range(2)]
    WQB = [P.dma_stream(f"wqbs{i}") for i in range(2)]
    WKB = [P.dma_stream(f"wkbs{i}") for i in range(2)]
    WVB = [P.dma_stream(f"wvbs{i}") for i in range(2)]
    WGB = [P.dma_stream(f"wgbs{i}") for i in range(2)]
    XPS = [P.dma_stream(f"xps{i}") for i in range(2)]
    X3S = [P.dma_stream(f"x3s{i}") for i in range(2)]
    HIDS = [P.dma_stream(f"hids{i}") for i in range(4)]
    WQIN_SL = [[P.dma_stream(f"wqin_s{i}_{j}") for j in range(2)]
               for i in range(4)]
    WOBF_S = P.dma_stream("wobf_s")
    WOUTQ_S = P.dma_stream("woutq_s")
    W2SL = [P.dma_stream(f"w2s{i}") for i in range(4)]
    LWS = [P.dma_stream(f"lws{i}") for i in range(2)]
    XL3L = [P.dma_stream(f"xl3l{i}") for i in range(4)]
    HLB = [P.dma_stream(f"hlb{i}") for i in range(4)]
    XPL = [P.dma_stream(f"xpl{i}") for i in range(4)]
    STY = [P.dma_stream(f"sty{i}") for i in range(2)]
    DBGS = P.dma_stream("dbgs")

    B = Buf
    b_consts, b_ident, b_mask = B("consts"), B("ident"), B("mask")
    b_onesf, b_onesc = B("onesf"), B("onesc")
    b_vecs = B("vecs")
    b_boutrow = B("boutrow")
    b_Sb = B("Sb")
    b_Sms = [B(f"Sms{i}") for i in range(8)]
    b_bn, b_mv, b_st = B("bn"), B("mv"), B("st")
    b_mo, b_rsO = B("mo"), B("rsO")
    b_GL = B("GL")
    b_AMX, b_AMH = B("AMX"), B("AMH")
    b_r1 = B("r1")
    b_sx, b_sh = B("sx"), B("sh")
    b_ALF, b_AMF = B("ALF"), B("AMF")
    b_SOC, b_BET = B("SOC"), B("BET")
    b_ccb, b_cc = B("ccb"), B("cc")
    b_Wgk = B("Wgk")
    b_wsb = [B() for _ in range(2)]
    b_wqb = [B() for _ in range(2)]
    b_wkb = [B() for _ in range(2)]
    b_wvb = [B() for _ in range(2)]
    b_wgb = [B() for _ in range(2)]
    b_wqc = [B() for _ in range(8)]
    b_wkc = [B() for _ in range(8)]
    b_wvbf = [B() for _ in range(8)]
    b_bnF = [B() for _ in range(2)]
    b_mvF = [B() for _ in range(2)]
    b_stF = [B() for _ in range(2)]
    b_scg = [B(f"scg{i}") for i in range(4)]
    b_scq = [B() for _ in range(4)]
    b_sck = [B() for _ in range(4)]
    b_rtA = B("rtA")
    b_scB1 = B("scB2")
    b_z1 = B("z1")
    b_sc0, b_sc1, b_sc2, b_sci = B("sc0"), B("sc1"), B("sc2"), B("sci")
    b_sc3, b_sc4 = B("sc3"), B("sc4")
    b_wpb = [B() for _ in range(2)]
    b_wacc, b_wqt = B("wacc"), B("wqt")
    b_xg = [[B(f"x_g{p}{t}") for t in range(G)] for p in range(2)]
    b_Xfm = [B() for _ in range(2)]
    b_QG = [B() for _ in range(2)]
    b_KG = [B() for _ in range(2)]
    b_KTM = [[B() for _ in range(G)] for _ in range(2)]
    b_VT = [[B() for _ in range(G)] for _ in range(2)]
    b_SGT = [[B() for _ in range(G)] for _ in range(2)]
    b_xlnt = [B(), B()]
    b_GLP = [B(), B()]
    b_z1P = [B(), B()]
    b_OF, b_OFm = B("OF"), B("OFm")
    b_xpb = [B() for _ in range(2)]
    b_xl3b = [B() for _ in range(2)]
    b_XSIM = [B() for _ in range(8)]
    b_PP0H = B("PP0H")
    b_xl3l = [B() for _ in range(4)]
    b_hbb = [B() for _ in range(4)]
    b_h2b = [B() for _ in range(4)]
    b_hlb = [B() for _ in range(4)]
    b_xpl = [B() for _ in range(4)]
    b_outb = [B() for _ in range(2)]
    b_PP = [B(f"PP{i}") for i in range(4)]
    b_xp_t = [B() for _ in range(NT)]
    b_xl3_t = [B() for _ in range(NT)]
    b_hid = [[B() for _ in range(32)] for _ in range(NG)]
    b_wgbf = [B() for _ in range(8)]
    b_wobf = [B() for _ in range(16)]
    b_wqin = [B() for _ in range(32)]
    b_wqinB = [B() for _ in range(32)]
    b_woutq = [B() for _ in range(32)]
    b_w2s = [B() for _ in range(32)]
    b_w2l = [B() for _ in range(4)]

    C_EPS, C_LN16 = 0, 1

    def cst(i):
        return consts[:, i:i + 1]

    ew = P.emit

    P.dma("y", LDC, out=consts[:, :], in_=cons_h[:, :], dram_final=(b_consts,))
    ew("v", lambda v: v.tensor_copy(bgk2n[:, :], consts[:, 2:10]),
       reads=(b_consts,), writes=(b_vecs,))
    ew("v", lambda v: v.tensor_copy(binb[:, :], consts[:, 10:42]),
       reads=(b_consts,), writes=(b_vecs,))
    ew("v", lambda v: v.tensor_copy(rms4[:, :], consts[:, 42:46]),
       reads=(b_consts,), writes=(b_vecs,))
    ew("g", lambda g: g.memset(onesf[:, :], 1.0), writes=(b_onesf,))
    ew("g", lambda g: g.memset(onesc[:1, :], 1.0), writes=(b_onesc,))
    ew("g", lambda g: g.affine_select(identb[:, :], onesf[:, :], [[1, 128]],
                                      ALU.is_equal, 0.0, base=0,
                                      channel_multiplier=-1),
       reads=(b_onesf,), writes=(b_ident,))
    ew("g", lambda g: g.affine_select(identf[:, :], onesf[:, :], [[1, 128]],
                                      ALU.is_equal, 0.0, base=0,
                                      channel_multiplier=-1),
       reads=(b_onesf,), writes=(b_ident,))
    for hh in range(4):
        ew("g", lambda g, hh=hh: g.affine_select(
            mask4[:, hh * 128:(hh + 1) * 128], onesf[:, :], [[1, 128]],
            ALU.is_ge, 0.0, base=0, channel_multiplier=-1),
           reads=(b_onesf,), writes=(b_mask,))
    ew("v", lambda v: v.tensor_scalar_mul(bgk2n[:, :], bgk2n[:, :], -1.0),
       reads=(b_vecs,), writes=(b_vecs,))
    for k8 in range(8):
        ew("t", lambda t, k8=k8: t.matmul(
            PP[2][:1, k8 * 128:(k8 + 1) * 128], consts[:, 46 + k8:47 + k8],
            identf[:, :], start=True, stop=True),
           reads=(b_consts, b_ident), writes=(b_PP[2],), inc=(k8 == 7))
    ew("v", lambda v: v.tensor_copy(bout_row[:1, :], PP[2][:1, :]),
       reads=(b_PP[2],), writes=(b_boutrow,))
    for ns in range(2):
        ew("t", lambda t, ns=ns: t.matmul(
            PP[3][:, ns * 512:(ns + 1) * 512], onesc[:1, :],
            bout_row[:1, ns * 512:(ns + 1) * 512], start=True, stop=True),
           reads=(b_onesc, b_boutrow), writes=(b_PP[3],), inc=(ns == 1))
    ew("v", lambda v: v.tensor_copy(boutb[:, :], PP[3][:, :]),
       reads=(b_PP[3],), writes=(b_vecs,))
    ew("v", lambda v: v.memset(S_bfc[:, :], 0.0),
       writes=tuple(b_Sms) + (b_Sb,))
    ew("v", lambda v: v.memset(AMX[:, :], 0.0), writes=(b_AMX,))
    ew("v", lambda v: v.memset(AMH[:, :], 0.0), writes=(b_AMH,))

    for kc in range(8):
        P.dma("g", LDW, out=Wgk1_s[:, kc * 16:(kc + 1) * 16],
              in_=wgk1_h[kc * 128:(kc + 1) * 128, :], dram_final=(b_Wgk,))
        P.dma("g", LDW, out=wgbf_d[kc], in_=wg_h[kc * 128:(kc + 1) * 128, :],
              dram_final=(b_wgbf[kc],))
        P.dma("g", LDW, out=wvbf_d[kc], in_=wv_h[kc * 128:(kc + 1) * 128, :],
              dram_final=(b_wvbf[kc],))
        P.dma("g", LDW,
              out=wqc_d[:, :, kc * 128:(kc + 1) * 128],
              in_=wq_h[kc * 128:(kc + 1) * 128, :].rearrange(
                  "p (m c) -> m p c", m=8, c=128),
              dram_final=(b_wqc[kc],))
        P.dma("g", LDW,
              out=wkc_d[:, :, kc * 128:(kc + 1) * 128],
              in_=wk_h[kc * 128:(kc + 1) * 128, :].rearrange(
                  "p (m c) -> m p c", m=8, c=128),
              dram_final=(b_wkc[kc],))
    P.dma("g", LDW, out=Wgk2_s[:16, :], in_=wgk2_h[:, :], dram_final=(b_Wgk,))
    for vc in range(16):
        pb = b_wpb[vc % 2]
        P.dma("y", WPB[vc % 2], out=wpb[vc % 2][:, :],
              in_=wo_h[vc * 128:(vc + 1) * 128, :], writes=(pb,))
        ew("v", lambda v, vc=vc: v.tensor_scalar_mul(
            wqt[:, :], wpb[vc % 2][:, :], rms4[:, vc % 4:vc % 4 + 1]),
           reads=(pb, b_vecs), writes=(b_wqt,))
        P.dma("y", WOBF_S, out=wobf_d[vc], in_=wqt[:, :], reads=(b_wqt,),
              dram_final=(b_wobf[vc],))

    SIMF = os.environ.get("BASS_SIM_FP64") == "1"

    def emit_round(src_ap, src_bufs, scale_ap, scale_buf, out_ap, out_buf, w):
        outs = out_buf if isinstance(out_buf, tuple) else (out_buf,)
        ew("v", lambda v: v.tensor_scalar_mul(sci[:, :w], src_ap, scale_ap),
           reads=tuple(src_bufs) + (scale_buf,), writes=(b_sci,))
        if not SIMF:
            ew("v", lambda v: v.tensor_copy(out_ap, sci[:, :w]),
               reads=(b_sci,), writes=outs)
            return
        ew("v", lambda v: v.tensor_scalar_mul(sc0[:, :w], src_ap, scale_ap),
           reads=tuple(src_bufs) + (scale_buf,), writes=(b_sc0,))
        ew("v", lambda v: v.tensor_sub(sc1[:, :w], sc0[:, :w], sci[:, :w]),
           reads=(b_sc0, b_sci), writes=(b_sc1,))
        ew("v", lambda v: v.tensor_single_scalar(sc2[:, :w], sc1[:, :w], 0.5,
                                                 ALU.is_gt),
           reads=(b_sc1,), writes=(b_sc2,))
        ew("v", lambda v: v.tensor_single_scalar(sc1[:, :w], sc1[:, :w], -0.5,
                                                 ALU.is_lt),
           reads=(b_sc1,), writes=(b_sc1,))
        ew("v", lambda v: v.scalar_tensor_tensor(sc2[:, :w], sc1[:, :w], -1.0,
                                                 sc2[:, :w], ALU.mult, ALU.add),
           reads=(b_sc1, b_sc2), writes=(b_sc2,))
        ew("v", lambda v: v.tensor_add(out_ap, sci[:, :w], sc2[:, :w]),
           reads=(b_sci, b_sc2), writes=outs)

    def emit_round_w(out_buf, dst=None, src=None, bsrc=None):
        if dst is None:
            dst = wqt
        if src is None:
            src, bsrc = sc3, b_sc3
        ew("v", lambda v, src=src: v.tensor_copy(sci[:, :1024], src[:, :]),
           reads=(bsrc,), writes=(b_sci,))
        if not SIMF:
            ew("v", lambda v, dst=dst: v.tensor_copy(dst[:, :], sci[:, :1024]),
               reads=(b_sci,), writes=(out_buf,))
            return
        ew("v", lambda v, src=src: v.tensor_sub(scB2[:, :], src[:, :],
                                                sci[:, :1024]),
           reads=(bsrc, b_sci), writes=(b_scB1,))
        ew("v", lambda v, src=src: v.tensor_single_scalar(
            src[:, :], scB2[:, :], 0.5, ALU.is_gt),
           reads=(b_scB1,), writes=(bsrc,))
        ew("v", lambda v: v.tensor_single_scalar(scB2[:, :], scB2[:, :], -0.5,
                                                 ALU.is_lt),
           reads=(b_scB1,), writes=(b_scB1,))
        ew("v", lambda v, src=src: v.scalar_tensor_tensor(
            src[:, :], scB2[:, :], -1.0, src[:, :], ALU.mult, ALU.add),
           reads=(bsrc, b_scB1), writes=(bsrc,))
        ew("v", lambda v, dst=dst, src=src: v.tensor_add(
            dst[:, :], sci[:, :1024], src[:, :]),
           reads=(b_sci, bsrc), writes=(out_buf,))

    def emit_win_prep(q):
        wp4 = [wpb[0], wpb[1], wpbX[0], wpbX[1]]
        bw4 = [(b_wpb[0],), (b_wpb[1],), (b_XSIM[4], b_KG[0]),
               (b_XSIM[5], b_KTM[0][0], b_KTM[0][1])]
        cl = slice(q * 1024, (q + 1) * 1024)
        for kc in range(8):
            sl = kc % 4
            pb = bw4[sl][0]
            P.dma("y", WPB[sl], out=wp4[sl][:, :],
                  in_=win_h[kc * 128:(kc + 1) * 128, cl], writes=bw4[sl])
            if kc == 0:
                ew("v", lambda v: v.tensor_copy(wacc[0][:, :], wpb[0][:, :]),
                   reads=(pb,), writes=(b_wacc,))
                ew("v", lambda v: v.tensor_copy(wacc[1][:, :], wpb[0][:, :]),
                   reads=(pb,), writes=(b_wacc,))
            else:
                ew("v", lambda v, sl=sl: v.tensor_max(
                    wacc[0][:, :], wacc[0][:, :], wp4[sl][:, :]),
                   reads=(pb, b_wacc), writes=(b_wacc,))
                ew("v", lambda v, sl=sl: v.tensor_tensor(
                    wacc[1][:, :], wacc[1][:, :], wp4[sl][:, :], ALU.min),
                   reads=(pb, b_wacc), writes=(b_wacc,))
            yield
        ew("v", lambda v: v.scalar_tensor_tensor(wacc[0][:, :], wacc[1][:, :],
                                                 -1.0, wacc[0][:, :],
                                                 ALU.mult, ALU.max),
           reads=(b_wacc,), writes=(b_wacc,))
        for sb in range(8):
            s = q * 8 + sb
            ew("t", lambda t, sb=sb: t.matmul(
                PP[1][:, :128], wacc[0][:, sb * 128:(sb + 1) * 128],
                identf[:, :], start=True, stop=True),
               reads=(b_wacc, b_ident), writes=(b_PP[1],))
            ew("v", lambda v, s=s: v.reduce_max(AMF[:, s:s + 1], PP[1][:, :128],
                                                AX.X),
               reads=(b_PP[1],), writes=(b_AMF,))
            yield
        ew("v", lambda v, q=q: v.tensor_single_scalar(
            AMF[:, q * 8:(q + 1) * 8], AMF[:, q * 8:(q + 1) * 8], 1e-8, ALU.max),
           reads=(b_AMF,), writes=(b_AMF,))
        ew("v", lambda v, q=q: v.reciprocal(wacc[1][:, :8],
                                            AMF[:, q * 8:(q + 1) * 8]),
           reads=(b_AMF,), writes=(b_wacc,))
        ew("v", lambda v: v.tensor_single_scalar(wacc[1][:, :8], wacc[1][:, :8],
                                                 127.0, ALU.mult),
           reads=(b_wacc,), writes=(b_wacc,))
        for sb in range(8):
            ew("t", lambda t, sb=sb: t.matmul(
                PP[1][:1, sb * 128:(sb + 1) * 128], wacc[1][:, sb:sb + 1],
                identf[:, :], start=True, stop=True),
               reads=(b_wacc, b_ident), writes=(b_PP[1],), inc=(sb == 7))
        ew("v", lambda v: v.tensor_copy(wacc[0][:1, :1024], PP[1][:1, :]),
           reads=(b_PP[1],), writes=(b_wacc,))
        for nb in range(2):
            ew("t", lambda t, nb=nb: t.matmul(
                PP[1][:, nb * 512:(nb + 1) * 512], onesc[:1, :],
                wacc[0][:1, nb * 512:(nb + 1) * 512], start=True, stop=True),
               reads=(b_onesc, b_wacc), writes=(b_PP[1],), inc=(nb == 1))
        ew("v", lambda v: v.tensor_copy(wacc[1][:, :], PP[1][:, :]),
           reads=(b_PP[1],), writes=(b_wacc,))
        for kc in range(8):
            sl = kc % 4
            pb = bw4[sl][0]
            P.dma("y", WPB[sl], out=wp4[sl][:, :],
                  in_=win_h[kc * 128:(kc + 1) * 128, cl], writes=bw4[sl])
            ew("v", lambda v, sl=sl: v.tensor_mul(wacc[0][:, :], wp4[sl][:, :],
                                                  wacc[1][:, :]),
               reads=(pb, b_wacc), writes=(b_wacc,))
            wq = wqt if kc % 2 == 0 else wqt2
            bwq = b_wqt if kc % 2 == 0 else b_xlnt[0]
            emit_round_w(bwq, dst=wq, src=wacc[0], bsrc=b_wacc)
            j = kc % 2
            bl = b_wqin if j == 0 else b_wqinB
            P.dma("y", WQIN_SL[q][j],
                  out=wqin_d[q * 8:(q + 1) * 8, :, kc * 128:(kc + 1) * 128]
                  .rearrange("s p c -> p s c"),
                  in_=wq.rearrange("q (sb c) -> q sb c", sb=8, c=128),
                  reads=(bwq,),
                  writes=tuple(bl[q * 8 + sb] for sb in range(8)))
            yield

    def emit_wout_prep(c):
        YIELD = True
        pb = b_wpb[c % 2]
        P.dma("y", WPB[c % 2], out=wpb[c % 2][:, :],
              in_=wout_h[c * 128:(c + 1) * 128, :], writes=(pb,))
        ew("v", lambda v, c=c: v.reduce_max(SOC[:, c:c + 1], wpb[c % 2][:, :],
                                            AX.X, apply_absolute_value=True),
           reads=(pb,), writes=(b_SOC,))
        ew("v", lambda v, c=c: v.tensor_single_scalar(
            SOC[:, c:c + 1], SOC[:, c:c + 1], 1e-8, ALU.max),
           reads=(b_SOC,), writes=(b_SOC,))
        ew("v", lambda v, c=c: v.reciprocal(RSOC[:, c:c + 1], SOC[:, c:c + 1]),
           reads=(b_SOC,), writes=(b_SOC,))
        ew("v", lambda v, c=c: v.tensor_scalar(wacc[1][:, :], wpb[c % 2][:, :],
                                               RSOC[:, c:c + 1], 127.0,
                                               ALU.mult, ALU.mult),
           reads=(pb, b_SOC), writes=(b_wacc,))
        emit_round_w(b_wqt, src=wacc[1], bsrc=b_wacc)
        P.dma("y", WOUTQ_S, out=woutq_d[c], in_=wqt[:, :], reads=(b_wqt,),
              dram_final=(b_woutq[c],))
        yield

    def bank(i):
        return PP[2 + i // 2][:, (i % 2) * 512:(i % 2 + 1) * 512]

    def bbank(i):
        return b_PP[2 + i // 2]

    def front(g):
        p = g % 2
        t0g = g * G
        x_g, X_fm, QG, KG = x_gP[p], X_fmP[p], QGP[p], KGP[p]
        KTM, VT, SGT, xlnt = KTMP[p], VTP[p], SGTP[p], xlntP[p]
        bn12f, mvgf = bnF[p], mvF[p]
        stAf, stBf = stAF[p], stBF[p]
        g63, g127, egln, egls = GL63[p], GL127[p], EGLN[p], EGLS[p]

        bbn, bmv, bst = b_bnF[p], b_mvF[p], b_stF[p]
        bgl, bz1 = b_GLP[p], b_z1P[p]

        for t in range(G):
            P.dma("y", LDX[p][t], out=x_g[:, t * 1024:(t + 1) * 1024],
                  in_=x_h[(t0g + t) * 128:(t0g + t + 1) * 128, :],
                  writes=(b_xg[p][t],))
        for t in range(G):
            xs = x_g[:, t * 1024:(t + 1) * 1024]
            ew("v", lambda v, xs=xs, bn12f=bn12f: v.bn_stats(bn12f[:, :6], xs[:, :512]),
               reads=(b_xg[p][t],), writes=(bbn,))
            ew("v", lambda v, xs=xs, bn12f=bn12f: v.bn_stats(bn12f[:, 6:12], xs[:, 512:]),
               reads=(b_xg[p][t],), writes=(bbn,))
            ew("v", lambda v, t=t, bn12f=bn12f, mvgf=mvgf: v.bn_aggr(
                mvgf[:, 2 * t:2 * t + 2], bn12f[:, :]),
               reads=(bbn,), writes=(bmv,))
        ew("s", lambda s, mvgf=mvgf, stAf=stAf: s.activation(
            stAf[:, :G], mvgf[:, 1::2], AF.Ln, bias=cst(C_EPS), scale=1.0),
           reads=(bmv, b_consts), writes=(bst,))
        ew("s", lambda s, stAf=stAf: s.activation(stAf[:, :G], stAf[:, :G],
                                                  AF.Exp, bias=0.0, scale=-0.5),
           reads=(bst,), writes=(bst,))
        ew("v", lambda v, mvgf=mvgf, stBf=stBf: v.tensor_scalar_mul(
            stBf[:, :G], mvgf[:, 0::2], -1.0),
           reads=(bmv,), writes=(bst,))
        for t in range(G):
            ew("v", lambda v, t=t, xlnt=xlnt, x_g=x_g, stAf=stAf, stBf=stBf:
               v.tensor_scalar(xlnt[:, :], x_g[:, t * 1024:(t + 1) * 1024],
                               stBf[:, t:t + 1], stAf[:, t:t + 1],
                               ALU.add, ALU.mult),
               reads=(b_xg[p][t], bst), writes=(b_xlnt[p],))
            for half in range(2):
                pb = PP[0][:, half * 512:(half + 1) * 512]
                for j in range(4):
                    kc = half * 4 + j
                    ew("t", lambda t_, kc=kc, j=j, xlnt=xlnt, pb=pb: t_.matmul(
                        pb[:, j * 128:(j + 1) * 128],
                        xlnt[:, kc * 128:(kc + 1) * 128], identb[:, :],
                        start=True, stop=True),
                       reads=(b_xlnt[p], b_ident), writes=(b_PP[0],),
                       inc=(j == 3))
                dst = X_fm.rearrange("q (k c) -> q k c", k=8, c=W)[
                    :, half * 4:(half + 1) * 4, t * 128:(t + 1) * 128]
                ew("s", lambda s, dst=dst, pb=pb: s.copy(dst, pb[:, :512]),
                   reads=(b_PP[0],), writes=(b_Xfm[p],))
        yield
        for kc in range(8):
            ew("t", lambda t_, kc=kc, X_fm=X_fm: t_.matmul(
                PP[1][:16, :W], Wgk1_s[:, kc * 16:(kc + 1) * 16],
                X_fm[:, kc * W:(kc + 1) * W], start=(kc == 0), stop=(kc == 7)),
               reads=(b_Wgk, b_Xfm[p]), writes=(b_PP[1],), inc=(kc == 7))
        ew("s", lambda s, z1bp=z1b[p]: s.copy(z1bp[:16, :], PP[1][:16, :W]),
           reads=(b_PP[1],), writes=(bz1,))

        def gates(ms):
            sQ = scq[p * 2 + ms % 2]
            sK = sck[p * 2 + ms % 2]
            bQ = b_scq[p * 2 + ms % 2]
            bK = b_sck[p * 2 + ms % 2]
            sA = scg[2 * p + ms % 2]
            bA = b_scg[2 * p + ms % 2]
            ew("t", lambda t_, ms=ms, z1bp=z1b[p]: t_.matmul(
                PP[0][:, :W], Wgk2_s[:16, ms * 128:(ms + 1) * 128],
                z1bp[:16, :], start=True, stop=True),
               reads=(b_Wgk, bz1), writes=(b_PP[0],))
            ew("v", lambda v, ms=ms, sA=sA: v.tensor_scalar(
                sA[:, :W], PP[0][:, :W], bgk2n[:, ms:ms + 1], -1.0,
                ALU.subtract, ALU.mult),
               reads=(b_PP[0], b_vecs), writes=(bA,))
            ew("v", lambda v, sA=sA, sK=sK: v.tensor_single_scalar(
                sK[:, :W], sA[:, :W], 0.0, ALU.max),
               reads=(bA,), writes=(bK,))
            ew("s", lambda s, sA=sA, sQ=sQ: s.activation(sQ[:, :W], sA[:, :W],
                                                         AF.Abs),
               reads=(bA,), writes=(bQ,))
            ew("s", lambda s, sQ=sQ: s.activation(sQ[:, :W], sQ[:, :W], AF.Exp,
                                                  bias=0.0, scale=-1.0),
               reads=(bQ,), writes=(bQ,))
            ew("s", lambda s, sQ=sQ: s.activation(sQ[:, :W], sQ[:, :W], AF.Ln,
                                                  bias=1.0, scale=1.0),
               reads=(bQ,), writes=(bQ,))
            ew("v", lambda v, sA=sA, sQ=sQ, sK=sK: v.tensor_add(
                sA[:, :W], sK[:, :W], sQ[:, :W]),
               reads=(bK, bQ), writes=(bA,))
            ew("s", lambda s, sA=sA, sQ=sQ: s.activation(
                sQ[:, :W], sA[:, :W], AF.Exp, bias=cst(C_LN16), scale=-1.0 / 16),
               reads=(bA, b_consts), writes=(bQ,))
            ew("s", lambda s, sA=sA, sK=sK: s.activation(
                sK[:, :W], sA[:, :W], AF.Exp, bias=0.0, scale=1.0 / 16),
               reads=(bA,), writes=(bK,))
            for t in range(G):
                ew("v", lambda v, ms=ms, t=t, sK=sK, g63=g63: v.tensor_copy(
                    g63[:, ms * G + t:ms * G + t + 1],
                    sK[:, t * 128 + 63:t * 128 + 64]),
                   reads=(bK,), writes=(bgl,))
                ew("v", lambda v, ms=ms, t=t, sK=sK, g127=g127: v.tensor_copy(
                    g127[:, ms * G + t:ms * G + t + 1],
                    sK[:, t * 128 + 127:t * 128 + 128]),
                   reads=(bK,), writes=(bgl,))
            ew("v", lambda v, ms=ms, egln=egln, g63=g63: v.reciprocal(
                egln[:, ms * G:(ms + 1) * G], g63[:, ms * G:(ms + 1) * G]),
               reads=(bgl,), writes=(bgl,))
            ew("v", lambda v, ms=ms, egls=egls, g63=g63, g127=g127: v.tensor_mul(
                egls[:, ms * G:(ms + 1) * G], g63[:, ms * G:(ms + 1) * G],
                g127[:, ms * G:(ms + 1) * G]),
               reads=(bgl,), writes=(bgl,))
            ew("v", lambda v, ms=ms, egls=egls: v.reciprocal(
                egls[:, ms * G:(ms + 1) * G], egls[:, ms * G:(ms + 1) * G]),
               reads=(bgl,), writes=(bgl,))
            for t in range(G):
                h2 = slice(t * 128 + 64, t * 128 + 128)
                ew("v", lambda v, ms=ms, t=t, h2=h2, sQ=sQ, egln=egln:
                   v.tensor_scalar_mul(sQ[:, h2], sQ[:, h2],
                                       egln[:, ms * G + t:ms * G + t + 1]),
                   reads=(bgl,), writes=(bQ,))
                ew("v", lambda v, ms=ms, t=t, h2=h2, sK=sK, g63=g63:
                   v.tensor_scalar_mul(sK[:, h2], sK[:, h2],
                                       g63[:, ms * G + t:ms * G + t + 1]),
                   reads=(bgl,), writes=(bK,))

        def qkproj(ms):
            sQ = scq[p * 2 + ms % 2]
            sK = sck[p * 2 + ms % 2]
            bQ = b_scq[p * 2 + ms % 2]
            bK = b_sck[p * 2 + ms % 2]
            wq_ = wqb[ms % 2]
            bwq = b_wqb[ms % 2]
            wk_ = wkb[ms % 2]
            bwk = b_wkb[ms % 2]
            for kc in range(8):
                ew("t", lambda t_, ms=ms, kc=kc, wq_=wq_, X_fm=X_fm: t_.matmul(
                    PP[1][:, :W], wq_[:, kc * 128:(kc + 1) * 128],
                    X_fm[:, kc * W:(kc + 1) * W], start=(kc == 0), stop=(kc == 7)),
                   reads=(bwq, b_Xfm[p]), writes=(b_PP[1],), inc=(kc == 7))
            ew("v", lambda v, ms=ms, QG=QG, sQ=sQ: v.tensor_mul(
                QG[:, ms * W:(ms + 1) * W], PP[1][:, :W], sQ[:, :W]),
               reads=(b_PP[1], bQ), writes=(b_QG[p],))
            for kc in range(8):
                ew("t", lambda t_, ms=ms, kc=kc, wk_=wk_, X_fm=X_fm: t_.matmul(
                    PP[1][:, 512:512 + W], wk_[:, kc * 128:(kc + 1) * 128],
                    X_fm[:, kc * W:(kc + 1) * W], start=(kc == 0), stop=(kc == 7)),
                   reads=(bwk, b_Xfm[p]), writes=(b_PP[1],), inc=(kc == 7))
            ew("v", lambda v, ms=ms, KG=KG, sK=sK: v.tensor_mul(
                KG[:, ms * W:(ms + 1) * W], PP[1][:, 512:512 + W], sK[:, :W]),
               reads=(b_PP[1], bK), writes=(b_KG[p],))

        for ms in range(8):
            P.dma("y", WQB[ms % 2], out=wqb[ms % 2][:, :], in_=wqc_d[ms],
                  reads=tuple(b_wqc), writes=(b_wqb[ms % 2],))
            P.dma("y", WKB[ms % 2], out=wkb[ms % 2][:, :], in_=wkc_d[ms],
                  reads=tuple(b_wkc), writes=(b_wkb[ms % 2],))
            gates(ms)
            if ms >= 1:
                qkproj(ms - 1)
            yield
        qkproj(7)
        yield
        for t in range(G):
            for half in range(2):
                pb = PP[0][:, half * 512:(half + 1) * 512]
                for j in range(4):
                    ms = half * 4 + j
                    ew("t", lambda t_, ms=ms, j=j, t=t, KG=KG, pb=pb: t_.matmul(
                        pb[:, j * 128:(j + 1) * 128],
                        KG[:, ms * W + t * 128: ms * W + (t + 1) * 128],
                        identb[:, :], start=True, stop=True),
                       reads=(b_KG[p], b_ident), writes=(b_PP[0],), inc=(j == 3))
                ew("s", lambda s, half=half, t=t, KTM=KTM, pb=pb: s.copy(
                    KTM[t][:, half * 512:(half + 1) * 512], pb[:, :512]),
                   reads=(b_PP[0],), writes=(b_KTM[p][t],))
            yield
        for which, wbuf, bwbuf, WSTR, src_d, b_src, dstP, bdstP in (
                ("gt", [wgb[0], wgb[1], wkb[0], wkb[1]],
                 [b_wgb[0], b_wgb[1], b_wkb[0], b_wkb[1]],
                 [WGB[0], WGB[1], WKB[0], WKB[1]],
                 wgbf_d, b_wgbf, SGT, b_SGT[p]),
                ("v", [wvb[0], wvb[1], wqb[0], wqb[1]],
                 [b_wvb[0], b_wvb[1], b_wqb[0], b_wqb[1]],
                 [WVB[0], WVB[1], WQB[0], WQB[1]],
                 wvbf_d, b_wvbf, VT, b_VT[p])):
            for ns2 in range(2):
                for kc in range(8):
                    wb = wbuf[kc % 4]
                    bwb = bwbuf[kc % 4]
                    P.dma("y", WSTR[kc % 4], out=wb[:, :],
                          in_=src_d[kc][:, ns2 * 1024:(ns2 + 1) * 1024],
                          reads=(b_src[kc],), writes=(bwb,))
                    for t in range(G):
                        for nsi in range(2):
                            bk = PP[t][:, nsi * 512:(nsi + 1) * 512]
                            ew("t", lambda t_, kc=kc, t=t, nsi=nsi, wb=wb,
                               bk=bk, X_fm=X_fm: t_.matmul(
                                   bk,
                                   X_fm[:, kc * W + t * 128: kc * W + (t + 1) * 128],
                                   wb[:, nsi * 512:(nsi + 1) * 512],
                                   start=(kc == 0), stop=(kc == 7)),
                               reads=(b_Xfm[p], bwb), writes=(b_PP[t],),
                               inc=(kc == 7 or (t == G - 1 and nsi == 1)))
                yield
                for t in range(G):
                    for nsi in range(2):
                        q4 = ns2 * 2 + nsi
                        bk = PP[t][:, nsi * 512:(nsi + 1) * 512]
                        if which == "v":
                            ew("s", lambda s, t=t, q4=q4, bk=bk, dstP=dstP: s.copy(
                                dstP[t][:, q4 * 512:(q4 + 1) * 512], bk),
                               reads=(b_PP[t],), writes=(bdstP[t],))
                        else:
                            sS = scg[2 * p + (t * 2 + nsi) % 2]
                            bS = b_scg[2 * p + (t * 2 + nsi) % 2]
                            ew("s", lambda s, bk=bk, sS=sS: s.activation(
                                sS[:, :512], bk, AF.Exp, bias=0.0, scale=-1.0),
                               reads=(b_PP[t],), writes=(bS,))
                            ew("v", lambda v, sS=sS: v.tensor_single_scalar(
                                sS[:, :512], sS[:, :512], 1e30, ALU.min),
                               reads=(bS,), writes=(bS,))
                            ew("s", lambda s, sS=sS: s.activation(
                                sS[:, :512], sS[:, :512], AF.Ln, bias=1.0,
                                scale=1.0),
                               reads=(bS,), writes=(bS,))
                            ew("s", lambda s, sS=sS: s.activation(
                                sS[:, :512], sS[:, :512], AF.Exp, bias=0.0,
                                scale=-1.0),
                               reads=(bS,), writes=(bS,))
                            ew("v", lambda v, t=t, q4=q4, bk=bk, sS=sS,
                               dstP=dstP: v.tensor_mul(
                                   dstP[t][:, q4 * 512:(q4 + 1) * 512], bk,
                                   sS[:, :512]),
                               reads=(b_PP[t], bS), writes=(bdstP[t],))

        yield

    def scanback(g):
        p = g % 2
        t0g = g * G
        x_g, QG, KG = x_gP[p], QGP[p], KGP[p]
        KTM, VT, SGT = KTMP[p], VTP[p], SGTP[p]
        g63, g127, egln, egls = GL63[p], GL127[p], EGLN[p], EGLS[p]
        bgl = b_GLP[p]
        AB = ABUF[p]
        bAB = b_xlnt[p]
        OF_l = [OF, OF2B[p]]
        bOF_l = [b_OF, b_QG[p]]
        OFm_l = [OFm, OFKB[p]]
        bOFm_l = [b_OFm, b_KG[p]]
        for t in range(G):
            tile = t0g + t
            OFt = OF_l[t]
            bOFt = bOF_l[t]
            for h in range(4):
                for ci in range(2):
                    ms = 2 * h + ci
                    ew("t", lambda t_, h=h, ms=ms, ci=ci, t=t, KG=KG, QG=QG:
                       t_.matmul(bank(0)[:, h * 128:(h + 1) * 128],
                                 KG[:, ms * W + t * 128: ms * W + (t + 1) * 128],
                                 QG[:, ms * W + t * 128: ms * W + (t + 1) * 128],
                                 start=(ci == 0), stop=(ci == 1)),
                       reads=(b_KG[p], b_QG[p]), writes=(bbank(0),),
                       inc=(h == 3 and ci == 1))
            ew("v", lambda v, AB=AB: v.tensor_mul(AB[:, :512], bank(0),
                                                  mask4[:, :]),
               reads=(bbank(0), b_mask), writes=(bAB,))
            yield
            obk = [1, 2, 3, 0]
            for h in range(4):
                bi = obk[h]
                for ci in range(2):
                    ms = 2 * h + ci
                    ew("t", lambda t_, ms=ms, t=t, ci=ci, bi=bi, QG=QG: t_.matmul(
                        bank(bi), QG[:, ms * W + t * 128: ms * W + (t + 1) * 128],
                        S_bfc[:, ms * 512:(ms + 1) * 512],
                        start=(ci == 0), stop=False),
                       reads=(b_QG[p], b_Sms[ms]), writes=(bbank(bi),),
                       inc=(ci == 1))
                ew("t", lambda t_, h=h, t=t, bi=bi, VT=VT, AB=AB: t_.matmul(
                    bank(bi), AB[:, h * 128:(h + 1) * 128],
                    VT[t][:, h * 512:(h + 1) * 512], start=False, stop=True),
                   reads=(bAB, b_VT[p][t]), writes=(bbank(bi),), inc=True)
            yield
            for h in range(4):
                bi = obk[h]
                ew("v", lambda v, bi=bi: v.bn_stats(bn12[:, :6], bank(bi)),
                   reads=(bbank(bi),), writes=(b_bn,))
                ew("v", lambda v, h=h: v.bn_aggr(mo8[:, 2 * h:2 * h + 2],
                                                 bn12[:, :6]),
                   reads=(b_bn,), writes=(b_mo,))
            ew("v", lambda v: v.scalar_tensor_tensor(
                rsO[:, :4], mo8[:, 0::2], 1.0, mo8[:, 0::2], ALU.mult, ALU.mult),
               reads=(b_mo,), writes=(b_rsO,))
            ew("v", lambda v: v.tensor_add(rsO[:, :4], rsO[:, :4], mo8[:, 1::2]),
               reads=(b_mo,), writes=(b_rsO,))
            ew("s", lambda s: s.activation(rsO[:, :4], rsO[:, :4], AF.Ln,
                                           bias=cst(C_EPS), scale=1.0),
               reads=(b_consts,), writes=(b_rsO,))
            ew("s", lambda s: s.activation(rsO[:, :4], rsO[:, :4], AF.Exp,
                                           bias=0.0, scale=-0.5),
               reads=(), writes=(b_rsO,))
            yield
            for h in range(4):
                bi = obk[h]
                ew("v", lambda v, h=h, bi=bi, t=t, SGT=SGT, OFt=OFt:
                   v.scalar_tensor_tensor(
                       OFt[:, h * 512:(h + 1) * 512], bank(bi), rsO[:, h:h + 1],
                       SGT[t][:, h * 512:(h + 1) * 512], ALU.mult, ALU.mult),
                   reads=(bbank(bi), b_rsO, b_SGT[p][t]), writes=(bOFt,))
                yield
            for h in range(4):
                for ci in range(2):
                    ms = 2 * h + ci
                    ew("t", lambda t_, ms=ms, t=t, ci=ci, KTM=KTM, VT=VT: t_.matmul(
                        bank(2 + ci), KTM[t][:, ms * 128:(ms + 1) * 128],
                        VT[t][:, (ms // 2) * 512:(ms // 2 + 1) * 512],
                        start=True, stop=True),
                       reads=(b_KTM[p][t], b_VT[p][t]), writes=(bbank(2 + ci),),
                       inc=True)
                ew("v", lambda v, h=h: v.tensor_add(
                    scB2[:, :], S_bfc[:, h * 1024:(h + 1) * 1024], PP[3][:, :]),
                   reads=(b_Sms[2 * h], b_Sms[2 * h + 1], b_PP[3]),
                   writes=(b_scB1,))
                for ci in range(2):
                    ms = 2 * h + ci
                    ew("v", lambda v, ms=ms, t=t, ci=ci, egls=egls:
                       v.tensor_scalar_mul(
                           S_bfc[:, ms * 512:(ms + 1) * 512],
                           scB2[:, ci * 512:(ci + 1) * 512],
                           egls[:, ms * G + t:ms * G + t + 1]),
                       reads=(b_scB1, bgl), writes=(b_Sms[ms],))
                yield
            yield
        for t in range(G):
            OFt, bOFt = OF_l[t], bOF_l[t]
            OMt, bOMt = OFm_l[t], bOFm_l[t]
            for half in range(4):
                bi = 3 if half % 2 == 0 else 0
                for j in range(4):
                    vc = half * 4 + j
                    ew("t", lambda t_, vc=vc, j=j, bi=bi, OFt=OFt: t_.matmul(
                        bank(bi)[:, j * 128:(j + 1) * 128],
                        OFt[:, vc * 128:(vc + 1) * 128], identb[:, :],
                        start=True, stop=True),
                       reads=(bOFt, b_ident), writes=(bbank(bi),), inc=(j == 3))
                ew("s", lambda s, half=half, bi=bi, OMt=OMt: s.copy(
                    OMt[:, half * 512:(half + 1) * 512], bank(bi)),
                   reads=(bbank(bi),), writes=(bOMt,))
            yield
        WOL = [wsb[0], wsb[1], wgb[0], wgb[1]]
        bWOL = [b_wsb[0], b_wsb[1], b_wgb[0], b_wgb[1]]
        SOL = [LWS[0], LWS[1], WGB[0], WGB[1]]
        for vc in range(16):
            wb = WOL[vc % 4]
            bwb = bWOL[vc % 4]
            P.dma("y", SOL[vc % 4], out=wb[:, :1024], in_=wobf_d[vc],
                  reads=(b_wobf[vc],), writes=(bwb,))
            for t in range(G):
                ew("t", lambda t_, vc=vc, wb=wb, t=t: t_.matmul(
                    PP[2 + t][:, 0:512],
                    OFm_l[t][:, vc * 128:(vc + 1) * 128],
                    wb[:, 0:512],
                    start=(vc == 0), stop=(vc == 15)),
                   reads=(bOFm_l[t], bwb), writes=(b_PP[2 + t],), inc=False)
                ew("t", lambda t_, vc=vc, wb=wb, t=t: t_.matmul(
                    PP[2 + t][:, 512:1024],
                    OFm_l[t][:, vc * 128:(vc + 1) * 128],
                    wb[:, 512:1024],
                    start=(vc == 0), stop=(vc == 15)),
                   reads=(bOFm_l[t], bwb), writes=(b_PP[2 + t],), inc=True)
            if vc % 4 == 3:
                yield
        ew("s", lambda s: s.copy(sc3[:, :], PP[2][:, :]),
           reads=(b_PP[2],), writes=(b_sc3,))
        ew("s", lambda s: s.copy(sc4[:, :], PP[3][:, :]),
           reads=(b_PP[3],), writes=(b_sc4,))

        def ln_chain(t):
            tile = t0g + t
            if t == 0:
                bnT, mvT, stAT, stBT, scT = bn12, mvg, stA, stB, sc3
                bbnT, bmvT, bstT, bscT = b_bn, b_mv, b_st, b_sc3
            else:
                bnT, mvT, stAT, stBT, scT = bnF[p], mvF[p], stAF[p], stBF[p], sc4
                bbnT, bmvT, bstT, bscT = b_bnF[p], b_mvF[p], b_stF[p], b_sc4
            ew("v", lambda v, scT=scT, bnT=bnT: v.bn_stats(
                bnT[:, :6], scT[:, :512]), reads=(bscT,), writes=(bbnT,))
            ew("v", lambda v, scT=scT, bnT=bnT: v.bn_stats(
                bnT[:, 6:12], scT[:, 512:]), reads=(bscT,), writes=(bbnT,))
            yield
            ew("v", lambda v, bnT=bnT, mvT=mvT: v.bn_aggr(mvT[:, :2], bnT[:, :]),
               reads=(bbnT,), writes=(bmvT,))
            yield
            ew("s", lambda s, mvT=mvT, stAT=stAT: s.activation(
                stAT[:, :1], mvT[:, 1:2], AF.Ln, bias=cst(C_EPS), scale=1.0),
               reads=(bmvT, b_consts), writes=(bstT,))
            ew("s", lambda s, stAT=stAT: s.activation(
                stAT[:, :1], stAT[:, :1], AF.Exp, bias=0.0, scale=-0.5),
               reads=(), writes=(bstT,))
            ew("v", lambda v, mvT=mvT, stBT=stBT: v.tensor_scalar_mul(
                stBT[:, :1], mvT[:, 0:1], -1.0),
               reads=(bmvT,), writes=(bstT,))
            yield
            ew("v", lambda v, scT=scT, stAT=stAT, stBT=stBT:
               v.tensor_scalar(scT[:, :], scT[:, :], stBT[:, :1], stAT[:, :1],
                               ALU.add, ALU.mult),
               reads=(bstT,), writes=(bscT,))
            yield
            xb = xpb[t % 2]
            bxb = b_xpb[t % 2]
            ew("v", lambda v, t=t, xb=xb, x_g=x_g, scT=scT: v.tensor_add(
                xb[:, :], scT[:, :], x_g[:, t * 1024:(t + 1) * 1024]),
               reads=(bscT, b_xg[p][t]), writes=(bxb,))
            P.dma("s", XPS[t % 2], out=xp_d[tile * 128:(tile + 1) * 128, :],
                  in_=xb[:, :], reads=(bxb,), dram_final=(b_xp_t[tile],))
            yield
            ew("v", lambda v, xb=xb, bnT=bnT: v.bn_stats(bnT[:, :6], xb[:, :512]),
               reads=(bxb,), writes=(bbnT,))
            ew("v", lambda v, xb=xb, bnT=bnT: v.bn_stats(bnT[:, 6:12], xb[:, 512:]),
               reads=(bxb,), writes=(bbnT,))
            yield
            ew("v", lambda v, bnT=bnT, mvT=mvT: v.bn_aggr(mvT[:, :2], bnT[:, :]),
               reads=(bbnT,), writes=(bmvT,))
            yield
            ew("s", lambda s, mvT=mvT, stAT=stAT: s.activation(
                stAT[:, :1], mvT[:, 1:2], AF.Ln, bias=cst(C_EPS), scale=1.0),
               reads=(bmvT, b_consts), writes=(bstT,))
            ew("s", lambda s, stAT=stAT: s.activation(
                stAT[:, :1], stAT[:, :1], AF.Exp, bias=0.0, scale=-0.5),
               reads=(), writes=(bstT,))
            ew("v", lambda v, mvT=mvT, stBT=stBT: v.tensor_scalar_mul(
                stBT[:, :1], mvT[:, 0:1], -1.0),
               reads=(bmvT,), writes=(bstT,))
            yield
            x3 = xl3b[t % 2]
            bx3 = b_xl3b[t % 2]
            ew("v", lambda v, xb=xb, x3=x3, stAT=stAT, stBT=stBT:
               v.tensor_scalar(x3[:, :], xb[:, :], stBT[:, :1], stAT[:, :1],
                               ALU.add, ALU.mult),
               reads=(bxb, bstT), writes=(bx3,))
            yield
            ew("v", lambda v, x3=x3, t=t: v.reduce_max(
                rtA[:, t % 2:t % 2 + 1], x3[:, :], AX.X,
                apply_absolute_value=True),
               reads=(bx3,), writes=(b_rtA,))
            ew("v", lambda v, t=t: v.tensor_max(AMX[:, :], AMX[:, :],
                                                rtA[:, t % 2:t % 2 + 1]),
               reads=(b_rtA,), writes=(b_AMX,))
            P.dma("s", X3S[t % 2], out=xl3_d[tile * 128:(tile + 1) * 128, :],
                  in_=x3[:, :], reads=(bx3,), dram_final=(b_xl3_t[tile],))
            yield

        chains = [ln_chain(0), ln_chain(1)]
        while chains:
            for c in list(chains):
                try:
                    next(c)
                except StopIteration:
                    chains.remove(c)
            yield

    def emit_allreduce(amx_buf, amx_t, cin_d, cout_d, b_scol, sx_col, rsx_col):
        ew("t", lambda t_: t_.matmul(PP[1][:1, :128], amx_t[:, :], identf[:, :],
                                     start=True, stop=True),
           reads=(amx_buf, b_ident), writes=(b_PP[1],))
        ew("v", lambda v: v.tensor_copy(r1a[:1, :], PP[1][:1, :128]),
           reads=(b_PP[1],), writes=(b_r1,))
        ew("v", lambda v: v.reduce_max(r1b[:1, :1], r1a[:1, :], AX.X),
           reads=(b_r1,), writes=(b_r1,))
        ew("v", lambda v: v.tensor_single_scalar(r1b[:1, :1], r1b[:1, :1],
                                                 1e-8, ALU.max),
           reads=(b_r1,), writes=(b_r1,))
        ew("v", lambda v: v.tensor_copy(ccb[:1, :1], r1b[:1, :1]),
           reads=(b_r1,), writes=(b_ccb,))
        P.dma("g", GDM, out=cin_d[:, :1], in_=ccb[:1, :1], reads=(b_ccb,),
              writes=(b_cc,))
        ew("g", lambda g: g.collective_compute(
            "AllReduce", ALU.max, replica_groups=[list(range(n_cores))],
            ins=[cin_d[:, :1].opt()], outs=[cout_d[:, :1].opt()]),
           reads=(b_cc,), writes=(b_cc,))
        P.dma("g", GDM, out=ccb[:1, :1], in_=cout_d[:, :1], reads=(b_cc,),
              writes=(b_ccb,))
        ew("v", lambda v: v.reciprocal(r1b[:1, :1], ccb[:1, :1]),
           reads=(b_ccb,), writes=(b_r1,))
        ew("v", lambda v: v.tensor_single_scalar(r1b[:1, :1], r1b[:1, :1],
                                                 32767.0, ALU.mult),
           reads=(), writes=(b_r1,))
        ew("t", lambda t_: t_.matmul(PP[1][:, 512:513], onesc[:1, :],
                                     r1b[:1, :1], start=True, stop=True),
           reads=(b_onesc, b_r1), writes=(b_PP[1],))
        ew("v", lambda v: v.tensor_copy(sx_col[:, :], PP[1][:, 512:513]),
           reads=(b_PP[1],), writes=(b_scol,))
        ew("v", lambda v: v.tensor_single_scalar(r1b[:1, :1], ccb[:1, :1],
                                                 1.0 / 32767.0, ALU.mult),
           reads=(b_ccb,), writes=(b_r1,))
        ew("t", lambda t_: t_.matmul(PP[1][:, 512:513], onesc[:1, :],
                                     r1b[:1, :1], start=True, stop=True),
           reads=(b_onesc, b_r1), writes=(b_PP[1],))
        ew("v", lambda v: v.tensor_copy(rsx_col[:, :], PP[1][:, 512:513]),
           reads=(b_PP[1],), writes=(b_scol,))

    SG1 = min(4, NG)

    def f1_pro(u):
        par = (u % 2) * SG1
        for gi in range(SG1):
            g = u * SG1 + gi
            XSIM = XSIMQ[par + gi]
            t0g = g * G
            for t in range(G):
                sl = (gi * G + t) % 4
                lb = xl3l[sl]
                blb = b_xl3l[sl]
                P.dma("y", XL3L[sl], out=lb[:, :],
                      in_=xl3_d[(t0g + t) * 128:(t0g + t + 1) * 128, :],
                      reads=(b_xl3_t[t0g + t],), writes=(blb,))
                for half in range(2):
                    for j in range(4):
                        kc = half * 4 + j
                        ew("t", lambda t_, kc=kc, j=j, lb=lb: t_.matmul(
                            PP[0][:, 512 + j * 128:512 + (j + 1) * 128],
                            lb[:, kc * 128:(kc + 1) * 128], identb[:, :],
                            start=True, stop=True),
                           reads=(blb, b_ident), writes=(b_PP0H,),
                           inc=(j == 3))
                    ew("s", lambda s: s.copy(sc3[:, :512], PP[0][:, 512:1024]),
                       reads=(b_PP0H,), writes=(b_sc3,))
                    xdst = XSIM.rearrange("q (k c) -> q k c", k=8, c=W)[
                        :, half * 4:(half + 1) * 4, t * 128:(t + 1) * 128]
                    emit_round(sc3[:, :512], (b_sc3,), SXC[:, :1], b_sx,
                               xdst, b_XSIM[par + gi], 512)
                    yield
            if dbg:
                P.dma("y", DBGS, out=dXS_d[g], in_=XSIM[:, :],
                      reads=(b_XSIM[par + gi],), writes=(Buf(),))

    def f1_mms(u):
        par = (u % 2) * SG1
        WSL = [wsb[0], wsb[1], wqb[0], wqb[1]]
        bWSL = [b_wsb[0], b_wsb[1], b_wqb[0], b_wqb[1]]
        STL = [LWS[0], LWS[1], WQB[0], WQB[1]]
        for s in range(32):
            wb = WSL[s % 4]
            bwb = bWSL[s % 4]
            P.dma("y", STL[s % 4], out=wb[:, :1024], in_=wqin_d[s],
                  reads=(b_wqin[s], b_wqinB[s]), writes=(bwb,))
            for gi in range(SG1):
                g = u * SG1 + gi
                pp = PP[gi]
                bpp = b_PP[gi]
                for kc in range(8):
                    ew("t", lambda t_, kc=kc, wb=wb, pp=pp, gi=gi, par=par:
                       t_.matmul(
                           pp[:, :W], wb[:, kc * 128:(kc + 1) * 128],
                           XSIMQ[par + gi][:, kc * W:(kc + 1) * W],
                           start=(kc == 0), stop=(kc == 7)),
                       reads=(bwb, b_XSIM[par + gi]), writes=(bpp,),
                       inc=(kc == 7))
                hb = hbb[gi]
                bhb = b_hbb[gi]
                ew("s", lambda s_, s=s, pp=pp, hb=hb: s_.activation(
                    hb[:, :], pp[:, :W], AF.Gelu, bias=binb[:, s:s + 1],
                    scale=ALF[:, s:s + 1]),
                   reads=(bpp, b_ALF, b_vecs), writes=(bhb,))
                P.dma("s", HIDS[gi], out=hid_d[g, s], in_=hb[:, :],
                      reads=(bhb,), dram_final=(b_hid[g][s],))
            for gi in range(SG1):
                ew("v", lambda v, gi=gi: v.reduce_max(
                    rtA[:, gi:gi + 1], hbb[gi][:, :], AX.X,
                    apply_absolute_value=True),
                   reads=(b_hbb[gi],), writes=(b_rtA,))
            ew("v", lambda v: v.reduce_max(
                rtA[:, 4:5], rtA[:, :SG1], AX.X),
               reads=(b_rtA,), writes=(b_rtA,))
            ew("v", lambda v: v.tensor_max(AMH[:, :], AMH[:, :], rtA[:, 4:5]),
               reads=(b_rtA,), writes=(b_AMH,))
            yield

    SG2 = min(2, NG)

    def f2_super(u):
        t0g = u * SG2 * G
        for ti in range(SG2 * G):
            P.dma("y", XPL[ti], out=xpl[ti][:, :],
                  in_=xp_d[(t0g + ti) * 128:(t0g + ti + 1) * 128, :],
                  reads=(b_xp_t[t0g + ti],),
                  writes=(b_xpl[ti],) + tuple(b_XSIM))
        WSL = [wsb[0], wsb[1], wqb[0], wqb[1]]
        bWSL = [b_wsb[0], b_wsb[1], b_wqb[0], b_wqb[1]]
        STL = [LWS[0], LWS[1], WQB[0], WQB[1]]
        for s in range(32):
            wb = WSL[s % 4]
            bwb = bWSL[s % 4]
            P.dma("y", STL[s % 4], out=wb[:, :1024], in_=w2_d[s],
                  reads=(b_w2s[s],), writes=(bwb,))
            j = s % 2
            hl2 = hlb2[j]
            bhl2 = (b_hlb[2 * j], b_hlb[2 * j + 1])
            P.dma("y", HLB[2 * j],
                  out=hl2.rearrange("p (g c) -> p g c", g=SG2, c=W),
                  in_=hid_d[u * SG2:(u + 1) * SG2, s]
                  .rearrange("g p c -> p g c"),
                  reads=tuple(b_hid[u * SG2 + gi][s] for gi in range(SG2)),
                  writes=bhl2)
            h22 = h2b2[j]
            bh22 = (b_h2b[2 * j], b_h2b[2 * j + 1])
            emit_round(hl2[:, :], bhl2, SHC[:, :1], b_sh, h22[:, :],
                       bh22, 2 * W)
            for gi in range(SG2):
                for t in range(G):
                    for ns in range(2):
                        ew("t", lambda t_, t=t, ns=ns, h22=h22, wb=wb, gi=gi,
                           s=s: t_.matmul(
                            PP[gi * G + t][:, ns * 512:(ns + 1) * 512],
                            h22[:, gi * W + t * 128:gi * W + (t + 1) * 128],
                            wb[:, ns * 512:(ns + 1) * 512],
                            start=(s == 0), stop=(s == 31)),
                           reads=(bh22[0], bh22[1], bwb),
                           writes=(b_PP[gi * G + t],),
                           inc=(ns == 1))
            yield
        def ln4_chain(ti):
            tile = t0g + ti
            pp = PP[ti]
            bpp = b_PP[ti]
            if ti % 2 == 0:
                bnT, mvT, stAT, stBT = bn12, mvg, stA, stB
                scT, sc4T = sc3, sc4
                bbnT, bmvT, bstT = b_bn, b_mv, b_st
                bscT, bsc4T = b_sc3, b_sc4
            else:
                bnT, mvT, stAT, stBT = bnF[0], mvF[0], stAF[0], stBF[0]
                scT, sc4T = scB2, wacc[0]
                bbnT, bmvT, bstT = b_bnF[0], b_mvF[0], b_stF[0]
                bscT, bsc4T = b_scB1, b_wacc
            ew("v", lambda v, pp=pp, scT=scT: v.tensor_add(
                scT[:, :], pp[:, :], boutb[:, :]),
               reads=(bpp, b_vecs), writes=(bscT,))
            yield
            ew("v", lambda v, scT=scT, bnT=bnT: v.bn_stats(
                bnT[:, :6], scT[:, :512]), reads=(bscT,), writes=(bbnT,))
            ew("v", lambda v, scT=scT, bnT=bnT: v.bn_stats(
                bnT[:, 6:12], scT[:, 512:]), reads=(bscT,), writes=(bbnT,))
            yield
            ew("v", lambda v, bnT=bnT, mvT=mvT: v.bn_aggr(mvT[:, :2], bnT[:, :]),
               reads=(bbnT,), writes=(bmvT,))
            yield
            ew("s", lambda s, mvT=mvT, stAT=stAT: s.activation(
                stAT[:, :1], mvT[:, 1:2], AF.Ln, bias=cst(C_EPS), scale=1.0),
               reads=(bmvT, b_consts), writes=(bstT,))
            ew("s", lambda s, stAT=stAT: s.activation(
                stAT[:, :1], stAT[:, :1], AF.Exp, bias=0.0, scale=-0.5),
               reads=(), writes=(bstT,))
            ew("v", lambda v, mvT=mvT, stBT=stBT: v.tensor_scalar_mul(
                stBT[:, :1], mvT[:, 0:1], -1.0),
               reads=(bmvT,), writes=(bstT,))
            yield
            ew("v", lambda v, scT=scT, sc4T=sc4T, stAT=stAT, stBT=stBT:
               v.tensor_scalar(sc4T[:, :], scT[:, :], stBT[:, :1], stAT[:, :1],
                               ALU.add, ALU.mult),
               reads=(bscT, bstT), writes=(bsc4T,))
            yield
            ob = outb[ti % 2]
            bob = b_outb[ti % 2]
            ew("v", lambda v, ob=ob, ti=ti, sc4T=sc4T: v.tensor_add(
                ob[:, :], sc4T[:, :], xpl[ti][:, :]),
               reads=(bsc4T, b_xpl[ti]), writes=(bob,))
            P.dma("s", STY[ti % 2], out=y_h[tile * 128:(tile + 1) * 128, :],
                  in_=ob[:, :], reads=(bob,), writes=(Buf(),))
            yield

        for base in range(0, SG2 * G, 2):
            chains = [ln4_chain(base)]
            if base + 1 < SG2 * G:
                chains.append(ln4_chain(base + 1))
            while chains:
                for c in list(chains):
                    try:
                        next(c)
                    except StopIteration:
                        chains.remove(c)
                yield

    def drive(*gens, bg=None):
        live = [gg for gg in gens if gg is not None]
        while live:
            for gg in list(live):
                try:
                    next(gg)
                except StopIteration:
                    live.remove(gg)
            if bg is not None:
                try:
                    next(bg)
                except StopIteration:
                    bg = None

    def prep_stream():
        for q in range(4):
            for _ in emit_win_prep(q):
                yield
        for c in range(32):
            for _ in emit_wout_prep(c):
                yield

    def wout_stream():
        for c in range(32):
            for _ in emit_wout_prep(c):
                yield

    def win_stream():
        for q in range(4):
            for _ in emit_win_prep(q):
                yield

    def _chain(*gens):
        for gg in gens:
            for _ in gg:
                yield

    wouts = wout_stream()
    preps = _chain(wouts, win_stream())
    for g in range(NG):
        drive(front(g), scanback(g - 1) if g >= 1 else None, bg=wouts)
    drive(scanback(NG - 1), bg=preps)
    drive(preps)

    emit_allreduce(b_AMX, AMX, ccxi_d, ccxo_d, b_sx, SXC, RSXC)
    ew("v", lambda v: v.tensor_single_scalar(ALF[:, :], AMF[:, :], 1.0 / 127.0,
                                             ALU.mult),
       reads=(b_AMF,), writes=(b_ALF,))
    ew("v", lambda v: v.tensor_scalar_mul(ALF[:, :], ALF[:, :], RSXC[:, :1]),
       reads=(b_sx,), writes=(b_ALF,))

    NU1 = NG // SG1
    drive(f1_pro(0))
    for u in range(NU1):
        drive(f1_mms(u), f1_pro(u + 1) if u + 1 < NU1 else None)

    emit_allreduce(b_AMH, AMH, cchi_d, ccho_d, b_sh, SHC, RSHC)
    ew("v", lambda v: v.tensor_single_scalar(BET[:, :], SOC[:, :], 1.0 / 127.0,
                                             ALU.mult),
       reads=(b_SOC,), writes=(b_BET,))
    ew("v", lambda v: v.tensor_scalar_mul(BET[:, :], BET[:, :], RSHC[:, :1]),
       reads=(b_sh,), writes=(b_BET,))
    def w2_prep():
        for c in range(32):
            sl = c % 4
            src_ = S_bfc[:, sl * 1024:(sl + 1) * 1024]
            P.dma("s", XL3L[sl], out=src_, in_=woutq_d[c],
                  reads=(b_woutq[c],),
                  writes=(b_w2l[sl], b_Sms[2 * sl], b_Sms[2 * sl + 1]))
            ew("v", lambda v, c=c, src_=src_: v.tensor_scalar_mul(
                wqt[:, :], src_, BET[:, c:c + 1]),
               reads=(b_w2l[sl], b_BET), writes=(b_wqt,))
            stq = W2SL[sl]
            if c >= 4:
                P.raw("s", lambda o, stq=stq, v=16 * (c // 4): o.wait_ge(
                    P.dma_sems[stq][0], v))
            P.dma("s", stq, out=w2_d[c], in_=wqt[:, :], reads=(b_wqt,),
                  writes=(b_w2s[c],))
            yield

    w2p = w2_prep()
    for _ in range(6):
        next(w2p, None)
    for u in range(NG // SG2):
        drive(f2_super(u), bg=w2p)
    drive(w2p)

    if dbg:
        ew("v", lambda v: v.tensor_copy(sc4[:, 0:1], SXC[:, :]), reads=(b_sx,), writes=(b_sc4,))
        ew("v", lambda v: v.tensor_copy(sc4[:, 1:2], SHC[:, :]), reads=(b_sh,), writes=(b_sc4,))
        ew("v", lambda v: v.tensor_copy(sc4[:, 2:3], AMX[:, :]), reads=(b_AMX,), writes=(b_sc4,))
        ew("v", lambda v: v.tensor_copy(sc4[:, 3:4], AMH[:, :]), reads=(b_AMH,), writes=(b_sc4,))
        ew("v", lambda v: v.tensor_copy(sc4[:, 4:36], ALF[:, :]), reads=(b_ALF,), writes=(b_sc4,))
        ew("v", lambda v: v.tensor_copy(sc4[:, 36:68], BET[:, :]), reads=(b_BET,), writes=(b_sc4,))
        ew("v", lambda v: v.tensor_copy(sc4[:, 68:100], AMF[:, :]), reads=(b_AMF,), writes=(b_sc4,))
        ew("v", lambda v: v.tensor_copy(sc4[:, 100:132], SOC[:, :]), reads=(b_SOC,), writes=(b_sc4,))
        P.dma("y", DBGS, out=dsc_d[:, :136], in_=sc4[:, :136],
              reads=(b_sc4,), writes=(Buf(),))
    for _sty in STY:
        P.raw("y", lambda o, _sty=_sty: o.wait_ge(
            P.dma_sems[_sty][0], P.dma_sems[_sty][1]))

    with nc.Block() as block:
        P.build(block)
    return nc


def _cons_table(bgk2, b_in, rms_w, b_out):
    cons = np.zeros((128, 54), dtype=np.float32)
    cons[:, 0] = 1e-5
    cons[:, 1] = LN16
    cons[:, 2:10] = np.asarray(bgk2, np.float32).reshape(8, 128).T
    cons[:, 10:42] = np.asarray(b_in, np.float32).reshape(32, 128).T
    cons[:, 42:46] = np.asarray(rms_w, np.float32).reshape(4, 128).T
    cons[:, 46:54] = np.asarray(b_out, np.float32).reshape(8, 128).T
    return cons


def make_in_maps(inputs_by_name, B):
    w = {k: np.ascontiguousarray(v, np.float32)
         for k, v in inputs_by_name.items() if k != "x"}
    w["cons"] = _cons_table(w["bgk2"], w["b_in"], w["rms_w"], w["b_out"])
    x = inputs_by_name["x"]
    return [dict(w, x=np.ascontiguousarray(x[b])) for b in range(B)]


def kernel(L0_attn_hidden_states, L0_attn_Wq, L0_attn_Wk, L0_attn_Wv,
           L0_attn_Wg, L0_attn_Wgk1, L0_attn_Wgk2, L0_attn_bgk2,
           L0_attn_rms_w, L0_attn_Wo, L0_ffn_W_in, L0_ffn_b_in,
           L0_ffn_W_out, L0_ffn_b_out):
    x = np.ascontiguousarray(np.asarray(L0_attn_hidden_states, dtype=np.float32))
    B, T, _ = x.shape
    nc = build(n_cores=B, T=T)
    in_maps = make_in_maps(dict(
        x=x, Wq=L0_attn_Wq, Wk=L0_attn_Wk, Wv=L0_attn_Wv, Wg=L0_attn_Wg,
        Wgk1=L0_attn_Wgk1, Wgk2=L0_attn_Wgk2, bgk2=L0_attn_bgk2,
        rms_w=L0_attn_rms_w, Wo=L0_attn_Wo, W_in=L0_ffn_W_in,
        b_in=L0_ffn_b_in, W_out=L0_ffn_W_out, b_out=L0_ffn_b_out), B)
    res = run_bass_kernel_spmd(nc, in_maps, core_ids=list(range(B)))
    out = np.stack([np.asarray(res.results[b]["y"]) for b in range(B)], axis=0)
    return out.astype(np.float32)
